# version 01
# baseline (speedup 1.0000x reference)
import contextlib
import numpy as np
import ml_dtypes
import concourse.bass as bass
import concourse.mybir as mybir
from concourse.bass_utils import run_bass_kernel_spmd

F32 = mybir.dt.float32
BF16 = mybir.dt.bfloat16
AF = mybir.ActivationFunctionType
ALU = mybir.AluOpType
AX = mybir.AxisListType
BF = ml_dtypes.bfloat16

NCORES = 8
B, T, D = 4, 4096, 1024
TPC = T // NCORES
TOK = B * TPC
DC = D // 128


class Eng:
    def __init__(self, nc, es, engine, name):
        self.e = engine
        self.sem = es.enter_context(nc.semaphore("sem_" + name))
        self.n = 0
    def mark(self, inst, k=1):
        inst.then_inc(self.sem, k)
        self.n += k
        return self.n
    def wait(self, producer, val=None):
        v = producer.n if val is None else val
        if v > 0:
            self.e.wait_ge(producer.sem, v)
    def fence(self):
        if self.n > 0:
            self.e.wait_ge(self.sem, self.n)

class Ctx:
    def __init__(self, nc, es):
        self.nc = nc
        self.es = es
        self.T = Eng(nc, es, nc.tensor, "T")
        self.V = Eng(nc, es, nc.vector, "V")
        self.S = Eng(nc, es, nc.scalar, "S")
        self.SY = Eng(nc, es, nc.sync, "SY")
        self.G = Eng(nc, es, nc.gpsimd, "G")
        self.CC = Eng(nc, es, nc.gpsimd, "CC")
    def sb(self, name, shape, dtype):
        return self.es.enter_context(self.nc.sbuf_tensor(name, list(shape), dtype))
    def ps(self, name, shape=(128, 512), dtype=F32):
        return self.es.enter_context(self.nc.psum_tensor(name, list(shape), dtype))

def elu1_evac(ctx, ps_ap, out_ap, scrA, scrB):
    V, S = ctx.V, ctx.S
    V.mark(V.e.tensor_single_scalar(scrA[:], ps_ap, 0.0, ALU.min))
    S.wait(V)
    S.mark(S.e.activation(scrB[:], scrA[:], AF.Exp))
    V.wait(S)
    return V.mark(V.e.scalar_tensor_tensor(out_ap, ps_ap, 0.0, scrB[:], ALU.max, ALU.add))


def to_fm(x_tok_d):
    t, d = x_tok_d.shape
    return np.ascontiguousarray(x_tok_d.reshape(t, d // 128, 128).transpose(2, 1, 0))

def from_fm(x_fm):
    p, dch, t = x_fm.shape
    return np.ascontiguousarray(np.asarray(x_fm).transpose(2, 1, 0).reshape(t, dch * 128))

def shard_tokens(x_btd):
    d = x_btd.shape[2]
    return [np.ascontiguousarray(x_btd[:, c * TPC:(c + 1) * TPC, :].reshape(TOK, d))
            for c in range(NCORES)]

def unshard_tokens(shards, d=None):
    d = d if d is not None else shards[0].shape[-1]
    x = np.zeros((B, T, d), np.float32)
    for c, s in enumerate(shards):
        x[:, c * TPC:(c + 1) * TPC, :] = np.asarray(s, dtype=np.float32).reshape(B, TPC, d)
    return x


def build_l0_attn(nc, stage=99):
    H = 16
    xs = nc.dram_tensor("xs", [128, DC, TOK], F32, kind="ExternalInput").ap()
    wqkvg = nc.dram_tensor("wqkvg", [DC, 128, 4096], F32, kind="ExternalInput").ap()
    wout = nc.dram_tensor("wout", [DC, 128, 1024], F32, kind="ExternalInput").ap()
    ctri = nc.dram_tensor("ctri", [128, 128], F32, kind="ExternalInput").ap()
    cones = nc.dram_tensor("cones", [128, 2], F32, kind="ExternalInput").ap()
    conesr = nc.dram_tensor("conesr", [2, 512], F32, kind="ExternalInput").ap()
    pmask = nc.dram_tensor("pmask", [128, 8], F32, kind="ExternalInput").ap()
    cflt = nc.dram_tensor("cflt", [128, 4], F32, kind="ExternalInput").ap()
    xo = nc.dram_tensor("xo", [128, DC, TOK], F32, kind="ExternalOutput").ap()
    ag_in = nc.dram_tensor("ag_in", [128, 32 * 65], BF16)
    ag_out = nc.dram_tensor("ag_out", [8 * 128, 32 * 65], BF16)
    ybuf = nc.dram_tensor("ybuf", [128, DC, TOK], F32)

    with contextlib.ExitStack() as es:
        ctx = Ctx(nc, es)
        T_, V, S, SY, G, CC = ctx.T, ctx.V, ctx.S, ctx.SY, ctx.G, ctx.CC
        sb, ps = ctx.sb, ctx.ps

        x_bf = sb("x_bf", (128, DC, TOK), BF16)
        W = sb("W", (128, DC, 4096), BF16)
        WO = sb("WO", (128, DC, 1024), BF16)
        u0 = sb("u0", (128, 8, 512), BF16)
        vtm = sb("vtm", (128, 4, 1024), BF16)
        kfm = sb("kfm", (128, 8, 512), BF16)
        gat = sb("gat", (128, 8, 512), BF16)
        oall = sb("oall", (128, 8, 512), BF16)
        stA = sb("stA", (128, 32, 65), BF16)
        stR = sb("stR", (128, 16 * 65), F32)
        stRB = sb("stRB", (128, 16 * 65), BF16)
        st_sb = [sb(f"st{i}", (128, 512), BF16) for i in range(4)]
        scrA = sb("scrA", (128, 512), F32)
        scrB = sb("scrB", (128, 512), F32)
        sq_bf = sb("sq_bf", (128, 512), BF16)
        y_bf = sb("y_bf", (128, 512), BF16)
        xchunk = [sb(f"xchunk{i}", (128, 512), F32) for i in range(2)]
        ychunk = [sb(f"ychunk{i}", (128, 512), F32) for i in range(2)]
        dden = sb("dden", (1, 512), F32)
        tri = sb("tri", (128, 128), BF16)
        onec = sb("onec", (128, 2), BF16)
        onerw = sb("onerw", (2, 512), BF16)
        onerf = sb("onerf", (2, 512), F32)
        pmk = sb("pmk", (128, 8), F32)
        cfl = sb("cfl", (128, 4), F32)
        arow = sb("arow", (1, 4, 512), F32)
        brow = sb("brow", (1, 4, 512), F32)

        P = [ps(f"P{i}") for i in range(8)]
        pPA, pPB = P[0], P[1]
        pST = P[2]
        pO = P[3]
        pDen = P[4]
        pBc = P[5]
        pSt = P[6]
        pSt2 = P[7]
        pSum, pSq = pST, pO

        blk = es.enter_context(nc.Block())
        nv = nc.vector; nt = nc.tensor; nsy = nc.sync; ng = nc.gpsimd

        G.mark(ng.dma_start(out=x_bf[:], in_=xs), 16)
        G.mark(ng.dma_start(out=W[:], in_=wqkvg.rearrange("c p f -> p c f")), 16)
        G.mark(ng.dma_start(out=WO[:], in_=wout.rearrange("c p f -> p c f")), 16)
        G.mark(ng.dma_start(out=tri[:], in_=ctri), 16)
        G.mark(ng.dma_start(out=onec[:], in_=cones), 16)
        G.mark(ng.dma_start(out=onerw[:], in_=conesr), 16)
        SY.mark(nsy.dma_start(out=onerf[:], in_=conesr), 16)
        SY.mark(nsy.dma_start(out=pmk[:], in_=pmask), 16)
        SY.mark(nsy.dma_start(out=cfl[:], in_=cflt), 16)
        for i in range(4):
            V.mark(nv.memset(st_sb[i][:], 0.0))
        T_.wait(G)
        T_.e.wait_ge(SY.sem, 48)
        V.wait(G)
        S.wait(SY)
        S.wait(G)

        for b in range(4):
            t0 = b * 512
            for ti in range(4):
                tok0 = t0 + ti * 128
                for of in range(2):
                    for half in range(2):
                        base = 1024 + half * 1024 + of * 512
                        dst_ps = pPA if (ti + of + half) % 2 == 0 else pPB
                        for dc in range(DC):
                            inst = nt.matmul(dst_ps[:], x_bf[:, dc, tok0:tok0 + 128],
                                             W[:, dc, base:base + 512],
                                             start=(dc == 0), stop=(dc == DC - 1))
                        T_.mark(inst)
                        V.wait(T_)
                        if half == 0:
                            elu1_evac(ctx, dst_ps[:], u0[:, ti * 2 + of, :], scrA, scrB)
                        else:
                            S.wait(T_)
                            S.mark(S.e.copy(vtm[:, ti, of * 512:(of + 1) * 512], dst_ps[:]))
                        T_.wait(V); T_.wait(S)
            for h in range(H):
                u2 = b * 8 + h // 2
                hp = (h % 2) * 64
                for kti in range(4):
                    ksl = u0[:, 2 * kti + h // 8, (h % 8) * 64:(h % 8) * 64 + 64]
                    i1 = nt.matmul(pSt[hp:hp + 64, 0:64], ksl,
                                   vtm[:, kti, h * 64:(h + 1) * 64],
                                   start=(kti == 0), stop=(kti == 3))
                    i2 = nt.matmul(pSt2[hp:hp + 64, 0:1], ksl,
                                   onec[:, 0:1], start=(kti == 0), stop=(kti == 3))
                T_.mark(i1); T_.mark(i2)
                V.wait(T_)
                V.mark(nv.tensor_copy(stA[hp:hp + 64, u2, 0:64], pSt[hp:hp + 64, 0:64]))
                V.mark(nv.tensor_copy(stA[hp:hp + 64, u2, 64:65], pSt2[hp:hp + 64, 0:1]))
                T_.wait(V)
        if stage <= 0:
            ng.wait_ge(SY.sem, SY.n); ng.wait_ge(G.sem, G.n)
            return nc
        G.wait(V)
        G.mark(ng.dma_start(out=ag_in.ap(), in_=stA[:].rearrange("p u c -> p (u c)")), 16)
        G.wait(G)
        CC.mark(ng.collective_compute(
            "AllGather", ALU.bypass, replica_groups=[list(range(NCORES))],
            ins=[ag_in.ap().opt()], outs=[ag_out.ap().opt()]))
        G.wait(CC)
        V.wait(SY)
        HALF = 16 * 65
        for hh in range(2):
            V.fence()
            V.mark(nv.memset(stR[:], 0.0))
            for r in range(NCORES):
                G.e.wait_ge(V.sem, V.n)
                G.mark(ng.dma_start(out=stRB[:],
                                    in_=ag_out.ap()[r * 128:(r + 1) * 128, hh * HALF:(hh + 1) * HALF]), 16)
                V.wait(G)
                V.fence()
                V.mark(nv.scalar_tensor_tensor(stR[:], stRB[:], pmk[:, r:r + 1], stR[:],
                                               ALU.mult, ALU.add))
            V.fence()
            V.mark(nv.tensor_copy(
                stA[:].rearrange("p u c -> p (u c)")[:, hh * HALF:(hh + 1) * HALF], stR[:]))

        if stage <= 1:
            ng.wait_ge(SY.sem, SY.n); ng.wait_ge(G.sem, G.n)
            return nc
        for b in range(4):
            t0 = b * 512
            for oc in range(8):
                for tgt, cbase, dst, kind in ((0, 0, u0, 'elu'), (1, 1024, kfm, 'elu'),
                                              (2, 3072, gat, 'sig')):
                    dst_ps = pPA if (oc + tgt) % 2 == 0 else pPB
                    for dc in range(DC):
                        inst = nt.matmul(dst_ps[:],
                                         W[:, dc, cbase + oc * 128:cbase + (oc + 1) * 128],
                                         x_bf[:, dc, t0:t0 + 512],
                                         start=(dc == 0), stop=(dc == DC - 1))
                    T_.mark(inst)
                    dsl = dst[:, oc, :]
                    V.wait(T_)
                    if kind == 'elu':
                        elu1_evac(ctx, dst_ps[:], dsl, scrA, scrB)
                    else:
                        S.wait(T_)
                        S.mark(S.e.activation(dsl, dst_ps[:], AF.Sigmoid))
                    T_.wait(V); T_.wait(S)
            for ti in range(4):
                tok0 = t0 + ti * 128
                for of in range(2):
                    base = 2048 + of * 512
                    dst_ps = pPA if (ti + of) % 2 == 0 else pPB
                    for dc in range(DC):
                        inst = nt.matmul(dst_ps[:], x_bf[:, dc, tok0:tok0 + 128],
                                         W[:, dc, base:base + 512],
                                         start=(dc == 0), stop=(dc == DC - 1))
                    T_.mark(inst)
                    S.wait(T_)
                    S.mark(S.e.copy(vtm[:, ti, of * 512:(of + 1) * 512], dst_ps[:]))
                    T_.wait(S)
            if stage <= 2:
                continue
            for h in range(H):
                u2 = b * 8 + h // 2
                hp = (h % 2) * 64
                hc = h // 2
                q_sl = u0[hp:hp + 64, hc, :]
                for kt in range(4):
                    ksl = kfm[hp:hp + 64, hc, kt * 128:(kt + 1) * 128]
                    T_.mark(nt.matmul(pST[:], ksl, q_sl, start=True, stop=True))
                    S.wait(T_)
                    S.mark(S.e.copy(st_sb[kt][:, kt * 128:512], pST[:, kt * 128:512]))
                    V.wait(S)
                    V.mark(nv.tensor_tensor(st_sb[kt][:, kt * 128:(kt + 1) * 128],
                                            st_sb[kt][:, kt * 128:(kt + 1) * 128], tri[:], ALU.mult))
                    T_.wait(V); T_.wait(S)
                    nt.matmul(pO[hp:hp + 64, :], vtm[:, kt, h * 64:(h + 1) * 64],
                              st_sb[kt][:], start=(kt == 0), stop=False)
                    nt.matmul(pDen[0:1, :], onec[:, 0:1], st_sb[kt][:],
                              start=(kt == 0), stop=False)
                i1 = nt.matmul(pO[hp:hp + 64, :], stA[hp:hp + 64, u2, 0:64], q_sl,
                               start=False, stop=True)
                T_.mark(i1)
                nt.matmul(pDen[0:1, :], stA[hp:hp + 64, u2, 64:65], q_sl, start=False, stop=False)
                i2 = nt.matmul(pDen[0:1, :], onec[0:1, 1:2], onerw[0:1, :], start=False, stop=True)
                T_.mark(i2)
                V.wait(T_)
                V.mark(nv.reciprocal(dden[:], pDen[0:1, :]))
                T_.wait(V)
                T_.mark(nt.matmul(pBc[hp:hp + 64, :], onerf[0:1, 0:64], dden[:], start=True, stop=True))
                V.wait(T_)
                V.mark(nv.tensor_tensor(scrA[hp:hp + 64, :], pO[hp:hp + 64, :],
                                        gat[hp:hp + 64, hc, :], ALU.mult))
                V.fence()
                V.mark(nv.tensor_tensor(oall[hp:hp + 64, hc, :],
                                        scrA[hp:hp + 64, :], pBc[hp:hp + 64, :], ALU.mult))
                T_.wait(V)
            if stage <= 3:
                continue
            for oc in range(DC):
                dst_ps = pPA if oc % 2 == 0 else pPB
                for fc in range(DC):
                    inst = nt.matmul(dst_ps[:], WO[:, fc, oc * 128:(oc + 1) * 128],
                                     oall[:, fc, :], start=(fc == 0), stop=(fc == DC - 1))
                T_.mark(inst)
                xb = xchunk[oc % 2]
                SY.wait(V)
                SY.mark(nsy.dma_start(out=xb[:], in_=xs[:, oc, t0:t0 + 512]), 16)
                V.wait(T_); V.wait(SY); V.wait(S)
                yb = ychunk[oc % 2]
                V.mark(nv.tensor_tensor(yb[:], dst_ps[:], xb[:], ALU.add))
                T_.wait(V)
                S.wait(V)
                S.mark(S.e.copy(y_bf[:], yb[:]))
                S.mark(S.e.activation(sq_bf[:], yb[:], AF.Square))
                SY.wait(V)
                SY.mark(nsy.dma_start(out=ybuf.ap()[:, oc, t0:t0 + 512], in_=yb[:]), 16)
                T_.wait(S)
                i1 = nt.matmul(pSum[0:1, :], onec[:, 0:1], y_bf[:],
                               start=(oc == 0), stop=(oc == DC - 1))
                i2 = nt.matmul(pSq[0:1, :], onec[:, 0:1], sq_bf[:],
                               start=(oc == 0), stop=(oc == DC - 1))
                T_.mark(i1); T_.mark(i2)
                S.wait(T_)
            V.wait(T_)
            V.fence()
            V.mark(nv.tensor_scalar(dden[:], pSum[0:1, :], 1.0 / D, None, ALU.mult))
            V.mark(nv.tensor_scalar(scrA[0:1, :], pSq[0:1, :], 1.0 / D, None, ALU.mult))
            V.fence()
            V.mark(nv.tensor_tensor(scrB[0:1, :], dden[:], dden[:], ALU.mult))
            V.fence()
            V.mark(nv.tensor_tensor(scrB[0:1, :], scrA[0:1, :], scrB[0:1, :], ALU.subtract))
            S.wait(V)
            S.mark(S.e.activation(scrA[0:1, :], scrB[0:1, :], AF.Sqrt, bias=cfl[0:1, 0:1]))
            V.wait(S)
            V.mark(nv.reciprocal(arow[:, b, :], scrA[0:1, :]))
            V.fence()
            V.mark(nv.tensor_tensor(scrB[0:1, :], dden[:], arow[:, b, :], ALU.mult))
            V.fence()
            V.mark(nv.tensor_scalar(brow[:, b, :], scrB[0:1, :], -1.0, None, ALU.mult))
            T_.wait(V)
        if stage <= 4:
            ng.wait_ge(SY.sem, SY.n); ng.wait_ge(G.sem, G.n)
            return nc
        for trow in range(4):
            tok0 = trow * 512
            T_.wait(V)
            ia = nt.matmul(pDen[:], onerf[0:1, 0:128], arow[:, trow, :], start=True, stop=True)
            ib = nt.matmul(pBc[:], onerf[0:1, 0:128], brow[:, trow, :], start=True, stop=True)
            T_.mark(ia); T_.mark(ib)
            for oc in range(DC):
                yb = ychunk[oc % 2]
                ob = xchunk[oc % 2]
                SY.wait(V)
                SY.mark(nsy.dma_start(out=yb[:], in_=ybuf.ap()[:, oc, tok0:tok0 + 512]), 16)
                V.wait(SY); V.wait(T_)
                V.mark(nv.tensor_tensor(ob[:], yb[:], pDen[:], ALU.mult))
                V.fence()
                V.mark(nv.tensor_tensor(ob[:], ob[:], pBc[:], ALU.add))
                SY.wait(V)
                SY.mark(nsy.dma_start(out=xo[:, oc, tok0:tok0 + 512], in_=ob[:]), 16)
            T_.wait(V)
        ng.wait_ge(SY.sem, SY.n)
        ng.wait_ge(G.sem, G.n)
    return nc

def host_l0_attn(ins, x_shards):
    wq = ins["L0_attn_w_qkv"].astype(np.float32)
    wg = ins["L0_attn_w_gate"].astype(np.float32)
    wo = ins["L0_attn_w_out"].astype(np.float32)
    wqkvg = np.ascontiguousarray(np.concatenate([wq, wg], axis=1).reshape(DC, 128, 4096))
    wof = np.ascontiguousarray(wo.reshape(DC, 128, 1024))
    tri = np.ascontiguousarray(np.tril(np.ones((128, 128), np.float32)).T)
    cones = np.zeros((128, 2), np.float32); cones[:, 0] = 1.0; cones[0, 1] = 1e-6
    conesr = np.zeros((2, 512), np.float32); conesr[0, :] = 1.0
    maps = []
    for c in range(NCORES):
        pm = np.zeros((128, 8), np.float32)
        pm[:, :c] = 1.0
        cf = np.zeros((128, 4), np.float32); cf[:, 0] = 1e-5
        maps.append(dict(xs=to_fm(x_shards[c]), wqkvg=wqkvg, wout=wof, ctri=tri,
                         cones=cones, conesr=conesr, pmask=pm, cflt=cf))
    return maps




def build_l0_ffn(nc):
    xs = nc.dram_tensor("xs", [128, DC, TOK], F32, kind="ExternalInput").ap()
    wqkv = nc.dram_tensor("wqkv", [DC, 128, 3072], F32, kind="ExternalInput").ap()
    wproj = nc.dram_tensor("wproj", [DC, 128, 1024], F32, kind="ExternalInput").ap()
    wrout = nc.dram_tensor("wrout", [128, DC * 8], F32, kind="ExternalInput").ap()
    w1 = nc.dram_tensor("w1", [8, DC, 128, 2048], F32, kind="ExternalInput").ap()
    w2 = nc.dram_tensor("w2", [8, 16, 128, 1024], F32, kind="ExternalInput").ap()
    lnw = nc.dram_tensor("lnw", [128, 4 * DC], F32, kind="ExternalInput").ap()
    cmh = nc.dram_tensor("cmh", [128, DC * 16], F32, kind="ExternalInput").ap()
    cmhT = nc.dram_tensor("cmhT", [16, DC * 128], F32, kind="ExternalInput").ap()
    cidt = nc.dram_tensor("cidt", [128, 128], F32, kind="ExternalInput").ap()
    conesr = nc.dram_tensor("conesr", [2, 512], F32, kind="ExternalInput").ap()
    cones = nc.dram_tensor("cones", [128, 2], F32, kind="ExternalInput").ap()
    cflt = nc.dram_tensor("cflt", [128, 4], F32, kind="ExternalInput").ap()
    csel = nc.dram_tensor("csel", [8, 8 * 128], F32, kind="ExternalInput").ap()
    xo = nc.dram_tensor("xo", [128, DC, TOK], F32, kind="ExternalOutput").ap()
    ybuf = nc.dram_tensor("ybuf", [128, DC, TOK], F32)

    with contextlib.ExitStack() as es:
        ctx = Ctx(nc, es)
        T_, V, S, SY, G, CC = ctx.T, ctx.V, ctx.S, ctx.SY, ctx.G, ctx.CC
        sb, ps = ctx.sb, ctx.ps
        nv = nc.vector; nt = nc.tensor; nsy = nc.sync; ng = nc.gpsimd

        A = sb("A", (128, 16384), BF16)
        Bl = sb("Bl", (128, 16384), BF16)
        Q = sb("Q", (128, 16384), BF16)
        K = sb("K", (128, 16384), BF16)
        moe = sb("moe", (128, 8, 512), F32)
        wsl = sb("wsl", (128, 2, 2048), BF16)
        wrt = sb("wrt", (128, DC, 8), BF16)
        scrA = sb("scrA", (128, 512), F32)
        scrB = sb("scrB", (128, 512), F32)
        sq_bf = sb("sq_bf", (128, 512), BF16)
        y_bf = sb("y_bf", (128, 512), BF16)
        _xc = sb("xchunk0", (128, 512), F32)
        xchunk = [_xc, _xc]
        ychunk = [sb(f"ychunk{i}", (128, 512), F32) for i in range(2)]
        wbc = sb("wbc", (128, 512), BF16)
        sc_sb = sb("sc_sb", (16, 4, 512), F32)
        prb = sb("prb", (16, 4, 512), BF16)
        tm8 = sb("tm8", (128, 8), F32)
        tm8b = sb("tm8b", (128, 8), F32)
        tm8m = sb("tm8m", (128, 1), F32)
        tm8c = sb("tm8c", (128, 8), BF16)
        wrows = sb("wrows", (8, 2048), BF16)
        mh = sb("mh", (128, DC, 16), BF16)
        mhT = sb("mhT", (16, DC, 128), BF16)
        idt = sb("idt", (128, 128), BF16)
        sel = sb("sel", (8, 8, 128), BF16)
        onerw = sb("onerw", (2, 512), BF16)
        onerf = sb("onerf", (2, 512), F32)
        onec = sb("onec", (128, 2), BF16)
        cfl = sb("cfl", (128, 4), F32)
        lnww = sb("lnww", (128, 4, DC), F32)
        arow = sb("arow", (1, 4, 512), F32)
        brow = sb("brow", (1, 4, 512), F32)

        P = [ps(f"P{i}") for i in range(7)]
        P7b = ps("P7b", (128, 1024), BF16)
        pPA, pPB = P[0], P[1]
        pSum, pSq = P[2], P[3]
        pBa, pBb = P[4], P[5]
        pX = P[6]

        blk = es.enter_context(nc.Block())

        G.mark(ng.dma_start(out=mh[:].rearrange("p c f -> p (c f)"), in_=cmh), 16)
        G.mark(ng.dma_start(out=mhT[:].rearrange("p c f -> p (c f)"), in_=cmhT), 16)
        G.mark(ng.dma_start(out=idt[:], in_=cidt), 16)
        G.mark(ng.dma_start(out=sel[:].rearrange("p c f -> p (c f)"), in_=csel), 16)
        G.mark(ng.dma_start(out=onerw[:], in_=conesr), 16)
        G.mark(ng.dma_start(out=onec[:], in_=cones), 16)
        G.mark(ng.dma_start(out=wrt[:].rearrange("p c f -> p (c f)"), in_=wrout), 16)
        SD = Eng(nc, es, nc.scalar, "SD")
        SD.mark(nc.scalar.dma_start(out=onerf[:], in_=conesr), 16)
        SD.mark(nc.scalar.dma_start(out=cfl[:], in_=cflt), 16)
        SD.mark(nc.scalar.dma_start(out=lnww[:].rearrange("p c f -> p (c f)"), in_=lnw), 16)
        T_.wait(G); V.wait(G); S.wait(G)
        T_.wait(SD)
        V.wait(SD); S.wait(SD)
        G.fence()

        wsem = [es.enter_context(nc.semaphore("wsl0")), es.enter_context(nc.semaphore("wsl1"))]
        wcnt = [0, 0]
        wcons = [0, 0]

        def wsl_load(slot, src_ap, n):
            G.e.wait_ge(T_.sem, wcons[slot])
            c = src_ap.shape[0]
            dst = wsl[:, slot, 0:n].rearrange("p (c f) -> p c f", c=c)
            ng.dma_start(out=dst, in_=src_ap.rearrange("c p f -> p c f")).then_inc(wsem[slot], 16)
            wcnt[slot] += 16

        def wsl_wait(slot):
            T_.e.wait_ge(wsem[slot], wcnt[slot])

        def wsl_done(slot):
            wcons[slot] = T_.n

        def stats_finish(b):
            V.wait(T_)
            V.wait(SY)
            V.fence()
            V.mark(nv.tensor_scalar(ychunk[1][0:1, :], pSum[0:1, :], 1.0 / D, None, ALU.mult))
            V.mark(nv.tensor_scalar(scrA[0:1, :], pSq[0:1, :], 1.0 / D, None, ALU.mult))
            V.fence()
            V.mark(nv.tensor_tensor(scrB[0:1, :], ychunk[1][0:1, :], ychunk[1][0:1, :], ALU.mult))
            V.fence()
            V.mark(nv.tensor_tensor(scrB[0:1, :], scrA[0:1, :], scrB[0:1, :], ALU.subtract))
            S.wait(V)
            S.mark(S.e.activation(scrA[0:1, :], scrB[0:1, :], AF.Sqrt, bias=cfl[0:1, 0:1]))
            V.wait(S)
            V.mark(nv.reciprocal(arow[:, b, :], scrA[0:1, :]))
            V.fence()
            V.mark(nv.tensor_tensor(scrB[0:1, :], ychunk[1][0:1, :], arow[:, b, :], ALU.mult))
            V.fence()
            V.mark(nv.tensor_scalar(brow[:, b, :], scrB[0:1, :], -1.0, None, ALU.mult))
            T_.wait(V)

        def bcast_rows(tr):
            T_.wait(V)
            ia = nt.matmul(pBa[:], onerf[0:1, 0:128], arow[:, tr, :], start=True, stop=True)
            ib = nt.matmul(pBb[:], onerf[0:1, 0:128], brow[:, tr, :], start=True, stop=True)
            T_.mark(ia)
            T_.mark(ib)

        for b in range(4):
            t0 = b * 512
            for oc in range(DC):
                xb = xchunk[oc % 2]
                SY.wait(S)
                SY.wait(V)
                SY.mark(nsy.dma_start(out=xb[:], in_=xs[:, oc, t0:t0 + 512]), 16)
                S.wait(SY)
                S.wait(T_)
                S.mark(S.e.copy(y_bf[:], xb[:]))
                S.mark(S.e.activation(sq_bf[:], xb[:], AF.Square))
                T_.wait(S)
                i1 = nt.matmul(pSum[0:1, :], onec[:, 0:1], y_bf[:], start=(oc == 0), stop=(oc == DC - 1))
                i2 = nt.matmul(pSq[0:1, :], onec[:, 0:1], sq_bf[:], start=(oc == 0), stop=(oc == DC - 1))
                T_.mark(i1)
                T_.mark(i2)
            stats_finish(b)
            bcast_rows(b)
            for oc in range(DC):
                xb = xchunk[oc % 2]
                SY.wait(V)
                SY.wait(S)
                SY.mark(nsy.dma_start(out=xb[:], in_=xs[:, oc, t0:t0 + 512]), 16)
                V.wait(SY)
                V.wait(T_)
                V.fence()
                V.mark(nv.tensor_tensor(scrA[:], xb[:], pBa[:], ALU.mult))
                V.fence()
                V.mark(nv.tensor_tensor(scrA[:], scrA[:], pBb[:], ALU.add))
                V.fence()
                V.mark(nv.tensor_scalar(Bl[:, oc * 2048 + t0: oc * 2048 + t0 + 512], scrA[:],
                                        lnww[:, 0, oc:oc + 1], lnww[:, 1, oc:oc + 1],
                                        ALU.mult, ALU.add))
            T_.wait(V)
        wsl_load(0, wqkv[:, :, 0:128], 1024)
        for oc24 in range(24):
            tgt = Q if oc24 < 8 else (K if oc24 < 16 else A)
            ocl = oc24 % 8
            slot = oc24 % 2
            if oc24 + 1 < 24:
                wsl_load((oc24 + 1) % 2, wqkv[:, :, (oc24 + 1) * 128:(oc24 + 2) * 128], 1024)
            wsl_wait(slot)
            for b in range(4):
                t0 = b * 512
                dst_ps = pPA if (oc24 + b) % 2 == 0 else pPB
                for dc in range(DC):
                    inst = nt.matmul(dst_ps[:], wsl[:, slot, dc * 128:(dc + 1) * 128],
                                     Bl[:, dc * 2048 + t0: dc * 2048 + t0 + 512],
                                     start=(dc == 0), stop=(dc == DC - 1))
                T_.mark(inst)
                S.wait(T_)
                S.mark(S.e.copy(tgt[:, ocl * 2048 + t0: ocl * 2048 + t0 + 512], dst_ps[:]))
                T_.wait(S)
            wsl_done(slot)
        for s in range(4):
            for t in range(s + 1):
                for hc in range(DC):
                    V.wait(T_)
                    V.fence()
                    V.mark(nv.tensor_tensor(sq_bf[:], Q[:, hc * 2048 + s * 512: hc * 2048 + (s + 1) * 512],
                                            K[:, hc * 2048 + t * 512: hc * 2048 + (t + 1) * 512], ALU.mult))
                    T_.wait(V)
                    T_.mark(nt.matmul(P[2 + t][0:16, :], mh[:, hc, :], sq_bf[:],
                                      start=(hc == 0), stop=(hc == DC - 1)))
                V.wait(T_)
                V.fence()
                V.mark(nv.tensor_scalar(sc_sb[:, t, :], P[2 + t][0:16, :], 0.125, None, ALU.mult))
            V.fence()
            V.mark(nv.tensor_copy(scrA[0:16, :], sc_sb[:, 0, :]))
            for t in range(1, s + 1):
                V.fence()
                V.mark(nv.tensor_tensor(scrA[0:16, :], scrA[0:16, :], sc_sb[:, t, :], ALU.max))
            for t in range(s + 1):
                V.fence()
                V.mark(nv.tensor_tensor(sc_sb[:, t, :], sc_sb[:, t, :], scrA[0:16, :], ALU.subtract))
            S.wait(V)
            for t in range(s + 1):
                S.mark(S.e.activation(sc_sb[:, t, :], sc_sb[:, t, :], AF.Exp))
            V.wait(S)
            V.mark(nv.tensor_copy(scrB[0:16, :], sc_sb[:, 0, :]))
            for t in range(1, s + 1):
                V.fence()
                V.mark(nv.tensor_tensor(scrB[0:16, :], scrB[0:16, :], sc_sb[:, t, :], ALU.add))
            V.fence()
            V.mark(nv.reciprocal(scrB[0:16, :], scrB[0:16, :]))
            for t in range(s + 1):
                V.fence()
                V.mark(nv.tensor_tensor(prb[:, t, :], sc_sb[:, t, :], scrB[0:16, :], ALU.mult))
            for hc in range(DC):
                for t in range(s + 1):
                    T_.wait(V)
                    T_.mark(nt.matmul(pX[:], mhT[:, hc, :], prb[:, t, :], start=True, stop=True))
                    V.wait(T_)
                    dsl = Bl[:, hc * 2048 + s * 512: hc * 2048 + (s + 1) * 512]
                    vsl = A[:, hc * 2048 + t * 512: hc * 2048 + (t + 1) * 512]
                    if t == 0:
                        V.mark(nv.tensor_tensor(dsl, pX[:], vsl, ALU.mult))
                    else:
                        V.mark(nv.tensor_tensor(scrA[:], pX[:], vsl, ALU.mult))
                        V.fence()
                        V.mark(nv.tensor_tensor(dsl, dsl, scrA[:], ALU.add))
        wsl_load(0, wproj[:, :, 0:128], 1024)
        for b in range(4):
            t0 = b * 512
            for oc in range(DC):
                li = b * 8 + oc
                slot = li % 2
                if li + 1 < 32:
                    noc = (li + 1) % 8
                    wsl_load((li + 1) % 2, wproj[:, :, noc * 128:(noc + 1) * 128], 1024)
                wsl_wait(slot)
                dst_ps = pPA if oc % 2 == 0 else pPB
                for fc in range(DC):
                    inst = nt.matmul(dst_ps[:], wsl[:, slot, fc * 128:(fc + 1) * 128],
                                     Bl[:, fc * 2048 + t0: fc * 2048 + t0 + 512],
                                     start=(fc == 0), stop=(fc == DC - 1))
                T_.mark(inst)
                wsl_done(slot)
                xb = xchunk[oc % 2]
                SY.wait(V)
                SY.wait(S)
                SY.mark(nsy.dma_start(out=xb[:], in_=xs[:, oc, t0:t0 + 512]), 16)
                V.wait(T_)
                V.wait(SY)
                V.wait(S)
                yb = ychunk[oc % 2]
                V.mark(nv.tensor_tensor(yb[:], dst_ps[:], xb[:], ALU.add))
                T_.wait(V)
                S.wait(V)
                S.mark(S.e.copy(y_bf[:], yb[:]))
                S.mark(S.e.activation(sq_bf[:], yb[:], AF.Square))
                SY.wait(V)
                SY.mark(nsy.dma_start(out=ybuf.ap()[:, oc, t0:t0 + 512], in_=yb[:]), 16)
                T_.wait(S)
                i1 = nt.matmul(pSum[0:1, :], onec[:, 0:1], y_bf[:], start=(oc == 0), stop=(oc == DC - 1))
                i2 = nt.matmul(pSq[0:1, :], onec[:, 0:1], sq_bf[:], start=(oc == 0), stop=(oc == DC - 1))
                T_.mark(i1)
                T_.mark(i2)
                S.wait(T_)
            stats_finish(b)
            bcast_rows(b)
            for oc in range(DC):
                yb = ychunk[oc % 2]
                SY.wait(V)
                SY.mark(nsy.dma_start(out=yb[:], in_=ybuf.ap()[:, oc, t0:t0 + 512]), 16)
                V.wait(SY)
                V.wait(T_)
                V.fence()
                V.mark(nv.tensor_tensor(scrA[:], yb[:], pBa[:], ALU.mult))
                V.fence()
                V.mark(nv.tensor_tensor(scrA[:], scrA[:], pBb[:], ALU.add))
                V.fence()
                V.mark(nv.tensor_scalar(A[:, oc * 2048 + t0: oc * 2048 + t0 + 512], scrA[:],
                                        lnww[:, 2, oc:oc + 1], lnww[:, 3, oc:oc + 1],
                                        ALU.mult, ALU.add))
            T_.wait(V)
        for tt in range(4):
            t0 = tt * 512
            for dc in range(DC):
                inst = nt.matmul(pX[0:8, :], wrt[:, dc, :],
                                 A[:, dc * 2048 + t0: dc * 2048 + t0 + 512],
                                 start=(dc == 0), stop=(dc == DC - 1))
            T_.mark(inst)
            S.wait(T_)
            S.mark(S.e.copy(prb[0:8, 0, :], pX[0:8, :]))
            T_.wait(S)
            for q4 in range(4):
                T_.wait(V)
                T_.mark(nt.transpose(P7b[:, 0:8], prb[0:8, 0, q4 * 128:(q4 + 1) * 128], idt[0:8, 0:8]))
                V.wait(T_)
                V.mark(nv.tensor_copy(tm8[:], P7b[:, 0:8]))
                V.fence()
                V.mark(nv.reduce_max(tm8m[:], tm8[:], axis=AX.X, negate=True))
                V.fence()
                V.mark(nv.tensor_scalar(tm8[:], tm8[:], tm8m[:, 0:1], None, ALU.add))
                S.wait(V)
                S.mark(S.e.activation(tm8[:], tm8[:], AF.Exp))
                V.wait(S)
                V.mark(nv.reduce_sum(tm8m[:], tm8[:], axis=AX.X))
                V.fence()
                V.mark(nv.reciprocal(tm8m[:], tm8m[:]))
                V.fence()
                V.mark(nv.tensor_scalar(tm8[:], tm8[:], tm8m[:, 0:1], None, ALU.mult))
                V.fence()
                V.mark(nv.reduce_max(tm8m[:], tm8[:], axis=AX.X))
                V.fence()
                V.mark(nv.tensor_scalar(tm8b[:], tm8[:], tm8m[:, 0:1], None, ALU.is_ge))
                V.fence()
                V.mark(nv.scalar_tensor_tensor(tm8b[:], tm8b[:], 2.0, tm8[:], ALU.mult, ALU.subtract))
                V.fence()
                V.mark(nv.tensor_scalar(tm8b[:], tm8b[:], -1.0, None, ALU.mult))
                V.fence()
                V.mark(nv.reduce_max(tm8m[:], tm8b[:], axis=AX.X))
                V.fence()
                V.mark(nv.tensor_scalar(tm8b[:], tm8[:], tm8m[:, 0:1], None, ALU.is_ge))
                V.fence()
                V.mark(nv.tensor_tensor(tm8c[:], tm8[:], tm8b[:], ALU.mult))
                T_.wait(V)
                T_.mark(nt.transpose(P7b[0:8, 0:128], tm8c[:], idt[:]))
                V.wait(T_)
                V.mark(nv.tensor_copy(wrows[:, t0 + q4 * 128: t0 + (q4 + 1) * 128], P7b[0:8, 0:128]))
        for slab in range(4):
            s0 = slab * 512
            V.fence()
            V.mark(nv.memset(moe[:], 0.0))
            for e in range(8):
                T_.wait(V)
                T_.wait(S)
                T_.mark(nt.matmul(pBa[:], sel[:, e, :], wrows[:, s0:s0 + 512],
                                  start=True, stop=True))
                S.wait(T_)
                S.mark(S.e.copy(wbc[:], pBa[:]))
                wsl_load(0, w1[e, :, :, 0:256], 2048)
                for fj in range(8):
                    slot = fj % 2
                    if fj + 1 < 8:
                        wsl_load((fj + 1) % 2, w1[e, :, :, (fj + 1) * 256:(fj + 2) * 256], 2048)
                    wsl_wait(slot)
                    for fi in range(2):
                        fc = fj * 2 + fi
                        dst_ps = pPA if fc % 2 == 0 else pPB
                        for dc in range(DC):
                            inst = nt.matmul(dst_ps[:], wsl[:, slot, dc * 256 + fi * 128: dc * 256 + (fi + 1) * 128],
                                             A[:, dc * 2048 + s0: dc * 2048 + s0 + 512],
                                             start=(dc == 0), stop=(dc == DC - 1))
                        T_.mark(inst)
                        S.wait(T_)
                        S.mark(S.e.activation(Bl[:, fc * 512: (fc + 1) * 512], dst_ps[:], AF.Gelu_apprx_tanh))
                        T_.wait(S)
                    wsl_done(slot)
                wsl_load(0, w2[e, :, :, 0:128], 2048)
                for oc in range(8):
                    slot = oc % 2
                    if oc + 1 < 8:
                        wsl_load((oc + 1) % 2, w2[e, :, :, (oc + 1) * 128:(oc + 2) * 128], 2048)
                    wsl_wait(slot)
                    dst_ps = pPA if oc % 2 == 0 else pPB
                    for fc in range(16):
                        inst = nt.matmul(dst_ps[:], wsl[:, slot, fc * 128:(fc + 1) * 128],
                                         Bl[:, fc * 512: (fc + 1) * 512],
                                         start=(fc == 0), stop=(fc == 15))
                    T_.mark(inst)
                    wsl_done(slot)
                    V.wait(T_)
                    V.wait(S)
                    V.fence()
                    V.mark(nv.tensor_tensor(scrA[:], dst_ps[:], wbc[:], ALU.mult))
                    V.fence()
                    V.mark(nv.tensor_tensor(moe[:, oc, :], moe[:, oc, :], scrA[:], ALU.add))
                    T_.wait(V)
            for oc in range(DC):
                xb = xchunk[oc % 2]
                SY.wait(V)
                SY.wait(S)
                SY.mark(nsy.dma_start(out=xb[:], in_=ybuf.ap()[:, oc, s0:s0 + 512]), 16)
                SY.mark(nsy.dma_start(out=scrB[:], in_=xs[:, oc, s0:s0 + 512]), 16)
                V.wait(SY)
                V.wait(S)
                yb = ychunk[oc % 2]
                V.fence()
                V.mark(nv.tensor_tensor(yb[:], moe[:, oc, :], xb[:], ALU.add))
                V.fence()
                V.mark(nv.tensor_tensor(yb[:], yb[:], scrB[:], ALU.add))
                S.wait(V)
                S.mark(S.e.copy(y_bf[:], yb[:]))
                S.mark(S.e.activation(sq_bf[:], yb[:], AF.Square))
                SY.wait(V)
                SY.mark(nsy.dma_start(out=ybuf.ap()[:, oc, s0:s0 + 512], in_=yb[:]), 16)
                T_.wait(S)
                i1 = nt.matmul(pSum[0:1, :], onec[:, 0:1], y_bf[:], start=(oc == 0), stop=(oc == DC - 1))
                i2 = nt.matmul(pSq[0:1, :], onec[:, 0:1], sq_bf[:], start=(oc == 0), stop=(oc == DC - 1))
                T_.mark(i1)
                T_.mark(i2)
                S.wait(T_)
            stats_finish(slab)
        for b in range(4):
            t0 = b * 512
            bcast_rows(b)
            for oc in range(DC):
                yb = ychunk[oc % 2]
                ob = xchunk[oc % 2]
                SY.wait(V)
                SY.mark(nsy.dma_start(out=yb[:], in_=ybuf.ap()[:, oc, t0:t0 + 512]), 16)
                V.wait(SY)
                V.wait(T_)
                V.fence()
                V.mark(nv.tensor_tensor(ob[:], yb[:], pBa[:], ALU.mult))
                V.fence()
                V.mark(nv.tensor_tensor(ob[:], ob[:], pBb[:], ALU.add))
                SY.wait(V)
                SY.mark(nsy.dma_start(out=xo[:, oc, t0:t0 + 512], in_=ob[:]), 16)
            T_.wait(V)
        ng.wait_ge(SY.sem, SY.n)
        ng.wait_ge(G.sem, G.n)
        ng.wait_ge(wsem[0], wcnt[0])
        ng.wait_ge(wsem[1], wcnt[1])
    return nc


def host_l0_ffn(ins):
    NH, NKV = 16, 4
    qkvw = ins["L0_ffn_qkv_weight"].astype(np.float32)
    wq = qkvw[0:1024].T
    idx = np.repeat(np.arange(NKV), NH // NKV)
    colk = (idx[:, None] * 64 + np.arange(64)[None, :]).reshape(-1)
    wk = qkvw[1024:1280].T[:, colk]
    wv = qkvw[1280:1536].T[:, colk]
    wqkv = np.ascontiguousarray(np.concatenate([wq, wk, wv], 1).reshape(DC, 128, 3072))
    wproj = np.ascontiguousarray(ins["L0_ffn_proj_weight"].astype(np.float32).T.reshape(DC, 128, 1024))
    wrout = np.ascontiguousarray(ins["L0_ffn_router_weight"].astype(np.float32).T.reshape(DC, 128, 8).transpose(1, 0, 2).reshape(128, DC * 8))
    w1 = np.ascontiguousarray(ins["L0_ffn_moe_w1"].astype(np.float32).reshape(8, DC, 128, 2048))
    w2 = np.ascontiguousarray(ins["L0_ffn_moe_w2"].astype(np.float32).reshape(8, 16, 128, 1024))
    lnw = np.stack([ins["L0_ffn_ln1_weight"], ins["L0_ffn_ln1_bias"],
                    ins["L0_ffn_ln2_weight"], ins["L0_ffn_ln2_bias"]], 0).astype(np.float32)
    lnw = np.ascontiguousarray(lnw.reshape(4, DC, 128).transpose(2, 0, 1).reshape(128, 4 * DC))
    mh = np.zeros((128, DC, 16), np.float32)
    for hc in range(DC):
        mh[0:64, hc, 2 * hc] = 1.0
        mh[64:128, hc, 2 * hc + 1] = 1.0
    mhT = np.ascontiguousarray(mh.transpose(2, 1, 0).reshape(16, DC * 128))
    idt = np.eye(128, dtype=np.float32)
    conesr = np.zeros((2, 512), np.float32); conesr[0] = 1.0
    cones = np.zeros((128, 2), np.float32); cones[:, 0] = 1.0
    cflt = np.zeros((128, 4), np.float32); cflt[:, 0] = 1e-5
    csel = np.zeros((8, 8, 128), np.float32)
    for e in range(8):
        csel[e, e, :] = 1.0
    csel = csel.reshape(8, 8 * 128)
    return dict(wqkv=wqkv, wproj=wproj, wrout=wrout, w1=w1, w2=w2, lnw=lnw,
                cmh=np.ascontiguousarray(mh.reshape(128, DC * 16)), cmhT=mhT, cidt=idt,
                conesr=conesr, cones=cones, cflt=cflt, csel=csel)




def std_setup(ctx, nc, es, lite=False):
    class NS: pass
    h = NS()
    sb, ps = ctx.sb, ctx.ps
    nv = nc.vector
    T_, V, S, SY, G = ctx.T, ctx.V, ctx.S, ctx.SY, ctx.G
    h.scrA = sb("scrA", (128, 512), F32)
    h.scrB = sb("scrB", (128, 512), F32)
    h.sq_bf = sb("sq_bf", (128, 512), BF16)
    h.y_bf = sb("y_bf", (128, 512), BF16)
    if not lite:
        h.xchunk = [sb(f"xchunk{i}", (128, 512), F32) for i in range(2)]
        h.ychunk = [sb(f"ychunk{i}", (128, 512), F32) for i in range(2)]
    h.onerw = sb("onerw", (2, 512), BF16)
    h.onerf = sb("onerf", (2, 512), F32)
    h.onec = sb("onec", (128, 2), BF16)
    h.cfl = sb("cfl", (128, 4), F32)
    if not lite:
        h.arow = sb("arow", (1, 4, 512), F32)
        h.brow = sb("brow", (1, 4, 512), F32)
    h.P = [ps(f"P{i}") for i in range(7)]
    h.pPA, h.pPB = h.P[0], h.P[1]
    h.pSum, h.pSq = h.P[2], h.P[3]
    h.pBa, h.pBb = h.P[4], h.P[5]
    h.pX = h.P[6]
    h.wsl = None if lite else sb("wsl", (128, 2, 4096), BF16)
    h.wsem = [es.enter_context(nc.semaphore("wsl0")), es.enter_context(nc.semaphore("wsl1"))]
    h.wcnt = [0, 0]
    h.wcons = [0, 0]

    def wsl_load(slot, src_ap, n):
        ctx.G.e.wait_ge(ctx.T.sem, h.wcons[slot])
        c = src_ap.shape[0]
        dst = h.wsl[:, slot, 0:n].rearrange("p (c f) -> p c f", c=c)
        nc.gpsimd.dma_start(out=dst, in_=src_ap.rearrange("c p f -> p c f")).then_inc(h.wsem[slot], 16)
        h.wcnt[slot] += 16

    def wsl_wait(slot):
        ctx.T.e.wait_ge(h.wsem[slot], h.wcnt[slot])

    def wsl_done(slot):
        h.wcons[slot] = ctx.T.n

    def wsl_finish(eng):
        eng.wait_ge(h.wsem[0], h.wcnt[0])
        eng.wait_ge(h.wsem[1], h.wcnt[1])

    h.wsl_load, h.wsl_wait, h.wsl_done, h.wsl_finish = wsl_load, wsl_wait, wsl_done, wsl_finish

    def load_std_consts(conesr, cones, cflt):
        SD = Eng(nc, es, nc.scalar, "SD")
        h.SD = SD
        ctx.G.mark(nc.gpsimd.dma_start(out=h.onerw[:], in_=conesr), 16)
        ctx.G.mark(nc.gpsimd.dma_start(out=h.onec[:], in_=cones), 16)
        SD.mark(nc.scalar.dma_start(out=h.onerf[:], in_=conesr), 16)
        SD.mark(nc.scalar.dma_start(out=h.cfl[:], in_=cflt), 16)
    h.load_std_consts = load_std_consts

    def stats_finish(b):
        V.wait(T_); V.wait(SY)
        V.fence()
        V.mark(nv.tensor_scalar(h.ychunk[1][0:1, :], h.pSum[0:1, :], 1.0 / D, None, ALU.mult))
        V.mark(nv.tensor_scalar(h.scrA[0:1, :], h.pSq[0:1, :], 1.0 / D, None, ALU.mult))
        V.fence()
        V.mark(nv.tensor_tensor(h.scrB[0:1, :], h.ychunk[1][0:1, :], h.ychunk[1][0:1, :], ALU.mult))
        V.fence()
        V.mark(nv.tensor_tensor(h.scrB[0:1, :], h.scrA[0:1, :], h.scrB[0:1, :], ALU.subtract))
        S.wait(V)
        S.mark(S.e.activation(h.scrA[0:1, :], h.scrB[0:1, :], AF.Sqrt, bias=h.cfl[0:1, 0:1]))
        V.wait(S)
        V.mark(nv.reciprocal(h.arow[:, b, :], h.scrA[0:1, :]))
        V.fence()
        V.mark(nv.tensor_tensor(h.scrB[0:1, :], h.ychunk[1][0:1, :], h.arow[:, b, :], ALU.mult))
        V.fence()
        V.mark(nv.tensor_scalar(h.brow[:, b, :], h.scrB[0:1, :], -1.0, None, ALU.mult))
        T_.wait(V)
    h.stats_finish = stats_finish

    def bcast_rows(tr):
        T_.wait(V)
        ia = nc.tensor.matmul(h.pBa[:], h.onerf[0:1, 0:128], h.arow[:, tr, :], start=True, stop=True)
        ib = nc.tensor.matmul(h.pBb[:], h.onerf[0:1, 0:128], h.brow[:, tr, :], start=True, stop=True)
        T_.mark(ia); T_.mark(ib)
    h.bcast_rows = bcast_rows

    def resid_stats_chunk(oc, n_oc, y_from, xs_ap, t0, ybuf_ap, extra_ap=None):
        nsy = nc.sync
        xb = h.xchunk[oc % 2]
        SY.wait(V); SY.wait(S)
        SY.mark(nsy.dma_start(out=xb[:], in_=xs_ap[:, oc, t0:t0 + 512]), 16)
        if extra_ap is not None:
            SY.mark(nsy.dma_start(out=h.scrB[:], in_=extra_ap[:, oc, t0:t0 + 512]), 16)
        V.wait(T_); V.wait(SY); V.wait(S)
        yb = h.ychunk[oc % 2]
        V.fence()
        V.mark(nv.tensor_tensor(yb[:], y_from, xb[:], ALU.add))
        if extra_ap is not None:
            V.fence()
            V.mark(nv.tensor_tensor(yb[:], yb[:], h.scrB[:], ALU.add))
        T_.wait(V)
        S.wait(V)
        S.mark(S.e.copy(h.y_bf[:], yb[:]))
        S.mark(S.e.activation(h.sq_bf[:], yb[:], AF.Square))
        SY.wait(V)
        SY.mark(nsy.dma_start(out=ybuf_ap[:, oc, t0:t0 + 512], in_=yb[:]), 16)
        T_.wait(S)
        i1 = nc.tensor.matmul(h.pSum[0:1, :], h.onec[:, 0:1], h.y_bf[:],
                              start=(oc == 0), stop=(oc == n_oc - 1))
        i2 = nc.tensor.matmul(h.pSq[0:1, :], h.onec[:, 0:1], h.sq_bf[:],
                              start=(oc == 0), stop=(oc == n_oc - 1))
        T_.mark(i1); T_.mark(i2)
        S.wait(T_)
    h.resid_stats_chunk = resid_stats_chunk

    def ln_apply_pass(ybuf_ap, xo_ap, n_rows=4):
        nsy = nc.sync
        for trow in range(n_rows):
            t0 = trow * 512
            h.bcast_rows(trow)
            for oc in range(DC):
                yb = h.ychunk[oc % 2]
                ob = h.xchunk[oc % 2]
                SY.wait(V)
                SY.mark(nsy.dma_start(out=yb[:], in_=ybuf_ap[:, oc, t0:t0 + 512]), 16)
                V.wait(SY); V.wait(T_)
                V.fence()
                V.mark(nv.tensor_tensor(ob[:], yb[:], h.pBa[:], ALU.mult))
                V.fence()
                V.mark(nv.tensor_tensor(ob[:], ob[:], h.pBb[:], ALU.add))
                SY.wait(V)
                SY.mark(nsy.dma_start(out=xo_ap[:, oc, t0:t0 + 512], in_=ob[:]), 16)
            T_.wait(V)
    h.ln_apply_pass = ln_apply_pass
    return h

def std_host_consts():
    conesr = np.zeros((2, 512), np.float32); conesr[0] = 1.0
    cones = np.zeros((128, 2), np.float32); cones[:, 0] = 1.0; cones[0, 1] = 1e-6
    cflt = np.zeros((128, 4), np.float32); cflt[:, 0] = 1e-5
    return dict(conesr=conesr, cones=cones, cflt=cflt)


def build_l1_ffn(nc):
    xs = nc.dram_tensor("xs", [128, DC, TOK], F32, kind="ExternalInput").ap()
    wfc = nc.dram_tensor("wfc", [DC, 128, 4096], F32, kind="ExternalInput").ap()
    wpr = nc.dram_tensor("wpr", [32, 128, 1024], F32, kind="ExternalInput").ap()
    conesr = nc.dram_tensor("conesr", [2, 512], F32, kind="ExternalInput").ap()
    cones = nc.dram_tensor("cones", [128, 2], F32, kind="ExternalInput").ap()
    cflt = nc.dram_tensor("cflt", [128, 4], F32, kind="ExternalInput").ap()
    xo = nc.dram_tensor("xo", [128, DC, TOK], F32, kind="ExternalOutput").ap()
    ybuf = nc.dram_tensor("ybuf", [128, DC, TOK], F32)

    with contextlib.ExitStack() as es:
        ctx = Ctx(nc, es)
        T_, V, S, SY, G, CC = ctx.T, ctx.V, ctx.S, ctx.SY, ctx.G, ctx.CC
        sb = ctx.sb
        nv = nc.vector; nt = nc.tensor; ng = nc.gpsimd
        h = std_setup(ctx, nc, es)
        x_bf = sb("x_bf", (128, DC, TOK), BF16)
        WFC = sb("WFC", (128, DC, 4096), BF16)
        hbf = sb("hbf", (128, 32, 512), BF16)
        scrY = sb("scrY", (128, 512), F32)
        mem = sb("mem", (128, 512), F32)
        ymin = sb("ymin", (128, 512), F32)
        spk = sb("spk", (128, 512), F32)
        acc = sb("acc", (128, 512), F32)

        blk = es.enter_context(nc.Block())
        G.mark(ng.dma_start(out=x_bf[:], in_=xs), 16)
        G.mark(ng.dma_start(out=WFC[:], in_=wfc.rearrange("c p f -> p c f")), 16)
        h.load_std_consts(conesr, cones, cflt)
        T_.wait(G); V.wait(G); S.wait(G)
        T_.wait(h.SD); V.wait(h.SD); S.wait(h.SD)
        G.fence()

        for b in range(4):
            t0 = b * 512
            for fc in range(32):
                dst_ps = h.pPA if fc % 2 == 0 else h.pPB
                for dc in range(DC):
                    inst = nt.matmul(dst_ps[:], WFC[:, dc, fc * 128:(fc + 1) * 128],
                                     x_bf[:, dc, t0:t0 + 512],
                                     start=(dc == 0), stop=(dc == DC - 1))
                T_.mark(inst)
                V.wait(T_); V.wait(S)
                V.fence()
                V.mark(nv.tensor_single_scalar(h.scrA[:], dst_ps[:], 0.0, ALU.max))
                S.wait(V)
                S.mark(S.e.activation(hbf[:, fc, :], h.scrA[:], AF.Square))
                T_.wait(S)
            if b == 0:
                h.wsl_load(0, wpr[:, :, 0:128], 4096)
            for oc in range(DC):
                li = b * 8 + oc
                slot = li % 2
                if li + 1 < 32:
                    noc = (li + 1) % 8
                    h.wsl_load((li + 1) % 2, wpr[:, :, noc * 128:(noc + 1) * 128], 4096)
                h.wsl_wait(slot)
                dst_ps = h.pPA if oc % 2 == 0 else h.pPB
                for fc in range(32):
                    inst = nt.matmul(dst_ps[:], h.wsl[:, slot, fc * 128:(fc + 1) * 128],
                                     hbf[:, fc, :], start=(fc == 0), stop=(fc == 31))
                T_.mark(inst)
                h.wsl_done(slot)
                V.wait(T_); V.wait(S)
                V.fence()
                V.mark(nv.tensor_copy(scrY[:], dst_ps[:]))
                V.fence()
                V.mark(nv.tensor_copy(mem[:], scrY[:]))
                V.fence()
                V.mark(nv.tensor_single_scalar(acc[:], mem[:], 1.0, ALU.is_gt))
                for step in range(4):
                    V.fence()
                    V.mark(nv.tensor_single_scalar(spk[:], mem[:], 1.0, ALU.is_gt))
                    V.fence()
                    V.mark(nv.tensor_tensor(ymin[:], scrY[:], spk[:], ALU.subtract))
                    V.fence()
                    V.mark(nv.scalar_tensor_tensor(mem[:], mem[:], 0.9, ymin[:], ALU.mult, ALU.add))
                    V.fence()
                    V.mark(nv.tensor_single_scalar(spk[:], mem[:], 1.0, ALU.is_gt))
                    V.fence()
                    V.mark(nv.tensor_tensor(acc[:], acc[:], spk[:], ALU.add))
                V.fence()
                V.mark(nv.tensor_scalar(acc[:], acc[:], 0.2, None, ALU.mult))
                h.resid_stats_chunk(oc, DC, acc[:], xs, t0, ybuf.ap())
            h.stats_finish(b)
        h.ln_apply_pass(ybuf.ap(), xo)
        ng.wait_ge(SY.sem, SY.n)
        ng.wait_ge(G.sem, G.n)
        h.wsl_finish(ng)
    return nc

def host_l1_ffn(ins):
    wfc = np.ascontiguousarray(ins["L1_ffn_W_fc"].astype(np.float32).T.reshape(DC, 128, 4096))
    wpr = np.ascontiguousarray(ins["L1_ffn_W_proj"].astype(np.float32).T.reshape(32, 128, 1024))
    return dict(wfc=wfc, wpr=wpr, **std_host_consts())



def build_l2_ffn(nc):
    xs = nc.dram_tensor("xs", [128, DC, TOK], F32, kind="ExternalInput").ap()
    atm = nc.dram_tensor("atm", [2, 4, 1024], F32, kind="ExternalInput").ap()
    btm = nc.dram_tensor("btm", [2, 4, 256], F32, kind="ExternalInput").ap()
    bias = nc.dram_tensor("bias", [128, 8], F32, kind="ExternalInput").ap()
    conesr = nc.dram_tensor("conesr", [2, 512], F32, kind="ExternalInput").ap()
    cones = nc.dram_tensor("cones", [128, 2], F32, kind="ExternalInput").ap()
    cflt = nc.dram_tensor("cflt", [128, 4], F32, kind="ExternalInput").ap()
    xo = nc.dram_tensor("xo", [128, DC, TOK], F32, kind="ExternalOutput").ap()
    ybuf = nc.dram_tensor("ybuf", [128, DC, TOK], F32)
    wd1 = nc.dram_tensor("wd1", [2, 1024, 256], BF16)
    wd2 = nc.dram_tensor("wd2", [2, 512, 512], BF16)

    with contextlib.ExitStack() as es:
        ctx = Ctx(nc, es)
        T_, V, S, SY, G, CC = ctx.T, ctx.V, ctx.S, ctx.SY, ctx.G, ctx.CC
        sb = ctx.sb
        nv = nc.vector; nt = nc.tensor; ng = nc.gpsimd; nsy = nc.sync
        h = std_setup(ctx, nc, es)
        x_bf = sb("x_bf", (128, DC, TOK), BF16)
        WT = sb("WT", (128, 2, 4, 512), BF16)
        afac = sb("afac", (4, 2, 1024), BF16)
        bfac = sb("bfac", (4, 2, 256), BF16)
        bia = sb("bia", (128, 8), F32)
        wbb = sb("wbb", (128, 256), BF16)

        blk = es.enter_context(nc.Block())
        G.mark(ng.dma_start(out=x_bf[:], in_=xs), 16)
        for m in range(2):
            G.mark(ng.dma_start(out=afac[:, m, :], in_=atm[m]), 16)
            G.mark(ng.dma_start(out=bfac[:, m, :], in_=btm[m]), 16)
        h.load_std_consts(conesr, cones, cflt)
        h.SD.mark(nc.scalar.dma_start(out=bia[:], in_=bias), 16)
        T_.wait(G); V.wait(G); S.wait(G)
        T_.wait(h.SD); V.wait(h.SD); S.wait(h.SD)
        G.fence()
        for m in range(2):
            for ch in range(8):
                T_.wait(S)
                T_.mark(nt.matmul(h.pX[:, 0:256], afac[:, m, ch * 128:(ch + 1) * 128],
                                  bfac[:, m, :], start=True, stop=True))
                S.wait(T_); S.wait(SY)
                S.mark(S.e.copy(wbb[:], h.pX[:, 0:256]))
                SY.wait(S)
                SY.mark(nsy.dma_start(out=wd1.ap()[m, ch * 128:(ch + 1) * 128, :], in_=wbb[:]), 16)
        SY.fence()
        G.e.wait_ge(SY.sem, SY.n)
        for m in range(2):
            srcv = wd1.ap()[m].rearrange("(b d) (e g) -> b e d g", b=32, e=16)
            dstv = wd2.ap()[m].rearrange("(b e) (d g) -> b e d g", b=32, d=32)
            for bb in range(32):
                G.mark(ng.dma_start(out=dstv[bb], in_=srcv[bb]), 16)
        G.fence()
        SY.wait(G)
        for m in range(2):
            for ic in range(4):
                SY.mark(nsy.dma_start(out=WT[:, m, ic, :],
                                      in_=wd2.ap()[m, :, ic * 128:(ic + 1) * 128],
                                      transpose=True), 16)
        T_.e.wait_ge(SY.sem, SY.n)
        for b in range(4):
            t0 = b * 512
            for oc in range(DC):
                dst_ps = h.pPA if oc % 2 == 0 else h.pPB
                if oc < 4:
                    for ic in range(4):
                        i1 = nt.matmul(dst_ps[:], WT[:, 0, ic, oc * 128:(oc + 1) * 128],
                                       x_bf[:, ic, t0:t0 + 512], start=(ic == 0), stop=(ic == 3))
                    T_.mark(i1)
                    T_.wait(V)
                    for ic in range(4):
                        i2 = nt.matmul(h.pX[:], WT[:, 1, ic, oc * 128:(oc + 1) * 128],
                                       x_bf[:, 4 + ic, t0:t0 + 512], start=(ic == 0), stop=(ic == 3))
                    T_.mark(i2)
                    V.wait(T_)
                    V.fence()
                    V.mark(nv.tensor_scalar(h.scrA[:], dst_ps[:], bia[:, oc:oc + 1], None, ALU.add))
                    V.fence()
                    V.mark(nv.tensor_tensor(h.scrA[:], h.scrA[:], h.pX[:], ALU.subtract))
                else:
                    for ic in range(4):
                        nt.matmul(dst_ps[:], WT[:, 1, ic, (oc - 4) * 128:(oc - 3) * 128],
                                  x_bf[:, ic, t0:t0 + 512], start=(ic == 0), stop=False)
                    for ic in range(4):
                        i1 = nt.matmul(dst_ps[:], WT[:, 0, ic, (oc - 4) * 128:(oc - 3) * 128],
                                       x_bf[:, 4 + ic, t0:t0 + 512], start=False, stop=(ic == 3))
                    T_.mark(i1)
                    V.wait(T_)
                    V.fence()
                    V.mark(nv.tensor_scalar(h.scrA[:], dst_ps[:], bia[:, oc:oc + 1], None, ALU.add))
                h.resid_stats_chunk(oc, DC, h.scrA[:], xs, t0, ybuf.ap())
            h.stats_finish(b)
        h.ln_apply_pass(ybuf.ap(), xo)
        ng.wait_ge(SY.sem, SY.n)
        ng.wait_ge(G.sem, G.n)
    return nc

def host_l2_ffn(ins):
    G1R = ins["L2_ffn_G1_R"].astype(np.float32)
    G2R = ins["L2_ffn_G2_R"].astype(np.float32)
    G1I = ins["L2_ffn_G1_I"].astype(np.float32)
    G2I = ins["L2_ffn_G2_I"].astype(np.float32)
    atm = np.stack([G1R[0].transpose(1, 0, 2).reshape(4, 1024),
                    G1I[0].transpose(1, 0, 2).reshape(4, 1024)], 0)
    btm = np.stack([G2R[:, :, 0, :].reshape(4, 256),
                    G2I[:, :, 0, :].reshape(4, 256)], 0)
    bias = np.concatenate([ins["L2_ffn_bias_real"], ins["L2_ffn_bias_imag"]]).astype(np.float32)
    bias = np.ascontiguousarray(bias.reshape(8, 128).T)
    return dict(atm=np.ascontiguousarray(atm), btm=np.ascontiguousarray(btm), bias=bias,
                **std_host_consts())



def build_l3_ffn(nc):
    xs = nc.dram_tensor("xs", [128, DC, TOK], F32, kind="ExternalInput").ap()
    wgr = nc.dram_tensor("wgr", [16, DC, 128, 256], F32, kind="ExternalInput").ap()
    wur = nc.dram_tensor("wur", [16, DC, 128, 256], F32, kind="ExternalInput").ap()
    wdr = nc.dram_tensor("wdr", [16, 2, 128, 1024], F32, kind="ExternalInput").ap()
    wgs = nc.dram_tensor("wgs", [2, DC, 128, 512], F32, kind="ExternalInput").ap()
    wus = nc.dram_tensor("wus", [2, DC, 128, 512], F32, kind="ExternalInput").ap()
    wds = nc.dram_tensor("wds", [2, 4, 128, 1024], F32, kind="ExternalInput").ap()
    wrt_in = nc.dram_tensor("wrt_in", [128, DC * 16], F32, kind="ExternalInput").ap()
    csel = nc.dram_tensor("csel", [16, 16 * 128], F32, kind="ExternalInput").ap()
    cidt = nc.dram_tensor("cidt", [128, 128], F32, kind="ExternalInput").ap()
    conesr = nc.dram_tensor("conesr", [2, 512], F32, kind="ExternalInput").ap()
    cones = nc.dram_tensor("cones", [128, 2], F32, kind="ExternalInput").ap()
    cflt = nc.dram_tensor("cflt", [128, 4], F32, kind="ExternalInput").ap()
    xo = nc.dram_tensor("xo", [128, DC, TOK], F32, kind="ExternalOutput").ap()
    ybuf = nc.dram_tensor("ybuf", [128, DC, TOK], F32)

    with contextlib.ExitStack() as es:
        ctx = Ctx(nc, es)
        T_, V, S, SY, G, CC = ctx.T, ctx.V, ctx.S, ctx.SY, ctx.G, ctx.CC
        sb = ctx.sb
        nv = nc.vector; nt = nc.tensor; ng = nc.gpsimd; nsy = nc.sync
        h = std_setup(ctx, nc, es)
        x_bf = sb("x_bf", (128, DC, TOK), BF16)
        moe = sb("moe", (128, DC, 1024), F32)
        hb = sb("hb", (128, 4, 1024), BF16)
        sil = sb("sil", (128, 512), BF16)
        wbc = sb("wbc", (128, 1024), BF16)
        wrows = sb("wrows", (16, 2048), BF16)
        wrt = sb("wrt", (128, DC, 16), BF16)
        sel = sb("sel", (16, 16, 128), BF16)
        idt = sb("idt", (128, 128), BF16)
        tm = sb("tm", (128, 16), F32)
        tmb = sb("tmb", (128, 16), F32)
        tmm = sb("tmm", (128, 1), F32)
        tmc = sb("tmc", (128, 16), BF16)
        P7b = ctx.ps("P7b", (128, 1024), BF16)
        wsl2 = sb("wsl2", (128, 2, 2048), BF16)
        w2sem = [es.enter_context(nc.semaphore("w2sl0")), es.enter_context(nc.semaphore("w2sl1"))]
        w2cnt = [0, 0]
        w2cons = [0, 0]
        def wsl2_load(slot, src_ap, n):
            G.e.wait_ge(T_.sem, w2cons[slot])
            c = src_ap.shape[0]
            dst = wsl2[:, slot, 0:n].rearrange("p (c f) -> p c f", c=c)
            ng.dma_start(out=dst, in_=src_ap.rearrange("c p f -> p c f")).then_inc(w2sem[slot], 16)
            w2cnt[slot] += 16
        def wsl2_wait(slot):
            T_.e.wait_ge(w2sem[slot], w2cnt[slot])
        def wsl2_done(slot):
            w2cons[slot] = T_.n

        blk = es.enter_context(nc.Block())
        G.mark(ng.dma_start(out=x_bf[:], in_=xs), 16)
        G.mark(ng.dma_start(out=wrt[:].rearrange("p c f -> p (c f)"), in_=wrt_in), 16)
        G.mark(ng.dma_start(out=sel[:].rearrange("p c f -> p (c f)"), in_=csel), 16)
        G.mark(ng.dma_start(out=idt[:], in_=cidt), 16)
        h.load_std_consts(conesr, cones, cflt)
        T_.wait(G); V.wait(G); S.wait(G)
        T_.wait(h.SD); V.wait(h.SD); S.wait(h.SD)
        G.fence()

        for tt in range(4):
            t0 = tt * 512
            for dc in range(DC):
                inst = nt.matmul(h.pX[0:16, :], wrt[:, dc, :],
                                 x_bf[:, dc, t0:t0 + 512], start=(dc == 0), stop=(dc == DC - 1))
            T_.mark(inst)
            S.wait(T_)
            S.mark(S.e.copy(h.sq_bf[0:16, :], h.pX[0:16, :]))
            T_.wait(S)
            for q4 in range(4):
                T_.wait(V)
                T_.mark(nt.transpose(P7b[:, 0:16], h.sq_bf[0:16, q4 * 128:(q4 + 1) * 128], idt[0:16, 0:16]))
                V.wait(T_)
                V.mark(nv.tensor_copy(tm[:], P7b[:, 0:16]))
                V.fence()
                V.mark(nv.reduce_max(tmm[:], tm[:], axis=AX.X, negate=True))
                V.fence()
                V.mark(nv.tensor_scalar(tm[:], tm[:], tmm[:, 0:1], None, ALU.add))
                S.wait(V)
                S.mark(S.e.activation(tm[:], tm[:], AF.Exp))
                V.wait(S)
                V.mark(nv.reduce_sum(tmm[:], tm[:], axis=AX.X))
                V.fence()
                V.mark(nv.reciprocal(tmm[:], tmm[:]))
                V.fence()
                V.mark(nv.tensor_scalar(tm[:], tm[:], tmm[:, 0:1], None, ALU.mult))
                V.fence()
                V.mark(nv.reduce_max(tmm[:], tm[:], axis=AX.X))
                V.fence()
                V.mark(nv.tensor_scalar(tmb[:], tm[:], tmm[:, 0:1], None, ALU.is_ge))
                V.fence()
                V.mark(nv.scalar_tensor_tensor(tmb[:], tmb[:], 2.0, tm[:], ALU.mult, ALU.subtract))
                V.fence()
                V.mark(nv.tensor_scalar(tmb[:], tmb[:], -1.0, None, ALU.mult))
                V.fence()
                V.mark(nv.reduce_max(tmm[:], tmb[:], axis=AX.X))
                V.fence()
                V.mark(nv.tensor_scalar(tmb[:], tm[:], tmm[:, 0:1], None, ALU.is_ge))
                V.fence()
                V.mark(nv.tensor_tensor(tm[:], tm[:], tmb[:], ALU.mult))
                V.fence()
                V.mark(nv.reduce_sum(tmm[:], tm[:], axis=AX.X))
                V.fence()
                V.mark(nv.tensor_scalar(tmm[:], tmm[:], 1e-9, None, ALU.add))
                V.fence()
                V.mark(nv.reciprocal(tmm[:], tmm[:]))
                V.fence()
                V.mark(nv.tensor_scalar(tmc[:], tm[:], tmm[:, 0:1], None, ALU.mult))
                T_.wait(V)
                T_.mark(nt.transpose(P7b[0:16, 0:128], tmc[:], idt[:]))
                V.wait(T_)
                V.mark(nv.tensor_copy(wrows[:, t0 + q4 * 128: t0 + (q4 + 1) * 128], P7b[0:16, 0:128]))
        for slab in range(2):
            s0 = slab * 1024
            V.fence()
            V.mark(nv.memset(moe[:], 0.0))
            for n in range(2):
                for fj in range(2):
                    sl = (n * 2 + fj) % 2
                    h.wsl_load(sl, wgs[n, :, :, fj * 256:(fj + 1) * 256], 2048)
                    wsl2_load(sl, wus[n, :, :, fj * 256:(fj + 1) * 256], 2048)
                    h.wsl_wait(sl); wsl2_wait(sl)
                    for fc2 in range(2):
                        fc = fj * 2 + fc2
                        for sub in range(2):
                            t0 = s0 + sub * 512
                            pg = h.pPA if (fc + sub) % 2 == 0 else h.pPB
                            pu = h.P[2] if (fc + sub) % 2 == 0 else h.P[3]
                            for dc in range(DC):
                                ig = nt.matmul(pg[:], h.wsl[:, sl, dc * 256 + fc2 * 128: dc * 256 + (fc2 + 1) * 128],
                                               x_bf[:, dc, t0:t0 + 512], start=(dc == 0), stop=(dc == DC - 1))
                            T_.mark(ig)
                            for dc in range(DC):
                                iu = nt.matmul(pu[:], wsl2[:, sl, dc * 256 + fc2 * 128: dc * 256 + (fc2 + 1) * 128],
                                               x_bf[:, dc, t0:t0 + 512], start=(dc == 0), stop=(dc == DC - 1))
                            T_.mark(iu)
                            S.wait(T_); S.wait(V)
                            S.mark(S.e.activation(sil[:], pg[:], AF.Silu))
                            V.wait(S)
                            V.fence()
                            V.mark(nv.tensor_tensor(hb[:, fc, sub * 512:sub * 512 + 512], sil[:], pu[:], ALU.mult))
                            T_.wait(V)
                    h.wsl_done(sl); wsl2_done(sl)
                for oh in range(2):
                    sl = oh % 2
                    h.wsl_load(sl, wds[n, :, :, oh * 512:(oh + 1) * 512], 2048)
                    h.wsl_wait(sl)
                    for oc2 in range(4):
                        oc = oh * 4 + oc2
                        for sub in range(2):
                            dst_ps = h.pPA if (oc + sub) % 2 == 0 else h.pPB
                            for fc in range(4):
                                inst = nt.matmul(dst_ps[:], h.wsl[:, sl, fc * 512 + oc2 * 128: fc * 512 + (oc2 + 1) * 128],
                                                 hb[:, fc, sub * 512:sub * 512 + 512],
                                                 start=(fc == 0), stop=(fc == 3))
                            T_.mark(inst)
                            V.wait(T_)
                            V.fence()
                            msl = moe[:, oc, sub * 512:sub * 512 + 512]
                            V.mark(nv.tensor_tensor(msl, msl, dst_ps[:], ALU.add))
                            T_.wait(V)
                    h.wsl_done(sl)
            for e in range(16):
                h.wsl_load(e % 2, wgr[e], 2048)
                wsl2_load(e % 2, wur[e], 2048)
                h.wsl_wait(e % 2); wsl2_wait(e % 2)
                for fc in range(2):
                    for sub in range(2):
                        t0 = s0 + sub * 512
                        pg = h.pPA if (fc + sub) % 2 == 0 else h.pPB
                        pu = h.P[2] if (fc + sub) % 2 == 0 else h.P[3]
                        for dc in range(DC):
                            ig = nt.matmul(pg[:], h.wsl[:, e % 2, dc * 256 + fc * 128: dc * 256 + (fc + 1) * 128],
                                           x_bf[:, dc, t0:t0 + 512], start=(dc == 0), stop=(dc == DC - 1))
                        T_.mark(ig)
                        for dc in range(DC):
                            iu = nt.matmul(pu[:], wsl2[:, e % 2, dc * 256 + fc * 128: dc * 256 + (fc + 1) * 128],
                                           x_bf[:, dc, t0:t0 + 512], start=(dc == 0), stop=(dc == DC - 1))
                        T_.mark(iu)
                        S.wait(T_); S.wait(V)
                        S.mark(S.e.activation(sil[:], pg[:], AF.Silu))
                        V.wait(S)
                        V.fence()
                        V.mark(nv.tensor_tensor(hb[:, fc, sub * 512:sub * 512 + 512], sil[:], pu[:], ALU.mult))
                        T_.wait(V)
                h.wsl_done(e % 2); wsl2_done(e % 2)
                for sub in range(2):
                    T_.wait(V); T_.wait(S)
                    T_.mark(nt.matmul(h.pBa[:], sel[:, e, :], wrows[:, s0 + sub * 512: s0 + sub * 512 + 512],
                                      start=True, stop=True))
                    S.wait(T_)
                    S.mark(S.e.copy(wbc[:, sub * 512:sub * 512 + 512], h.pBa[:]))
                h.wsl_load(e % 2, wdr[e], 2048)
                h.wsl_wait(e % 2)
                for oc in range(DC):
                    for sub in range(2):
                        dst_ps = h.pPA if (oc + sub) % 2 == 0 else h.pPB
                        for fc in range(2):
                            inst = nt.matmul(dst_ps[:], h.wsl[:, e % 2, fc * 1024 + oc * 128: fc * 1024 + (oc + 1) * 128],
                                             hb[:, fc, sub * 512:sub * 512 + 512],
                                             start=(fc == 0), stop=(fc == 1))
                        T_.mark(inst)
                        V.wait(T_); V.wait(S)
                        V.fence()
                        V.mark(nv.tensor_tensor(h.scrA[:], dst_ps[:], wbc[:, sub * 512:sub * 512 + 512], ALU.mult))
                        V.fence()
                        msl = moe[:, oc, sub * 512:sub * 512 + 512]
                        V.mark(nv.tensor_tensor(msl, msl, h.scrA[:], ALU.add))
                        T_.wait(V)
                h.wsl_done(e % 2)
            for sub in range(2):
                t0 = s0 + sub * 512
                b4 = slab * 2 + sub
                for oc in range(DC):
                    h.resid_stats_chunk(oc, DC, moe[:, oc, sub * 512:sub * 512 + 512],
                                        xs, t0, ybuf.ap())
                h.stats_finish(b4)
        h.ln_apply_pass(ybuf.ap(), xo)
        ng.wait_ge(SY.sem, SY.n)
        ng.wait_ge(G.sem, G.n)
        h.wsl_finish(ng)
        ng.wait_ge(w2sem[0], w2cnt[0])
        ng.wait_ge(w2sem[1], w2cnt[1])
    return nc

def host_l3_ffn(ins):
    wgr = np.ascontiguousarray(ins["L3_ffn_wg_r"].astype(np.float32).reshape(16, DC, 128, 256))
    wur = np.ascontiguousarray(ins["L3_ffn_wu_r"].astype(np.float32).reshape(16, DC, 128, 256))
    wdr = np.ascontiguousarray(ins["L3_ffn_wd_r"].astype(np.float32).reshape(16, 2, 128, 1024))
    wgs = np.ascontiguousarray(ins["L3_ffn_wg_s"].astype(np.float32).reshape(2, DC, 128, 512))
    wus = np.ascontiguousarray(ins["L3_ffn_wu_s"].astype(np.float32).reshape(2, DC, 128, 512))
    wds = np.ascontiguousarray(ins["L3_ffn_wd_s"].astype(np.float32).reshape(2, 4, 128, 1024))
    wrt_in = np.ascontiguousarray(ins["L3_ffn_router_w"].astype(np.float32)
                                  .reshape(DC, 128, 16).transpose(1, 0, 2).reshape(128, DC * 16))
    csel = np.zeros((16, 16, 128), np.float32)
    for e in range(16):
        csel[e, e, :] = 1.0
    idt = np.eye(128, dtype=np.float32)
    return dict(wgr=wgr, wur=wur, wdr=wdr, wgs=wgs, wus=wus, wds=wds, wrt_in=wrt_in,
                csel=csel.reshape(16, 16 * 128), cidt=idt, **std_host_consts())



def build_proj_resid_ln(nc):
    xs = nc.dram_tensor("xs", [128, DC, TOK], F32, kind="ExternalInput").ap()
    osd = nc.dram_tensor("osd", [128, DC, TOK], F32, kind="ExternalInput").ap()
    wp = nc.dram_tensor("wp", [DC, 128, 1024], F32, kind="ExternalInput").ap()
    wbias = nc.dram_tensor("wbias", [128, DC], F32, kind="ExternalInput").ap()
    conesr = nc.dram_tensor("conesr", [2, 512], F32, kind="ExternalInput").ap()
    cones = nc.dram_tensor("cones", [128, 2], F32, kind="ExternalInput").ap()
    cflt = nc.dram_tensor("cflt", [128, 4], F32, kind="ExternalInput").ap()
    xo = nc.dram_tensor("xo", [128, DC, TOK], F32, kind="ExternalOutput").ap()
    ybuf = nc.dram_tensor("ybuf", [128, DC, TOK], F32)

    with contextlib.ExitStack() as es:
        ctx = Ctx(nc, es)
        T_, V, S, SY, G, CC = ctx.T, ctx.V, ctx.S, ctx.SY, ctx.G, ctx.CC
        sb = ctx.sb
        nv = nc.vector; nt = nc.tensor; ng = nc.gpsimd
        h = std_setup(ctx, nc, es)
        o_bf = sb("o_bf", (128, DC, TOK), BF16)
        wbs = sb("wbs", (128, DC), F32)
        blk = es.enter_context(nc.Block())
        G.mark(ng.dma_start(out=o_bf[:], in_=osd), 16)
        h.load_std_consts(conesr, cones, cflt)
        h.SD.mark(nc.scalar.dma_start(out=wbs[:], in_=wbias), 16)
        T_.wait(G); V.wait(G); S.wait(G)
        T_.wait(h.SD); V.wait(h.SD); S.wait(h.SD)
        G.fence()
        h.wsl_load(0, wp[:, :, 0:128], 1024)
        for b in range(4):
            t0 = b * 512
            for oc in range(DC):
                li = b * 8 + oc
                slot = li % 2
                if li + 1 < 32:
                    noc = (li + 1) % 8
                    h.wsl_load((li + 1) % 2, wp[:, :, noc * 128:(noc + 1) * 128], 1024)
                h.wsl_wait(slot)
                dst_ps = h.pPA if oc % 2 == 0 else h.pPB
                for fc in range(DC):
                    inst = nt.matmul(dst_ps[:], h.wsl[:, slot, fc * 128:(fc + 1) * 128],
                                     o_bf[:, fc, t0:t0 + 512],
                                     start=(fc == 0), stop=(fc == DC - 1))
                T_.mark(inst)
                h.wsl_done(slot)
                V.wait(T_); V.wait(S)
                V.fence()
                V.mark(nv.tensor_scalar(h.scrA[:], dst_ps[:], wbs[:, oc:oc + 1], None, ALU.add))
                h.resid_stats_chunk(oc, DC, h.scrA[:], xs, t0, ybuf.ap())
            h.stats_finish(b)
        h.ln_apply_pass(ybuf.ap(), xo)
        ng.wait_ge(SY.sem, SY.n)
        ng.wait_ge(G.sem, G.n)
        h.wsl_finish(ng)
    return nc


def build_l2_attn_heads(nc):
    xb_in = nc.dram_tensor("xb_in", [128, DC, T], F32, kind="ExternalInput").ap()
    wqkv = nc.dram_tensor("wqkv", [DC, 128, 1536], F32, kind="ExternalInput").ap()
    ctab = nc.dram_tensor("ctab", [128, 4, T], F32, kind="ExternalInput").ap()
    cpsw = nc.dram_tensor("cpsw", [128, 128], F32, kind="ExternalInput").ap()
    ctri = nc.dram_tensor("ctri", [128, 128], F32, kind="ExternalInput").ap()
    conesr = nc.dram_tensor("conesr", [2, 512], F32, kind="ExternalInput").ap()
    cones = nc.dram_tensor("cones", [128, 2], F32, kind="ExternalInput").ap()
    cflt = nc.dram_tensor("cflt", [128, 4], F32, kind="ExternalInput").ap()
    oh = nc.dram_tensor("oh", [8, 64, T], BF16, kind="ExternalOutput")
    vstash = nc.dram_tensor("vstash", [32, 128, 512], BF16)

    with contextlib.ExitStack() as es:
        ctx = Ctx(nc, es)
        T_, V, S, SY, G, CC = ctx.T, ctx.V, ctx.S, ctx.SY, ctx.G, ctx.CC
        sb, ps = ctx.sb, ctx.ps
        nv = nc.vector; nt = nc.tensor; ng = nc.gpsimd; nsy = nc.sync
        h = std_setup(ctx, nc, es, lite=True)
        qa = sb("qa", (65, 8, T), BF16)
        ka = sb("ka", (65, 8, T), BF16)
        vhead = sb("vhead", (128, 32, 64), BF16)
        wqkv_sb = sb("wqkv_sb", (128, DC, 1536), BF16)
        xtile = sb("xtile", (128, DC, 512), BF16)
        tabs = sb("tabs", (128, 4, 512), BF16)
        psw = sb("psw", (128, 128), BF16)
        tri = sb("tri", (128, 128), BF16)
        nrm = sb("nrm", (1, 2, 512), F32)
        bcell = sb("bcell", (1, 16), F32)
        dden = sb("dden", (1, 512), F32)
        ost = sb("ost", (128, 512), BF16)
        obuf = sb("obuf", (64, 1, 512), BF16)

        blk = es.enter_context(nc.Block())
        G.mark(ng.dma_start(out=psw[:], in_=cpsw), 16)
        G.mark(ng.dma_start(out=wqkv_sb[:], in_=wqkv.rearrange("c p f -> p c f")), 16)
        G.mark(ng.dma_start(out=tri[:], in_=ctri), 16)
        h.load_std_consts(conesr, cones, cflt)
        T_.wait(G); V.wait(G); S.wait(G)
        T_.wait(h.SD); V.wait(h.SD); S.wait(h.SD)
        G.fence()
        for tt in range(8):
            t0 = tt * 512
            G.e.wait_ge(T_.sem, ctx.T.n)
            G.e.wait_ge(V.sem, ctx.V.n)
            G.mark(ng.dma_start(out=xtile[:], in_=xb_in[:, :, t0:t0 + 512]), 16)
            for cc4 in range(4):
                G.mark(ng.dma_start(out=tabs[:, cc4, :], in_=ctab[:, cc4, t0:t0 + 512]), 16)
            T_.wait(G)
            V.wait(G)
            for part, cbase, dst in ((0, 0, qa), (1, 512, ka)):
                for hh in range(8):
                    dst_ps = h.pPA if hh % 2 == 0 else h.pPB
                    for dc in range(DC):
                        inst = nt.matmul(dst_ps[0:64, :],
                                         wqkv_sb[:, dc, cbase + hh * 64:cbase + (hh + 1) * 64],
                                         xtile[:, dc, :], start=(dc == 0), stop=(dc == DC - 1))
                    T_.mark(inst)
                    S.wait(T_); S.wait(V)
                    S.mark(S.e.copy(h.scrA[0:64, :], dst_ps[0:64, :]))
                    S.mark(S.e.copy(h.y_bf[0:64, :], dst_ps[0:64, :]))
                    T_.wait(S)
                    T_.mark(nt.matmul(h.pX[0:64, :], psw[0:64, 0:64], h.y_bf[0:64, :],
                                      start=True, stop=True))
                    ci = 2 if part == 0 else 0
                    V.wait(T_)
                    V.fence()
                    V.mark(nv.tensor_tensor(h.scrB[0:64, :], h.scrA[0:64, :], tabs[0:64, ci, :], ALU.mult))
                    V.fence()
                    V.mark(nv.tensor_tensor(h.scrA[0:64, :], h.pX[0:64, :], tabs[0:64, ci + 1, :], ALU.mult))
                    V.fence()
                    V.mark(nv.tensor_tensor(dst[0:64, hh, t0:t0 + 512], h.scrA[0:64, :],
                                            h.scrB[0:64, :], ALU.add))
                    T_.wait(V)
            for st4 in range(4):
                ts0 = t0 + st4 * 128
                dst_ps = h.pPA if st4 % 2 == 0 else h.pPB
                for dc in range(DC):
                    inst = nt.matmul(dst_ps[:], xtile[:, dc, st4 * 128:(st4 + 1) * 128],
                                     wqkv_sb[:, dc, 1024:1536], start=(dc == 0), stop=(dc == DC - 1))
                T_.mark(inst)
                S.wait(T_); S.wait(SY)
                S.mark(S.e.copy(h.sq_bf[:], dst_ps[:]))
                SY.wait(S)
                SY.mark(nsy.dma_start(out=vstash.ap()[tt * 4 + st4], in_=h.sq_bf[:]), 16)
                T_.wait(S)
        V.fence()
        V.mark(nv.memset(ka[64:65, :, :].rearrange("p c t -> p (c t)"), 1.0))
        for hh in range(8):
            for part, buf in ((0, qa), (1, ka)):
                first = True
                for tt in range(8):
                    t0 = tt * 512
                    S.wait(T_); S.wait(V); S.wait(SY)
                    S.mark(S.e.activation(h.sq_bf[0:64, :], buf[0:64, hh, t0:t0 + 512], AF.Square))
                    T_.wait(S)
                    i1 = nt.matmul(h.pSum[0:1, :], h.onec[0:64, 0:1], h.sq_bf[0:64, :],
                                   start=True, stop=True)
                    T_.mark(i1)
                    V.wait(T_)
                    V.fence()
                    V.mark(nv.reduce_max(nrm[:, 0, 0:1], h.pSum[0:1, :], axis=AX.X))
                    V.fence()
                    if first:
                        V.mark(nv.tensor_copy(bcell[:, part * 8 + hh:part * 8 + hh + 1], nrm[:, 0, 0:1]))
                    else:
                        V.mark(nv.tensor_tensor(bcell[:, part * 8 + hh:part * 8 + hh + 1],
                                                bcell[:, part * 8 + hh:part * 8 + hh + 1],
                                                nrm[:, 0, 0:1], ALU.max))
                    first = False
            V.fence()
            V.mark(nv.tensor_tensor(nrm[:, 0, 0:1], bcell[:, hh:hh + 1], bcell[:, 8 + hh:8 + hh + 1], ALU.mult))
            S.wait(V)
            S.mark(S.e.activation(nrm[:, 0, 0:1], nrm[:, 0, 0:1], AF.Sqrt))
            V.wait(S)
            V.mark(nv.tensor_scalar(nrm[:, 0, 0:1], nrm[:, 0, 0:1], -1.0, None, ALU.mult))
            V.fence()
            V.mark(nv.memset(qa[64:65, hh, :], 1.0))
            V.fence()
            V.mark(nv.tensor_scalar(qa[64:65, hh, :], qa[64:65, hh, :], nrm[0:1, 0, 0:1], None, ALU.mult))
        SY.fence()
        for hh in range(8):
            G.e.wait_ge(T_.sem, ctx.T.n)
            G.mark(ng.dma_start(out=vhead[:],
                                in_=vstash.ap()[:, :, hh * 64:(hh + 1) * 64]
                                .rearrange("c p f -> p c f")), 16)
            T_.wait(G)
            for qt in range(8):
                q0 = qt * 512
                pO = h.P[3]
                pD = h.P[4]
                for kt in range(4 * (qt + 1)):
                    k0 = kt * 128
                    T_.wait(V); T_.wait(S)
                    T_.mark(nt.matmul(h.pX[:], ka[:, hh, k0:k0 + 128], qa[:, hh, q0:q0 + 512],
                                      start=True, stop=True))
                    S.wait(T_)
                    S.mark(S.e.activation(ost[:], h.pX[:], AF.Exp))
                    diag = (kt // 4 == qt)
                    if diag:
                        V.wait(S)
                        j = kt % 4
                        V.mark(nv.tensor_tensor(ost[:, j * 128:(j + 1) * 128],
                                                ost[:, j * 128:(j + 1) * 128], tri[:], ALU.mult))
                        if j > 0:
                            V.fence()
                            V.mark(nv.memset(ost[:, 0:j * 128], 0.0))
                        T_.wait(V)
                    else:
                        T_.wait(S)
                    last = (kt == 4 * (qt + 1) - 1)
                    io = nt.matmul(pO[0:64, :], vhead[:, kt, :], ost[:],
                                   start=(kt == 0), stop=last)
                    instd = nt.matmul(pD[0:1, :], h.onec[:, 0:1], ost[:],
                                      start=(kt == 0), stop=last)
                    if last:
                        T_.mark(io); T_.mark(instd)
                V.wait(T_)
                V.fence()
                V.mark(nv.reciprocal(dden[:], pD[0:1, :]))
                T_.wait(V)
                T_.mark(nt.matmul(h.pBa[0:64, :], h.onerf[0:1, 0:64], dden[:], start=True, stop=True))
                S.wait(T_)
                S.mark(S.e.copy(h.y_bf[0:64, :], h.pBa[0:64, :]))
                V.wait(S); V.wait(SY)
                V.fence()
                V.mark(nv.tensor_tensor(obuf[:, 0, :], pO[0:64, :], h.y_bf[0:64, :], ALU.mult))
                SY.wait(V)
                SY.mark(nsy.dma_start(out=oh.ap()[hh, :, q0:q0 + 512], in_=obuf[:, 0, :]), 16)
                T_.wait(V)
        ng.wait_ge(SY.sem, SY.n)
        ng.wait_ge(G.sem, G.n)
    return nc

def host_l2_attn_heads(ins, x_full):
    H, dk = 16, 64
    wq = ins["L2_attn_wq"].astype(np.float32)
    wk = ins["L2_attn_wk"].astype(np.float32)
    wv = ins["L2_attn_wv"].astype(np.float32)
    half = dk // 2
    inv = 10000.0 ** (-np.arange(half, dtype=np.float32) / half)
    ang = np.arange(T, dtype=np.float32)[:, None] * inv
    cos1, sin1 = np.cos(ang).astype(np.float32), np.sin(ang).astype(np.float32)
    cosr = np.zeros((64, T), np.float32); sinr = np.zeros((64, T), np.float32)
    cosr[0::2] = cos1.T; cosr[1::2] = cos1.T
    sinr[0::2] = -sin1.T; sinr[1::2] = sin1.T
    ctab = np.zeros((128, 4, T), np.float32)
    ctab[:, 0] = np.concatenate([cosr, cosr], 0)
    ctab[:, 1] = np.concatenate([sinr, sinr], 0)
    ctab[:, 2] = ctab[:, 0] / 8.0
    ctab[:, 3] = ctab[:, 1] / 8.0
    psw = np.zeros((128, 128), np.float32)
    for i in range(0, 128, 2):
        psw[i + 1, i] = 1.0
        psw[i, i + 1] = 1.0
    tri = np.ascontiguousarray(np.tril(np.ones((128, 128), np.float32)).T)
    maps = []
    for c in range(NCORES):
        b = c // 2
        h0 = (c % 2) * 8
        cols = []
        for w in (wq, wk):
            cols.append(w[h0 * 64:(h0 + 8) * 64].T)
        cols.append(wv[h0 * 64:(h0 + 8) * 64].T)
        wqkv = np.ascontiguousarray(np.concatenate(cols, 1).reshape(DC, 128, 1536))
        xb = np.ascontiguousarray(to_fm(x_full[b].astype(np.float32)))
        maps.append(dict(xb_in=xb, wqkv=wqkv, ctab=ctab, cpsw=psw, ctri=tri,
                         **std_host_consts()))
    return maps


def run_launch(build_fn, common, per_core_xs):
    nc = bass.Bass("TRN2", target_bir_lowering=False)
    build_fn(nc)
    maps = [dict(common, xs=per_core_xs[c]) for c in range(NCORES)]
    res = run_bass_kernel_spmd(nc, maps, core_ids=list(range(NCORES)))
    return [np.asarray(r["xo"]).reshape(128, DC, TOK) for r in res.results]




def build_l3_qkv(nc):
    xs = nc.dram_tensor("xs", [128, DC, TOK], F32, kind="ExternalInput").ap()
    wq = nc.dram_tensor("wq", [DC, 128, 1024], F32, kind="ExternalInput").ap()
    wk = nc.dram_tensor("wk", [DC, 128, 1024], F32, kind="ExternalInput").ap()
    wv = nc.dram_tensor("wv", [DC, 128, 1024], F32, kind="ExternalInput").ap()
    brows = nc.dram_tensor("brows", [128, 3, DC], F32, kind="ExternalInput").ap()
    bvrow = nc.dram_tensor("bvrow", [2, 1024], F32, kind="ExternalInput").ap()
    conesr = nc.dram_tensor("conesr", [2, 512], F32, kind="ExternalInput").ap()
    cones = nc.dram_tensor("cones", [128, 2], F32, kind="ExternalInput").ap()
    cflt = nc.dram_tensor("cflt", [128, 4], F32, kind="ExternalInput").ap()
    qs = nc.dram_tensor("qs", [128, DC, TOK], BF16, kind="ExternalOutput")
    ks = nc.dram_tensor("ks", [128, DC, TOK], BF16, kind="ExternalOutput")
    vs = nc.dram_tensor("vs", [16, 128, 1024], BF16, kind="ExternalOutput")
    knorm = nc.dram_tensor("knorm", [1, 1], F32, kind="ExternalOutput")

    with contextlib.ExitStack() as es:
        ctx = Ctx(nc, es)
        T_, V, S, SY, G, CC = ctx.T, ctx.V, ctx.S, ctx.SY, ctx.G, ctx.CC
        sb = ctx.sb
        nv = nc.vector; nt = nc.tensor; ng = nc.gpsimd; nsy = nc.sync
        h = std_setup(ctx, nc, es)
        x_bf = sb("x_bf", (128, DC, TOK), BF16)
        W3 = sb("W3", (128, 3, DC, 1024), BF16)
        brw = sb("brw", (128, 3, DC), F32)
        bvb = sb("bvb", (128, 1024), F32)
        kcell = sb("kcell", (1, 2), F32)
        blk = es.enter_context(nc.Block())
        G.mark(ng.dma_start(out=x_bf[:], in_=xs), 16)
        G.mark(ng.dma_start(out=W3[:, 0].rearrange("p c f -> p c f"), in_=wq.rearrange("c p f -> p c f")), 16)
        G.mark(ng.dma_start(out=W3[:, 1], in_=wk.rearrange("c p f -> p c f")), 16)
        G.mark(ng.dma_start(out=W3[:, 2], in_=wv.rearrange("c p f -> p c f")), 16)
        h.load_std_consts(conesr, cones, cflt)
        h.SD.mark(nc.scalar.dma_start(out=brw[:].rearrange("p a b -> p (a b)"),
                                      in_=brows.rearrange("p a b -> p (a b)")), 16)
        T_.wait(G); V.wait(G); S.wait(G)
        T_.wait(h.SD); V.wait(h.SD); S.wait(h.SD)
        G.fence()
        bvr = sb("bvr", (2, 1024), F32)
        SY.mark(nsy.dma_start(out=bvr[:], in_=bvrow), 16)
        T_.wait(SY)
        for half in range(2):
            T_.wait(V)
            T_.mark(nt.matmul(h.pBa[:, 0:512], h.onerf[0:1, 0:128], bvr[0:1, half * 512:(half + 1) * 512],
                              start=True, stop=True))
            V.wait(T_)
            V.mark(nv.tensor_copy(bvb[:, half * 512:(half + 1) * 512], h.pBa[:, 0:512]))
        T_.wait(V)
        V.fence()
        V.mark(nv.memset(kcell[:], 0.0))
        for w3i, outd in ((0, qs), (1, ks)):
            for b in range(4):
                t0 = b * 512
                for oc in range(DC):
                    dst_ps = h.pPA if oc % 2 == 0 else h.pPB
                    T_.wait(V)
                    for dc in range(DC):
                        inst = nt.matmul(dst_ps[:], W3[:, w3i, dc, oc * 128:(oc + 1) * 128],
                                         x_bf[:, dc, t0:t0 + 512], start=(dc == 0), stop=(dc == DC - 1))
                    T_.mark(inst)
                    V.wait(T_); V.wait(S); V.wait(SY)
                    V.fence()
                    V.mark(nv.tensor_scalar(h.scrA[:], dst_ps[:], brw[:, w3i, oc:oc + 1], None, ALU.add))
                    S.wait(V)
                    S.mark(S.e.copy(h.y_bf[:], h.scrA[:]))
                    SY.wait(S)
                    SY.mark(nsy.dma_start(out=outd.ap()[:, oc, t0:t0 + 512], in_=h.y_bf[:]), 16)
                    if w3i == 1:
                        S.mark(S.e.activation(h.sq_bf[:], h.scrA[:], AF.Square))
                        T_.wait(S)
                        i1 = nt.matmul(h.pSum[0:1, :], h.onec[:, 0:1], h.sq_bf[:],
                                       start=(oc == 0), stop=(oc == DC - 1))
                        T_.mark(i1)
                        if oc == DC - 1:
                            V.wait(T_)
                            V.fence()
                            V.mark(nv.reduce_max(kcell[:, 1:2], h.pSum[0:1, :], axis=AX.X))
                            V.fence()
                            V.mark(nv.tensor_tensor(kcell[:, 0:1], kcell[:, 0:1], kcell[:, 1:2], ALU.max))
                            T_.wait(V)
        for ti in range(16):
            t0 = ti * 128
            dst_ps = h.pPA if ti % 2 == 0 else h.pPB
            for half in range(2):
                pd = dst_ps if half == 0 else (h.pPB if ti % 2 == 0 else h.pPA)
                T_.wait(V)
                for dc in range(DC):
                    inst = nt.matmul(pd[:], x_bf[:, dc, t0:t0 + 128],
                                     W3[:, 2, dc, half * 512:(half + 1) * 512],
                                     start=(dc == 0), stop=(dc == DC - 1))
                T_.mark(inst)
                V.wait(T_); V.wait(SY)
                V.fence()
                V.mark(nv.tensor_tensor(h.scrA[:], pd[:], bvb[:, half * 512:(half + 1) * 512], ALU.add))
                S.wait(V)
                S.mark(S.e.copy(h.y_bf[:], h.scrA[:]))
                SY.wait(S)
                SY.mark(nsy.dma_start(out=vs.ap()[ti, :, half * 512:(half + 1) * 512], in_=h.y_bf[:]), 16)
        SY.wait(V)
        SY.mark(nsy.dma_start(out=knorm.ap(), in_=kcell[:, 0:1]), 16)
        ng.wait_ge(SY.sem, SY.n)
        ng.wait_ge(G.sem, G.n)
    return nc

def build_l3_attn(nc):
    xs = nc.dram_tensor("xs", [128, DC, TOK], F32, kind="ExternalInput").ap()
    qsh = nc.dram_tensor("qsh", [128, DC, TOK], BF16, kind="ExternalInput").ap()
    kfull = nc.dram_tensor("kfull", [4, 128, DC, T], BF16, kind="ExternalInput").ap()
    vfull = nc.dram_tensor("vfull", [4, 32, 128, 1024], BF16, kind="ExternalInput").ap()
    knorms = nc.dram_tensor("knorms", [1, 8], F32, kind="ExternalInput").ap()
    wo = nc.dram_tensor("wo", [DC, 128, 1024], F32, kind="ExternalInput").ap()
    borow = nc.dram_tensor("borow", [128, DC], F32, kind="ExternalInput").ap()
    cam = nc.dram_tensor("cam", [128, 32, 512], F32, kind="ExternalInput").ap()
    cumask = nc.dram_tensor("cumask", [128, 4], F32, kind="ExternalInput").ap()
    conesr = nc.dram_tensor("conesr", [2, 512], F32, kind="ExternalInput").ap()
    cones = nc.dram_tensor("cones", [128, 2], F32, kind="ExternalInput").ap()
    cflt = nc.dram_tensor("cflt", [128, 4], F32, kind="ExternalInput").ap()
    xo = nc.dram_tensor("xo", [128, DC, TOK], F32, kind="ExternalOutput").ap()
    ybuf = nc.dram_tensor("ybuf", [128, DC, TOK], F32)
    CBASE = 512

    with contextlib.ExitStack() as es:
        ctx = Ctx(nc, es)
        T_, V, S, SY, G, CC = ctx.T, ctx.V, ctx.S, ctx.SY, ctx.G, ctx.CC
        sb = ctx.sb
        nv = nc.vector; nt = nc.tensor; ng = nc.gpsimd; nsy = nc.sync
        h = std_setup(ctx, nc, es)
        q2 = sb("q2", (128, DC, TOK), BF16)
        st_all = sb("st_all", (128, 32, 512), BF16)
        ktile = [sb(f"ktile{i}", (128, DC, 128), BF16) for i in range(2)]
        vtile = [sb(f"vtile{i}", (128, 512), BF16) for i in range(2)]
        WO = sb("WO", (128, DC, 1024), BF16)
        o_sb = sb("o_sb", (128, DC, 512), BF16)
        bor = sb("bor", (128, DC), F32)
        am = sb("am", (128, 32, 512), BF16)
        um = sb("um", (128, 4), F32)
        kn = sb("kn", (1, 8), F32)
        negB = sb("negB", (1, 4, 512), BF16)
        denr = sb("denr", (1, 512), F32)
        denb = sb("denb", (1, 512), BF16)
        uni = sb("uni", (1, 2, 512), BF16)
        pT = ctx.ps("pT", (128, 16), BF16)
        unif = sb("unif", (128, DC), F32)
        ksem = [es.enter_context(nc.semaphore("kt0")), es.enter_context(nc.semaphore("kt1"))]
        kcnt = [0, 0]
        kcons = [0, 0]
        vsem = [es.enter_context(nc.semaphore("vt0")), es.enter_context(nc.semaphore("vt1"))]
        vcnt = [0, 0]
        vcons = [0, 0]

        blk = es.enter_context(nc.Block())
        G.mark(ng.dma_start(out=q2[:], in_=qsh), 16)
        G.mark(ng.dma_start(out=am[:].rearrange("p a c -> p (a c)"), in_=cam.rearrange("p a c -> p (a c)")), 16)
        G.mark(ng.dma_start(out=WO[:], in_=wo.rearrange("c p f -> p c f")), 16)
        h.load_std_consts(conesr, cones, cflt)
        h.SD.mark(nc.scalar.dma_start(out=bor[:], in_=borow), 16)
        h.SD.mark(nc.scalar.dma_start(out=um[:], in_=cumask), 16)
        h.SD.mark(nc.scalar.dma_start(out=kn[:], in_=knorms), 16)
        T_.wait(G); V.wait(G); S.wait(G)
        T_.wait(h.SD); V.wait(h.SD); S.wait(h.SD)
        G.fence()
        V.fence()
        V.mark(nv.reduce_max(denr[:, 0:1], kn[:], axis=AX.X))
        S.wait(V)
        S.mark(S.e.activation(denr[:, 0:1], denr[:, 0:1], AF.Sqrt))
        V.wait(S)
        for dc in range(DC):
            V.fence()
            V.mark(nv.tensor_scalar(q2[:, dc, :], q2[:, dc, :], 1.0 / 32.0, None, ALU.mult))
        for b in range(4):
            t0 = b * 512
            for oc in range(DC):
                S.wait(V); S.wait(T_)
                S.mark(S.e.activation(h.sq_bf[:], q2[:, oc, t0:t0 + 512], AF.Square))
                T_.wait(S)
                i1 = nt.matmul(h.pSum[0:1, :], h.onec[:, 0:1], h.sq_bf[:],
                               start=(oc == 0), stop=(oc == DC - 1))
                T_.mark(i1)
            S.wait(T_)
            S.mark(S.e.activation(h.scrA[0:1, :], h.pSum[0:1, :], AF.Sqrt))
            V.wait(S)
            V.mark(nv.tensor_scalar(h.scrA[0:1, :], h.scrA[0:1, :], denr[0:1, 0:1], None, ALU.mult))
            V.fence()
            V.mark(nv.tensor_scalar(negB[:, b, :], h.scrA[0:1, :], -1.0, None, ALU.mult))
            T_.wait(V)

        def kload(slot, b, kt):
            G.e.wait_ge(T_.sem, kcons[slot])
            ng.dma_start(out=ktile[slot][:], in_=kfull[b, :, :, kt * 128:(kt + 1) * 128]).then_inc(ksem[slot], 16)
            kcnt[slot] += 16

        def vload(slot, b, kt, half):
            G.e.wait_ge(T_.sem, vcons[slot])
            ng.dma_start(out=vtile[slot][:], in_=vfull[b, kt, :, half * 512:(half + 1) * 512]).then_inc(vsem[slot], 16)
            vcnt[slot] += 16

        for b in range(4):
            kload(0, b, 0)
            for kt in range(32):
                slot = kt % 2
                if kt + 1 < 32:
                    kload((kt + 1) % 2, b, kt + 1)
                T_.e.wait_ge(ksem[slot], kcnt[slot])
                T_.wait(S)
                for dc in range(DC):
                    nt.matmul(h.pX[:], ktile[slot][:, dc, :], q2[:, dc, b * 512:(b + 1) * 512],
                              start=(dc == 0), stop=False)
                ib = nt.matmul(h.pX[:], h.onerw[0:1, 0:128], negB[:, b, :], start=False, stop=True)
                T_.mark(ib)
                kcons[slot] = T_.n
                S.wait(T_); S.wait(V)
                S.mark(S.e.activation(st_all[:, kt, :], h.pX[:], AF.Exp))
                V.wait(S)
                V.fence()
                V.mark(nv.tensor_tensor(st_all[:, kt, :], st_all[:, kt, :],
                                        am[:, kt, :], ALU.mult))
            T_.wait(V)
            for kt in range(32):
                iD = nt.matmul(h.pSum[0:1, :], h.onec[:, 0:1], st_all[:, kt, :],
                               start=(kt == 0), stop=(kt == 31))
            T_.mark(iD)
            V.wait(T_)
            V.fence()
            V.mark(nv.tensor_scalar(denr[:], h.pSum[0:1, :], 1e-30, None, ALU.add))
            V.fence()
            V.mark(nv.reciprocal(denr[:], denr[:]))
            V.fence()
            V.mark(nv.tensor_copy(denb[:], denr[:]))
            T_.wait(V)
            T_.mark(nt.matmul(h.pBa[:], h.onerw[0:1, 0:128], denb[:], start=True, stop=True))
            S.wait(T_)
            S.mark(S.e.copy(h.y_bf[:], h.pBa[:]))
            for half in range(2):
                vload(0, b, 0, half)
                for kt in range(32):
                    slot = kt % 2
                    if kt + 1 < 32:
                        vload((kt + 1) % 2, b, kt + 1, half)
                    T_.e.wait_ge(vsem[slot], vcnt[slot])
                    for dvc in range(4):
                        io = nt.matmul(h.P[dvc][:], vtile[slot][:, dvc * 128:(dvc + 1) * 128],
                                       st_all[:, kt, :], start=(kt == 0), stop=(kt == 31))
                    iu = nt.matmul(h.P[5][0:1, :], h.onec[:, 0:1], vtile[slot][:],
                                   start=(kt == 0), stop=(kt == 31))
                    T_.mark(iu)
                    if kt == 31:
                        T_.mark(io)
                    vcons[slot] = T_.n
                V.wait(T_)
                V.fence()
                V.wait(S)
                V.mark(nv.tensor_scalar(uni[:, half, :], h.P[5][0:1, :], 1.0 / 4096.0, None, ALU.mult))
                for dvc in range(4):
                    oc = half * 4 + dvc
                    V.fence()
                    V.wait(S)
                    V.mark(nv.tensor_tensor(h.scrA[:], h.P[dvc][:], h.y_bf[:], ALU.mult))
                    T_.wait(V)
                    T_.mark(nt.transpose(pT[:, 0:1], uni[0:1, half, dvc * 128:(dvc + 1) * 128],
                                         h.onerw[0:1, 0:1]))
                    V.wait(T_)
                    for j in range(2):
                        V.fence()
                        V.mark(nv.tensor_scalar(h.scrA[:, 510 + j:511 + j], h.scrA[:, 510 + j:511 + j],
                                                um[:, 2 + j:3 + j], None, ALU.mult))
                        V.fence()
                        V.mark(nv.tensor_scalar(h.scrB[:, 0:1], pT[:, 0:1],
                                                um[:, j:j + 1], None, ALU.mult))
                        V.fence()
                        V.mark(nv.tensor_tensor(h.scrA[:, 510 + j:511 + j], h.scrA[:, 510 + j:511 + j],
                                                h.scrB[:, 0:1], ALU.add))
                    S.wait(V)
                    S.mark(S.e.copy(o_sb[:, oc, :], h.scrA[:]))
                    T_.wait(S)
            t0 = b * 512
            for oc in range(DC):
                dst_ps = h.pPA if oc % 2 == 0 else h.pPB
                for fc in range(DC):
                    inst = nt.matmul(dst_ps[:], WO[:, fc, oc * 128:(oc + 1) * 128],
                                     o_sb[:, fc, :], start=(fc == 0), stop=(fc == DC - 1))
                T_.mark(inst)
                V.wait(T_); V.wait(S)
                V.fence()
                V.mark(nv.tensor_scalar(h.scrA[:], dst_ps[:], bor[:, oc:oc + 1], None, ALU.add))
                h.resid_stats_chunk(oc, DC, h.scrA[:], xs, t0, ybuf.ap())
            h.stats_finish(b)
        h.ln_apply_pass(ybuf.ap(), xo)
        ng.wait_ge(SY.sem, SY.n)
        ng.wait_ge(G.sem, G.n)
        ng.wait_ge(ksem[0], kcnt[0]); ng.wait_ge(ksem[1], kcnt[1])
        ng.wait_ge(vsem[0], vcnt[0]); ng.wait_ge(vsem[1], vcnt[1])
    return nc


def host_l3_qkv(ins):
    wq = np.ascontiguousarray(ins["L3_attn_wq"].astype(np.float32).reshape(DC, 128, 1024))
    wk = np.ascontiguousarray(ins["L3_attn_wk"].astype(np.float32).reshape(DC, 128, 1024))
    wv = np.ascontiguousarray(ins["L3_attn_wv"].astype(np.float32).reshape(DC, 128, 1024))
    br = np.stack([ins["L3_attn_bq"], ins["L3_attn_bk"], ins["L3_attn_bv"]], 0).astype(np.float32)
    brows = np.ascontiguousarray(br.reshape(3, DC, 128).transpose(2, 0, 1))
    bvrow = np.zeros((2, 1024), np.float32)
    bvrow[0] = ins["L3_attn_bv"].astype(np.float32)
    return dict(wq=wq, wk=wk, wv=wv, brows=brows, bvrow=bvrow, **std_host_consts())

def host_l3_attn(ins, c, kfull, vfull, knorms):
    C0 = c * TPC
    p = np.arange(128)[:, None, None]
    kt = np.arange(32)[None, :, None]
    col = np.arange(512)[None, None, :]
    am = (kt * 128 + p >= C0 + col + 2).astype(np.float32)
    cum = np.zeros((128, 4), np.float32)
    own = 1.0 if c == NCORES - 1 else 0.0
    cum[:, 0:2] = own
    cum[:, 2:4] = 1.0 - own
    wo = np.ascontiguousarray(ins["L3_attn_wo"].astype(np.float32).reshape(DC, 128, 1024))
    borow = np.ascontiguousarray(ins["L3_attn_bo"].astype(np.float32).reshape(DC, 128).T)
    return dict(kfull=kfull, vfull=vfull, knorms=knorms, wo=wo, borow=borow,
                cam=np.ascontiguousarray(am), cumask=cum, **std_host_consts())

def run_l3_attn(ins, xs_fm_shards, runner):
    nc = bass.Bass("TRN2", target_bir_lowering=False)
    build_l3_qkv(nc)
    com = host_l3_qkv(ins)
    maps = [dict(com, xs=xs_fm_shards[c]) for c in range(NCORES)]
    res = runner(nc, maps)
    kfull = np.zeros((4, 128, DC, T), dtype=BF)
    vfull = np.zeros((4, 32, 128, 1024), dtype=BF)
    qs_sh = []
    knorms = np.zeros((1, 8), np.float32)
    for c in range(NCORES):
        ksx = np.asarray(res[c]["ks"]).reshape(128, DC, TOK)
        vsx = np.asarray(res[c]["vs"]).reshape(16, 128, 1024)
        qs_sh.append(np.ascontiguousarray(np.asarray(res[c]["qs"]).reshape(128, DC, TOK)))
        for b in range(4):
            kfull[b, :, :, c * TPC:(c + 1) * TPC] = ksx[:, :, b * TPC:(b + 1) * TPC]
            vfull[b, c * 4:(c + 1) * 4] = vsx[b * 4:(b + 1) * 4]
        knorms[0, c] = np.asarray(res[c]["knorm"]).astype(np.float32).flatten()[0]
    kfull = np.ascontiguousarray(kfull)
    vfull = np.ascontiguousarray(vfull)
    nc2 = bass.Bass("TRN2", target_bir_lowering=False)
    build_l3_attn(nc2)
    maps2 = [dict(host_l3_attn(ins, c, kfull, vfull, knorms),
                  xs=xs_fm_shards[c], qsh=qs_sh[c]) for c in range(NCORES)]
    res2 = runner(nc2, maps2)
    return [np.ascontiguousarray(np.asarray(r["xo"], dtype=np.float32).reshape(128, DC, TOK)) for r in res2]


def run_l2_attn(ins, xs_fm_shards, runner):
    full = unshard_tokens([from_fm(s) for s in xs_fm_shards], D)
    nc = bass.Bass("TRN2", target_bir_lowering=False)
    build_l2_attn_heads(nc)
    maps = host_l2_attn_heads(ins, full)
    res = runner(nc, maps)
    o_full = np.zeros((B, T, D), np.float32)
    for c in range(NCORES):
        b = c // 2
        h0 = (c % 2) * 8
        ohh = np.asarray(res[c]["oh"], dtype=np.float32)
        o_full[b, :, h0 * 64:(h0 + 8) * 64] = ohh.transpose(2, 0, 1).reshape(T, 512)
    o_shards = [to_fm(s) for s in shard_tokens(o_full)]
    wo = ins["L2_attn_wo"].astype(np.float32)
    wp = np.ascontiguousarray(wo.T.reshape(DC, 128, 1024))
    nc2 = bass.Bass("TRN2", target_bir_lowering=False)
    build_proj_resid_ln(nc2)
    com = dict(wp=wp, wbias=np.zeros((128, DC), np.float32), **std_host_consts())
    maps2 = [dict(com, xs=xs_fm_shards[c], osd=o_shards[c]) for c in range(NCORES)]
    res2 = runner(nc2, maps2)
    return [np.ascontiguousarray(np.asarray(r["xo"], dtype=np.float32).reshape(128, DC, TOK)) for r in res2]



def build_l1_proj(nc):
    xs = nc.dram_tensor("xs", [128, DC, TOK], F32, kind="ExternalInput").ap()
    xh = nc.dram_tensor("xh", [128, DC, 12], F32, kind="ExternalInput").ap()
    w3 = nc.dram_tensor("w3", [3, DC, 128, 1024], F32, kind="ExternalInput").ap()
    wbg = nc.dram_tensor("wbg", [DC, 128, 8], F32, kind="ExternalInput").ap()
    wmx = nc.dram_tensor("wmx", [DC, 128, 4], F32, kind="ExternalInput").ap()
    cvw = nc.dram_tensor("cvw", [128, 3, DC, 4], F32, kind="ExternalInput").ap()
    cmixw = nc.dram_tensor("cmixw", [4, 4], F32, kind="ExternalInput").ap()
    conesr = nc.dram_tensor("conesr", [2, 512], F32, kind="ExternalInput").ap()
    cones = nc.dram_tensor("cones", [128, 2], F32, kind="ExternalInput").ap()
    cflt = nc.dram_tensor("cflt", [128, 4], F32, kind="ExternalInput").ap()
    qs = nc.dram_tensor("qs", [128, DC, TOK], BF16, kind="ExternalOutput")
    ks = nc.dram_tensor("ks", [128, DC, TOK], BF16, kind="ExternalOutput")
    vs = nc.dram_tensor("vs", [128, DC, TOK], BF16, kind="ExternalOutput")
    bgm = nc.dram_tensor("bgm", [12, TOK], F32, kind="ExternalOutput")

    with contextlib.ExitStack() as es:
        ctx = Ctx(nc, es)
        T_, V, S, SY, G, CC = ctx.T, ctx.V, ctx.S, ctx.SY, ctx.G, ctx.CC
        sb = ctx.sb
        nv = nc.vector; nt = nc.tensor; ng = nc.gpsimd; nsy = nc.sync
        h = std_setup(ctx, nc, es, lite=True)
        x_bf = sb("x_bf", (128, DC, TOK), BF16)
        xh_bf = sb("xh_bf", (128, DC, 12), BF16)
        W3 = sb("W3", (128, 3, DC, 1024), BF16)
        wbgb = sb("wbgb", (128, DC, 8), BF16)
        wmxb = sb("wmxb", (128, DC, 4), BF16)
        cvf = sb("cvf", (128, 3, DC, 4), F32)
        cmx = sb("cmx", (4, 4), F32)
        ppad = sb("ppad", (128, 516), F32)
        ppadM = sb("ppadM", (4, 516), F32)
        scrF = sb("scrF", (128, 512), F32)
        bgm_sb = sb("bgm_sb", (36, TOK), F32)
        blk = es.enter_context(nc.Block())
        G.mark(ng.dma_start(out=x_bf[:], in_=xs), 16)
        G.mark(ng.dma_start(out=xh_bf[:].rearrange("p c f -> p (c f)"),
                            in_=xh.rearrange("p c f -> p (c f)")), 16)
        for w3i in range(3):
            G.mark(ng.dma_start(out=W3[:, w3i], in_=w3[w3i].rearrange("c p f -> p c f")), 16)
        G.mark(ng.dma_start(out=wbgb[:], in_=wbg.rearrange("c p f -> p c f")), 16)
        G.mark(ng.dma_start(out=wmxb[:], in_=wmx.rearrange("c p f -> p c f")), 16)
        h.load_std_consts(conesr, cones, cflt)
        h.SD.mark(nc.scalar.dma_start(out=cvf[:].rearrange("p a b c -> p (a b c)"),
                                      in_=cvw.rearrange("p a b c -> p (a b c)")), 16)
        h.SD.mark(nc.scalar.dma_start(out=cmx[:], in_=cmixw), 16)
        T_.wait(G); V.wait(G); S.wait(G)
        T_.wait(h.SD); V.wait(h.SD); S.wait(h.SD)
        G.fence()
        for w3i, outd in ((0, qs), (1, ks), (2, vs)):
            for b in range(4):
                t0 = b * 512
                for oc in range(DC):
                    dst_ps = h.pPA if oc % 2 == 0 else h.pPB
                    T_.wait(V)
                    for dc in range(DC):
                        im = nt.matmul(dst_ps[:], W3[:, w3i, dc, oc * 128:(oc + 1) * 128],
                                       x_bf[:, dc, t0:t0 + 512], start=(dc == 0), stop=(dc == DC - 1))
                    T_.mark(im)
                    for dc in range(DC):
                        ih = nt.matmul(h.pX[:, 0:3], W3[:, w3i, dc, oc * 128:(oc + 1) * 128],
                                       xh_bf[:, dc, b * 3:b * 3 + 3], start=(dc == 0), stop=(dc == DC - 1))
                    T_.mark(ih)
                    V.wait(T_); V.wait(S)
                    V.fence()
                    V.mark(nv.tensor_copy(ppad[:, 3:515], dst_ps[:]))
                    V.mark(nv.tensor_copy(ppad[:, 0:3], h.pX[:, 0:3]))
                    V.fence()
                    V.mark(nv.tensor_scalar(h.scrA[:], ppad[:, 3:515], cvf[:, w3i, oc, 3:4], None, ALU.mult))
                    for k in range(3):
                        V.fence()
                        V.mark(nv.scalar_tensor_tensor(h.scrA[:], ppad[:, k:k + 512], cvf[:, w3i, oc, k:k + 1],
                                                       h.scrA[:], ALU.mult, ALU.add))
                    S.wait(V); S.wait(SY)
                    if w3i < 2:
                        S.mark(S.e.activation(scrF[:], h.scrA[:], AF.Silu))
                        S.fence()
                        S.mark(S.e.activation(h.y_bf[:], scrF[:], AF.Silu))
                    else:
                        S.mark(S.e.activation(h.y_bf[:], h.scrA[:], AF.Silu))
                    SY.wait(S)
                    SY.mark(nsy.dma_start(out=outd.ap()[:, oc, t0:t0 + 512], in_=h.y_bf[:]), 16)
        for b in range(4):
            t0 = b * 512
            T_.wait(V); T_.wait(S)
            for dc in range(DC):
                ib = nt.matmul(h.pSum[0:8, :], wbgb[:, dc, :], x_bf[:, dc, t0:t0 + 512],
                               start=(dc == 0), stop=(dc == DC - 1))
            T_.mark(ib)
            for dc in range(DC):
                imx = nt.matmul(h.pSq[0:4, :], wmxb[:, dc, :], x_bf[:, dc, t0:t0 + 512],
                                start=(dc == 0), stop=(dc == DC - 1))
            T_.mark(imx)
            for dc in range(DC):
                imh = nt.matmul(h.pBa[0:4, 0:3], wmxb[:, dc, :], xh_bf[:, dc, b * 3:b * 3 + 3],
                                start=(dc == 0), stop=(dc == DC - 1))
            T_.mark(imh)
            S.wait(T_)
            S.mark(S.e.activation(bgm_sb[0:8, t0:t0 + 512], h.pSum[0:8, :], AF.Sigmoid))
            V.wait(T_); V.wait(S)
            V.fence()
            V.mark(nv.tensor_copy(ppadM[:, 3:515], h.pSq[0:4, :]))
            V.mark(nv.tensor_copy(ppadM[:, 0:3], h.pBa[0:4, 0:3]))
            V.fence()
            V.mark(nv.tensor_scalar(scrF[0:4, :], ppadM[:, 3:515], cmx[:, 3:4], None, ALU.mult))
            for k in range(3):
                V.fence()
                V.mark(nv.scalar_tensor_tensor(scrF[0:4, :], ppadM[:, k:k + 512], cmx[:, k:k + 1],
                                               scrF[0:4, :], ALU.mult, ALU.add))
            S.wait(V)
            S.mark(S.e.activation(bgm_sb[32:36, t0:t0 + 512], scrF[0:4, :], AF.Silu))
        SY.wait(S)
        SY.mark(nsy.dma_start(out=bgm.ap()[0:8, :], in_=bgm_sb[0:8, :]), 16)
        SY.mark(nsy.dma_start(out=bgm.ap()[8:12, :], in_=bgm_sb[32:36, :]), 16)
        ng.wait_ge(SY.sem, SY.n)
        ng.wait_ge(G.sem, G.n)
    return nc


def host_l1_proj(ins, xs_fm_shards):
    w3 = np.stack([ins["L1_attn_Wq"], ins["L1_attn_Wk"], ins["L1_attn_Wv"]], 0)
    w3 = np.ascontiguousarray(w3.astype(np.float32).reshape(3, DC, 128, 1024))
    wbg = np.concatenate([ins["L1_attn_Wb"], ins["L1_attn_Wdec"]], 1).astype(np.float32)
    wbg = np.ascontiguousarray(wbg.reshape(DC, 128, 8))
    wmx = np.ascontiguousarray(ins["L1_attn_Wmix"].astype(np.float32).reshape(DC, 128, 4))
    cv = np.stack([ins["L1_attn_conv_q"], ins["L1_attn_conv_k"], ins["L1_attn_conv_v"]], 0)
    cvw = np.ascontiguousarray(cv.astype(np.float32).reshape(3, DC, 128, 4).transpose(2, 0, 1, 3))
    com = dict(w3=w3, wbg=wbg, wmx=wmx, cvw=cvw,
               cmixw=np.ascontiguousarray(ins["L1_attn_conv_mix"].astype(np.float32)),
               **std_host_consts())
    maps = []
    for c in range(NCORES):
        xhl = np.zeros((128, DC, 12), np.float32)
        if c > 0:
            prev = xs_fm_shards[c - 1]
            for b in range(4):
                xhl[:, :, b * 3:b * 3 + 3] = prev[:, :, b * 512 + 509:b * 512 + 512]
        maps.append(dict(com, xs=xs_fm_shards[c], xh=np.ascontiguousarray(xhl)))
    return maps



def build_l1_scan(nc):
    qf = nc.dram_tensor("qf", [128, 2, 2, 4096], BF16, kind="ExternalInput").ap()
    kf = nc.dram_tensor("kf", [128, 2, 2, 4096], BF16, kind="ExternalInput").ap()
    vf = nc.dram_tensor("vf", [128, 2, 2, 4096], BF16, kind="ExternalInput").ap()
    ktm = nc.dram_tensor("ktm", [2, 32, 128, 256], BF16, kind="ExternalInput").ap()
    vtm = nc.dram_tensor("vtm", [2, 32, 128, 256], BF16, kind="ExternalInput").ap()
    bcol = nc.dram_tensor("bcol", [2, 128, 32], F32, kind="ExternalInput").ap()
    brw = nc.dram_tensor("brw", [2, 4096], F32, kind="ExternalInput").ap()
    grw = nc.dram_tensor("grw", [2, 4096], F32, kind="ExternalInput").ap()
    mrw = nc.dram_tensor("mrw", [2, 4096], F32, kind="ExternalInput").ap()
    mbias2 = nc.dram_tensor("mbias2", [1, 2], F32, kind="ExternalInput").ap()
    rmswf = nc.dram_tensor("rmswf", [128, 2], F32, kind="ExternalInput").ap()
    mtrin = nc.dram_tensor("mtrin", [128, 128], F32, kind="ExternalInput").ap()
    mtriun = nc.dram_tensor("mtriun", [128, 128], F32, kind="ExternalInput").ap()
    mtui = nc.dram_tensor("mtui", [128, 128], F32, kind="ExternalInput").ap()
    idtf = nc.dram_tensor("idtf", [128, 128], F32, kind="ExternalInput").ap()
    cellk = nc.dram_tensor("cellk", [128, 2], F32, kind="ExternalInput").ap()
    conesr = nc.dram_tensor("conesr", [2, 512], F32, kind="ExternalInput").ap()
    cones = nc.dram_tensor("cones", [128, 2], F32, kind="ExternalInput").ap()
    cflt = nc.dram_tensor("cflt", [128, 4], F32, kind="ExternalInput").ap()
    od = nc.dram_tensor("od", [128, 2, 2, 4096], F32, kind="ExternalOutput").ap()

    with contextlib.ExitStack() as es:
        ctx = Ctx(nc, es)
        T_, V, S, SY, G, CC = ctx.T, ctx.V, ctx.S, ctx.SY, ctx.G, ctx.CC
        sb = ctx.sb
        nv = nc.vector; nt = nc.tensor; ng = nc.gpsimd; nsy = nc.sync
        h = std_setup(ctx, nc, es, lite=True)
        q_fm = sb("q_fm", (128, 2, 4096), BF16)
        k_fm = sb("k_fm", (128, 2, 4096), BF16)
        o_sb = sb("o_sb", (128, 2, 4096), F32)
        gb = sb("gb", (128, 4096), F32)
        in1b = sb("in1b", (128, 4096), F32)
        emab = sb("emab", (128, 4096), F32)
        vfb = sb("vfb", (128, 4096), BF16)
        S_sb = sb("S_sb", (128, 2, 256), F32)
        S_bf = sb("S_bf", (128, 2, 256), BF16)
        ktt = [sb(f"ktt{i}", (128, 256), BF16) for i in range(2)]
        vtt = [sb(f"vtt{i}", (128, 256), BF16) for i in range(2)]
        ktn = sb("ktn", (128, 256), BF16)
        vbn = sb("vbn", (128, 256), BF16)
        vpr = sb("vpr", (128, 256), BF16)
        uadj = sb("uadj", (128, 256), BF16)
        sqk = sb("sqk", (128, 256), BF16)
        nrm = sb("nrm", (128, 2), F32)
        scrG = sb("scrG", (128, 128), F32)
        P_sb = sb("P_sb", (128, 128), BF16)
        Pt_sb = sb("Pt_sb", (128, 128), BF16)
        M_sb = sb("M_sb", (128, 128), BF16)
        Mt_sb = sb("Mt_sb", (128, 128), BF16)
        attnT = sb("attnT", (128, 128), BF16)
        mtl = sb("mtl", (128, 128), F32)
        mtu = sb("mtu", (128, 128), F32)
        mti = sb("mti", (128, 128), F32)
        idt = sb("idt", (128, 128), F32)
        ck = sb("ck", (128, 2), F32)
        bcolu = sb("bcolu", (128, 32), F32)
        rowA = sb("rowA", (1, 4096), F32)
        rowB = sb("rowB", (1, 4096), F32)
        rmsw = sb("rmsw", (128, 2), F32)
        ktsem = [es.enter_context(nc.semaphore("ktm0")), es.enter_context(nc.semaphore("ktm1"))]
        vtsem = [es.enter_context(nc.semaphore("vtm0")), es.enter_context(nc.semaphore("vtm1"))]
        ktc = [0, 0]; vtc = [0, 0]; ktcons = [0, 0]; vtcons = [0, 0]

        blk = es.enter_context(nc.Block())
        h.load_std_consts(conesr, cones, cflt)
        h.SD.mark(nc.scalar.dma_start(out=mtl[:], in_=mtrin), 16)
        h.SD.mark(nc.scalar.dma_start(out=mtu[:], in_=mtriun), 16)
        h.SD.mark(nc.scalar.dma_start(out=mti[:], in_=mtui), 16)
        h.SD.mark(nc.scalar.dma_start(out=idt[:], in_=idtf), 16)
        h.SD.mark(nc.scalar.dma_start(out=ck[:], in_=cellk), 16)
        h.SD.mark(nc.scalar.dma_start(out=rmsw[:], in_=rmswf), 16)
        T_.wait(G); V.wait(G); S.wait(G)
        T_.wait(h.SD); V.wait(h.SD); S.wait(h.SD)
        G.fence()

        def kloadB(slot, u, c):
            G.e.wait_ge(V.sem, ktcons[slot])
            ng.dma_start(out=ktt[slot][:], in_=ktm[u, c]).then_inc(ktsem[slot], 16)
            ktc[slot] += 16

        def vloadB(slot, u, c):
            G.e.wait_ge(V.sem, vtcons[slot])
            ng.dma_start(out=vtt[slot][:], in_=vtm[u, c]).then_inc(vtsem[slot], 16)
            vtc[slot] += 16

        for u in range(2):
            SY.wait(V); SY.wait(T_); SY.wait(S)
            SY.mark(nsy.dma_start(out=q_fm[:], in_=qf[:, u]), 16)
            SY.mark(nsy.dma_start(out=k_fm[:], in_=kf[:, u]), 16)
            SY.mark(nsy.dma_start(out=bcolu[:], in_=bcol[u]), 16)
            SY.mark(nsy.dma_start(out=rowB[:], in_=brw[u:u + 1, :]), 16)
            SY.mark(nsy.dma_start(out=rowA[:], in_=grw[u:u + 1, :]), 16)
            V.wait(SY); S.wait(SY); T_.wait(SY)
            V.fence()
            for fc in range(2):
                V.mark(nv.memset(S_sb[:, fc, :], 0.0))
                V.mark(nv.memset(S_bf[:, fc, :], 0.0))
            for tf in (q_fm, k_fm):
                for cc in range(8):
                    t0 = cc * 512
                    for fc in range(2):
                        S.wait(V)
                        if fc == 1:
                            S.wait(T_)
                        S.mark(S.e.activation(h.sq_bf[:], tf[:, fc, t0:t0 + 512], AF.Square))
                        T_.wait(S)
                        i1 = nt.matmul(h.pSum[0:1, :], h.onec[:, 0:1], h.sq_bf[:],
                                       start=(fc == 0), stop=(fc == 1))
                        T_.mark(i1)
                    S.wait(T_)
                    S.mark(S.e.activation(h.scrA[0:1, :], h.pSum[0:1, :], AF.Sqrt, bias=ck[0:1, 0:1]))
                    V.wait(S)
                    V.mark(nv.reciprocal(h.scrA[0:1, :], h.scrA[0:1, :]))
                    V.fence()
                    V.mark(nv.tensor_copy(h.y_bf[0:1, :], h.scrA[0:1, :]))
                    T_.wait(V)
                    T_.mark(nt.matmul(h.pBa[:], h.onerw[0:1, 0:128], h.y_bf[0:1, :], start=True, stop=True))
                    V.wait(T_)
                    V.fence()
                    for fc in range(2):
                        V.mark(nv.tensor_tensor(tf[:, fc, t0:t0 + 512], tf[:, fc, t0:t0 + 512],
                                                h.pBa[:], ALU.mult))
            kloadB(0, u, 0)
            vloadB(0, u, 0)
            for c in range(32):
                slot = c % 2
                if c + 1 < 32:
                    kloadB((c + 1) % 2, u, c + 1)
                    vloadB((c + 1) % 2, u, c + 1)
                t0 = c * 128
                S.e.wait_ge(ktsem[slot], ktc[slot])
                S.wait(V)
                S.mark(S.e.activation(sqk[:], ktt[slot][:], AF.Square))
                V.wait(S)
                V.fence()
                V.mark(nv.reduce_sum(nrm[:, 0:1], sqk[:], axis=AX.X))
                S.wait(V)
                S.mark(S.e.activation(nrm[:, 1:2], nrm[:, 0:1], AF.Sqrt, bias=ck[:, 0:1]))
                V.wait(S)
                V.fence()
                V.mark(nv.reciprocal(nrm[:, 1:2], nrm[:, 1:2]))
                V.fence()
                V.mark(nv.tensor_scalar(ktn[:], ktt[slot][:], nrm[:, 1:2], None, ALU.mult))
                ktcons[slot] = V.n
                V.e.wait_ge(vtsem[slot], vtc[slot])
                V.fence()
                V.mark(nv.tensor_scalar(vbn[:], vtt[slot][:], bcolu[:, c:c + 1], None, ALU.mult))
                vtcons[slot] = V.n
                T_.wait(V); T_.wait(S)
                for fc in range(2):
                    ig = nt.matmul(h.pX[:, 0:128], k_fm[:, fc, t0:t0 + 128], k_fm[:, fc, t0:t0 + 128],
                                   start=(fc == 0), stop=(fc == 1))
                T_.mark(ig)
                for fc in range(2):
                    ia = nt.matmul(h.pPB[:, 0:128], k_fm[:, fc, t0:t0 + 128], q_fm[:, fc, t0:t0 + 128],
                                   start=(fc == 0), stop=(fc == 1))
                T_.mark(ia)
                T_.mark(nt.matmul(h.pBa[:, 0:128], h.onerf[0:1, 0:128], rowB[0:1, t0:t0 + 128],
                                  start=True, stop=True))
                for fc in range(2):
                    iks = nt.matmul(h.pSq[:, 0:256], k_fm[:, fc, t0:t0 + 128], S_bf[:, fc, :],
                                    start=(fc == 0), stop=(fc == 1))
                T_.mark(iks)
                V.wait(T_)
                V.fence()
                V.mark(nv.tensor_copy(scrG[:], h.pX[:, 0:128]))
                V.fence()
                V.mark(nv.tensor_tensor(attnT[:], h.pPB[:, 0:128], mti[:], ALU.mult))
                V.mark(nv.tensor_scalar(h.scrB[:, 0:128], scrG[:], bcolu[:, c:c + 1], None, ALU.mult))
                V.fence()
                V.mark(nv.tensor_tensor(P_sb[:], h.scrB[:, 0:128], mtl[:], ALU.mult))
                V.mark(nv.tensor_tensor(h.scrA[:, 0:128], scrG[:], h.pBa[:, 0:128], ALU.mult))
                V.fence()
                V.mark(nv.tensor_tensor(Pt_sb[:], h.scrA[:, 0:128], mtu[:], ALU.mult))
                V.fence()
                V.mark(nv.tensor_tensor(M_sb[:], P_sb[:], idt[:], ALU.add))
                V.mark(nv.tensor_tensor(Mt_sb[:], Pt_sb[:], idt[:], ALU.add))
                V.mark(nv.tensor_scalar(h.scrA[:, 0:256], h.pSq[:, 0:256], bcolu[:, c:c + 1], None, ALU.mult))
                V.fence()
                V.mark(nv.tensor_tensor(vpr[:], vbn[:], h.scrA[:, 0:256], ALU.subtract))
                for j in range(6):
                    T_.wait(V)
                    i1 = nt.matmul(h.pPA[:, 0:128], Pt_sb[:], P_sb[:], start=True, stop=True)
                    i2 = nt.matmul(h.pPB[:, 0:128], P_sb[:], Pt_sb[:], start=True, stop=True)
                    T_.mark(i1); T_.mark(i2)
                    V.wait(T_)
                    V.fence()
                    V.mark(nv.tensor_copy(P_sb[:], h.pPA[:, 0:128]))
                    V.mark(nv.tensor_copy(Pt_sb[:], h.pPB[:, 0:128]))
                    T_.wait(V)
                    i3 = nt.matmul(h.pPA[:, 0:128], Mt_sb[:], P_sb[:], start=True, stop=True)
                    i4 = nt.matmul(h.pPB[:, 0:128], P_sb[:], Mt_sb[:], start=True, stop=True)
                    T_.mark(i3); T_.mark(i4)
                    V.wait(T_)
                    V.fence()
                    V.mark(nv.tensor_tensor(M_sb[:], M_sb[:], h.pPA[:, 0:128], ALU.add))
                    V.mark(nv.tensor_tensor(Mt_sb[:], Mt_sb[:], h.pPB[:, 0:128], ALU.add))
                T_.wait(V)
                T_.mark(nt.matmul(h.pSum[:, 0:256], Mt_sb[:], vpr[:], start=True, stop=True))
                V.wait(T_)
                V.fence()
                V.mark(nv.tensor_copy(uadj[:], h.pSum[:, 0:256]))
                T_.wait(V)
                for dc2 in range(2):
                    i1 = nt.matmul(h.pSq[:, 0:128], S_bf[:, 0, dc2 * 128:(dc2 + 1) * 128],
                                   q_fm[:, 0, t0:t0 + 128], start=True, stop=False)
                    i2 = nt.matmul(h.pSq[:, 0:128], S_bf[:, 1, dc2 * 128:(dc2 + 1) * 128],
                                   q_fm[:, 1, t0:t0 + 128], start=False, stop=False)
                    i3 = nt.matmul(h.pSq[:, 0:128], uadj[:, dc2 * 128:(dc2 + 1) * 128],
                                   attnT[:], start=False, stop=True)
                    T_.mark(i3)
                    V.wait(T_)
                    V.fence()
                    V.mark(nv.tensor_copy(o_sb[:, dc2, t0:t0 + 128], h.pSq[:, 0:128]))
                    T_.wait(V)
                for fcf in range(2):
                    ids = nt.matmul(h.P[5 + fcf][:, 0:256], ktn[:, fcf * 128:(fcf + 1) * 128],
                                    uadj[:], start=True, stop=True)
                    T_.mark(ids)
                V.wait(T_)
                V.fence()
                for fcf in range(2):
                    V.mark(nv.tensor_tensor(S_sb[:, fcf, :], S_sb[:, fcf, :],
                                            h.P[5 + fcf][:, 0:256], ALU.add))
                S.wait(V)
                for fcf in range(2):
                    S.mark(S.e.copy(S_bf[:, fcf, :], S_sb[:, fcf, :]))
                T_.wait(S)
            for cc in range(8):
                t0 = cc * 512
                T_.wait(V)
                T_.mark(nt.matmul(h.pBa[:], h.onerf[0:1, 0:128], rowA[0:1, t0:t0 + 512],
                                  start=True, stop=True))
                V.wait(T_)
                V.fence()
                V.mark(nv.tensor_copy(gb[:, t0:t0 + 512], h.pBa[:]))
            SY.wait(T_); SY.wait(S)
            SY.mark(nsy.dma_start(out=rowA[:], in_=mrw[u:u + 1, :]), 16)
            SY.mark(nsy.dma_start(out=nrm[0:1, 0:2].rearrange("p f -> p f"), in_=mbias2), 16)
            S.wait(SY); S.wait(V)
            S.mark(S.e.activation(rowA[:], rowA[:], AF.Sigmoid, bias=nrm[0:1, u:u + 1]))
            for fc in range(2):
                SY.wait(V); SY.wait(S)
                SY.mark(nsy.dma_start(out=vfb[:], in_=vf[:, u, fc, :]), 16)
                V.wait(SY)
                for cc in range(8):
                    t0 = cc * 512
                    V.fence()
                    V.mark(nv.tensor_tensor(h.scrA[:], gb[:, t0:t0 + 512], vfb[:, t0:t0 + 512], ALU.mult))
                    V.fence()
                    V.mark(nv.tensor_tensor(in1b[:, t0:t0 + 512], vfb[:, t0:t0 + 512], h.scrA[:], ALU.subtract))
                V.fence()
                V.mark(nv.tensor_tensor_scan(emab[:, :], gb[:, :], in1b[:, :], 0.0, ALU.mult, ALU.add))
                for cc in range(8):
                    t0 = cc * 512
                    T_.wait(V); T_.wait(S)
                    T_.mark(nt.matmul(h.pBb[:], h.onerf[0:1, 0:128], rowA[0:1, t0:t0 + 512],
                                      start=True, stop=True))
                    V.wait(T_); V.wait(S)
                    V.fence()
                    V.mark(nv.tensor_tensor(h.scrA[:], emab[:, t0:t0 + 512], o_sb[:, fc, t0:t0 + 512],
                                            ALU.subtract))
                    V.fence()
                    V.mark(nv.tensor_tensor(h.scrA[:], h.scrA[:], h.pBb[:], ALU.mult))
                    V.fence()
                    V.mark(nv.tensor_tensor(o_sb[:, fc, t0:t0 + 512], o_sb[:, fc, t0:t0 + 512],
                                            h.scrA[:], ALU.add))
            for cc in range(8):
                t0 = cc * 512
                for fc in range(2):
                    S.wait(V)
                    if fc == 1:
                        S.wait(T_)
                    S.mark(S.e.activation(h.sq_bf[:], o_sb[:, fc, t0:t0 + 512], AF.Square))
                    T_.wait(S)
                    i1 = nt.matmul(h.pSum[0:1, :], h.onec[:, 0:1], h.sq_bf[:],
                                   start=(fc == 0), stop=(fc == 1))
                    T_.mark(i1)
                V.wait(T_)
                V.fence()
                V.mark(nv.tensor_scalar(h.scrA[0:1, :], h.pSum[0:1, :], 1.0 / 256.0, None, ALU.mult))
                S.wait(V)
                S.mark(S.e.activation(h.scrA[0:1, :], h.scrA[0:1, :], AF.Sqrt, bias=ck[0:1, 1:2]))
                V.wait(S)
                V.fence()
                V.mark(nv.reciprocal(h.scrA[0:1, :], h.scrA[0:1, :]))
                V.fence()
                V.mark(nv.tensor_copy(h.y_bf[0:1, :], h.scrA[0:1, :]))
                T_.wait(V)
                T_.mark(nt.matmul(h.pBa[:], h.onerw[0:1, 0:128], h.y_bf[0:1, :], start=True, stop=True))
                V.wait(T_)
                for fc in range(2):
                    V.fence()
                    V.mark(nv.tensor_tensor(o_sb[:, fc, t0:t0 + 512], o_sb[:, fc, t0:t0 + 512],
                                            h.pBa[:], ALU.mult))
                    V.fence()
                    V.mark(nv.tensor_scalar(o_sb[:, fc, t0:t0 + 512], o_sb[:, fc, t0:t0 + 512],
                                            rmsw[:, fc:fc + 1], None, ALU.mult))
                    SY.wait(V)
                    SY.mark(nsy.dma_start(out=od[:, u, fc, t0:t0 + 512], in_=o_sb[:, fc, t0:t0 + 512]), 16)
        ng.wait_ge(SY.sem, SY.n)
        ng.wait_ge(G.sem, G.n)
        ng.wait_ge(ktsem[0], ktc[0]); ng.wait_ge(ktsem[1], ktc[1])
        ng.wait_ge(vtsem[0], vtc[0]); ng.wait_ge(vtsem[1], vtc[1])
    return nc


def host_l1_scan(ins, qs_sh, ks_sh, vs_sh, bgm_sh):
    BFd = BF
    def full_fm(shards):
        out = np.zeros((128, DC, 4, T), dtype=np.asarray(shards[0]).dtype)
        for c in range(NCORES):
            s = np.asarray(shards[c]).reshape(128, DC, 4, TPC)
            out[:, :, :, c * TPC:(c + 1) * TPC] = s
        return out
    qfull = full_fm(qs_sh); kfull = full_fm(ks_sh); vfull = full_fm(vs_sh)
    bg_full = np.zeros((12, 4, T), np.float32)
    for c in range(NCORES):
        bg_full[:, :, c * TPC:(c + 1) * TPC] = np.asarray(bgm_sh[c]).reshape(12, 4, TPC)
    mtl = -np.tril(np.ones((128, 128), np.float32), -1)
    mtu = -np.triu(np.ones((128, 128), np.float32), 1)
    mti = np.triu(np.ones((128, 128), np.float32), 0)
    idtf = np.eye(128, dtype=np.float32)
    cellk = np.zeros((128, 2), np.float32); cellk[:, 0] = 1e-6; cellk[:, 1] = 1e-5
    rmswf = np.ascontiguousarray(ins["L1_attn_rms_w"].astype(np.float32).reshape(2, 128).T)
    com = dict(mtrin=mtl, mtriun=mtu, mtui=mti, idtf=idtf, cellk=cellk, rmswf=rmswf,
               **std_host_consts())
    maps = []
    for c in range(NCORES):
        b = c // 2
        hs = [2 * (c % 2), 2 * (c % 2) + 1]
        qfu = np.stack([qfull[:, 2 * hh:2 * hh + 2, b, :] for hh in hs], 0)
        qfa = np.stack([qfull[:, 2 * hh:2 * hh + 2, b, :] for hh in hs], 1)
        kfa = np.stack([kfull[:, 2 * hh:2 * hh + 2, b, :] for hh in hs], 1)
        vfa = np.stack([vfull[:, 2 * hh:2 * hh + 2, b, :] for hh in hs], 1)
        def tmof(full, hh):
            f = full[:, 2 * hh:2 * hh + 2, b, :]
            ftm = f.transpose(2, 1, 0).reshape(T, 256)
            return ftm.reshape(32, 128, 256)
        ktma = np.stack([tmof(kfull, hh) for hh in hs], 0)
        vtma = np.stack([tmof(vfull, hh) for hh in hs], 0)
        bcola = np.stack([np.ascontiguousarray(
            bg_full[hh, b, :].reshape(32, 128).T) for hh in hs], 0)
        brwa = np.stack([bg_full[hh, b, :] for hh in hs], 0)
        grwa = np.stack([bg_full[4 + hh, b, :] for hh in hs], 0)
        mrwa = np.stack([bg_full[8 + hh, b, :] for hh in hs], 0)
        mb2 = ins["L1_attn_mix_bias"].astype(np.float32)[hs].reshape(1, 2)
        maps.append(dict(com,
                         qf=np.ascontiguousarray(qfa), kf=np.ascontiguousarray(kfa),
                         vf=np.ascontiguousarray(vfa),
                         ktm=np.ascontiguousarray(ktma), vtm=np.ascontiguousarray(vtma),
                         bcol=np.ascontiguousarray(bcola),
                         brw=np.ascontiguousarray(brwa), grw=np.ascontiguousarray(grwa),
                         mrw=np.ascontiguousarray(mrwa), mbias2=np.ascontiguousarray(mb2)))
    return maps


def run_l1_attn(ins, xs_fm_shards, runner):
    nc = bass.Bass("TRN2", target_bir_lowering=False)
    build_l1_proj(nc)
    maps = host_l1_proj(ins, xs_fm_shards)
    res = runner(nc, maps)
    qs_sh = [np.asarray(r["qs"]).reshape(128, DC, TOK) for r in res]
    ks_sh = [np.asarray(r["ks"]).reshape(128, DC, TOK) for r in res]
    vs_sh = [np.asarray(r["vs"]).reshape(128, DC, TOK) for r in res]
    bgm_sh = [np.asarray(r["bgm"]).reshape(12, TOK) for r in res]
    nc2 = bass.Bass("TRN2", target_bir_lowering=False)
    build_l1_scan(nc2)
    maps2 = host_l1_scan(ins, qs_sh, ks_sh, vs_sh, bgm_sh)
    res2 = runner(nc2, maps2)
    od_units = [np.asarray(r["od"]).reshape(128, 2, 2, 4096) for r in res2]
    osd = [np.zeros((128, DC, TOK), np.float32) for _ in range(NCORES)]
    for cs in range(NCORES):
        b = cs // 2
        hs = [2 * (cs % 2), 2 * (cs % 2) + 1]
        for ui, hh in enumerate(hs):
            for fc in range(2):
                col = od_units[cs][:, ui, fc, :]
                for cd in range(NCORES):
                    osd[cd][:, 2 * hh + fc, b * TPC:(b + 1) * TPC] = \
                        col[:, cd * TPC:(cd + 1) * TPC]
    nc3 = bass.Bass("TRN2", target_bir_lowering=False)
    build_proj_resid_ln(nc3)
    wp = np.ascontiguousarray(ins["L1_attn_Wo"].astype(np.float32).reshape(DC, 128, 1024))
    com3 = dict(wp=wp, wbias=np.zeros((128, DC), np.float32), **std_host_consts())
    maps3 = [dict(com3, xs=xs_fm_shards[c], osd=np.ascontiguousarray(osd[c]))
             for c in range(NCORES)]
    res3 = runner(nc3, maps3)
    return [np.ascontiguousarray(np.asarray(r["xo"], dtype=np.float32).reshape(128, DC, TOK))
            for r in res3]


def hw_runner(nc, maps):
    nc.finalize() if not nc.is_finalized else None
    res = run_bass_kernel_spmd(nc, maps, core_ids=list(range(NCORES)))
    return res.results


def kernel(**inputs):
    x = inputs["L0_attn_x"].astype(np.float32)
    shards = shard_tokens(x.reshape(B, T, D))
    xs = [to_fm(s) for s in shards]
    nc = bass.Bass("TRN2", target_bir_lowering=False)
    build_l0_attn(nc)
    maps = host_l0_attn(inputs, shards)
    res = run_bass_kernel_spmd(nc, maps, core_ids=list(range(NCORES)))
    xs = [np.ascontiguousarray(np.asarray(r["xo"]).reshape(128, DC, TOK)) for r in res.results]
    xs = run_launch(build_l0_ffn, host_l0_ffn(inputs), xs)
    xs = run_l1_attn(inputs, xs, hw_runner)
    xs = run_launch(build_l1_ffn, host_l1_ffn(inputs), xs)
    xs = run_l2_attn(inputs, xs, hw_runner)
    xs = run_launch(build_l2_ffn, host_l2_ffn(inputs), xs)
    xs = run_l3_attn(inputs, xs, hw_runner)
    xs = run_launch(build_l3_ffn, host_l3_ffn(inputs), xs)
    outs = [from_fm(s) for s in xs]
    return unshard_tokens(outs, D)


# version 02
# speedup vs baseline: 1.1659x; 1.1659x over previous
import contextlib
import numpy as np
import ml_dtypes
import concourse.bass as bass
import concourse.mybir as mybir
from concourse.bass_utils import run_bass_kernel_spmd

F32 = mybir.dt.float32
BF16 = mybir.dt.bfloat16
AF = mybir.ActivationFunctionType
ALU = mybir.AluOpType
AX = mybir.AxisListType
BF = ml_dtypes.bfloat16

NCORES = 8
B, T, D = 4, 4096, 1024
TPC = T // NCORES
TOK = B * TPC
DC = D // 128


class Eng:
    def __init__(self, nc, es, engine, name):
        self.e = engine
        self.sem = es.enter_context(nc.semaphore("sem_" + name))
        self.n = 0
    def mark(self, inst, k=1):
        inst.then_inc(self.sem, k)
        self.n += k
        return self.n
    def wait(self, producer, val=None):
        v = producer.n if val is None else val
        if v > 0:
            self.e.wait_ge(producer.sem, v)
    def fence(self):
        if self.n > 0:
            self.e.wait_ge(self.sem, self.n)

class PSem:
    def __init__(self, sem):
        self.sem = sem
        self.cnt = 0
        self.cons = 0

class Ctx:
    def __init__(self, nc, es):
        self.nc = nc
        self.es = es
        self.es_outer = es
        self.prefix = ""
        self._pool = {}
        self.SD = None
        self.T = Eng(nc, es, nc.tensor, "T")
        self.V = Eng(nc, es, nc.vector, "V")
        self.S = Eng(nc, es, nc.scalar, "S")
        self.SY = Eng(nc, es, nc.sync, "SY")
        self.G = Eng(nc, es, nc.gpsimd, "G")
        self.CC = Eng(nc, es, nc.gpsimd, "CC")
    def sb(self, name, shape, dtype):
        return self.es.enter_context(self.nc.sbuf_tensor(self.prefix + name, list(shape), dtype))
    def ps(self, name, shape=(128, 512), dtype=F32):
        return self.es.enter_context(self.nc.psum_tensor(self.prefix + name, list(shape), dtype))
    def pool(self, name):
        if name not in self._pool:
            self._pool[name] = PSem(self.es_outer.enter_context(self.nc.semaphore(name)))
        return self._pool[name]
    def sd_eng(self):
        if self.SD is None:
            self.SD = Eng(self.nc, self.es_outer, self.nc.scalar, "SD")
        return self.SD
    def quiesce(self):
        prods = [self.T, self.V, self.S, self.SY, self.G, self.CC]
        if self.SD is not None:
            prods.append(self.SD)
        for e in (self.T, self.V, self.S, self.SY, self.G):
            for p in prods:
                e.wait(p)
            for ps in self._pool.values():
                if ps.cnt > 0:
                    e.e.wait_ge(ps.sem, ps.cnt)

def elu1_evac(ctx, ps_ap, out_ap, scrA, scrB):
    V, S = ctx.V, ctx.S
    V.mark(V.e.tensor_single_scalar(scrA[:], ps_ap, 0.0, ALU.min))
    S.wait(V)
    S.mark(S.e.activation(scrB[:], scrA[:], AF.Exp))
    V.wait(S)
    return V.mark(V.e.scalar_tensor_tensor(out_ap, ps_ap, 0.0, scrB[:], ALU.max, ALU.add))


def to_fm(x_tok_d):
    t, d = x_tok_d.shape
    return np.ascontiguousarray(x_tok_d.reshape(t, d // 128, 128).transpose(2, 1, 0))

def from_fm(x_fm):
    p, dch, t = x_fm.shape
    return np.ascontiguousarray(np.asarray(x_fm).transpose(2, 1, 0).reshape(t, dch * 128))

def shard_tokens(x_btd):
    d = x_btd.shape[2]
    return [np.ascontiguousarray(x_btd[:, c * TPC:(c + 1) * TPC, :].reshape(TOK, d))
            for c in range(NCORES)]

def unshard_tokens(shards, d=None):
    d = d if d is not None else shards[0].shape[-1]
    x = np.zeros((B, T, d), np.float32)
    for c, s in enumerate(shards):
        x[:, c * TPC:(c + 1) * TPC, :] = np.asarray(s, dtype=np.float32).reshape(B, TPC, d)
    return x


def build_l0_attn(nc, stage=99):
    H = 16
    xs = nc.dram_tensor("xs", [128, DC, TOK], F32, kind="ExternalInput").ap()
    wqkvg = nc.dram_tensor("wqkvg", [DC, 128, 4096], F32, kind="ExternalInput").ap()
    wout = nc.dram_tensor("wout", [DC, 128, 1024], F32, kind="ExternalInput").ap()
    ctri = nc.dram_tensor("ctri", [128, 128], F32, kind="ExternalInput").ap()
    cones = nc.dram_tensor("cones", [128, 2], F32, kind="ExternalInput").ap()
    conesr = nc.dram_tensor("conesr", [2, 512], F32, kind="ExternalInput").ap()
    pmask = nc.dram_tensor("pmask", [128, 8], F32, kind="ExternalInput").ap()
    cflt = nc.dram_tensor("cflt", [128, 4], F32, kind="ExternalInput").ap()
    xo = nc.dram_tensor("xo", [128, DC, TOK], F32, kind="ExternalOutput").ap()
    ag_in = nc.dram_tensor("ag_in", [128, 32 * 65], BF16)
    ag_out = nc.dram_tensor("ag_out", [8 * 128, 32 * 65], BF16)
    ybuf = nc.dram_tensor("ybuf", [128, DC, TOK], F32)

    io = dict(xs=xs, wqkvg=wqkvg, wout=wout, ctri=ctri, cones=cones, conesr=conesr, pmask=pmask, cflt=cflt, xo=xo, ag_in=ag_in, ag_out=ag_out, ybuf=ybuf)
    with contextlib.ExitStack() as es0:
        ctx = Ctx(nc, es0)
        es0.enter_context(nc.Block())
        body_l0_attn(nc, ctx, io)
    return nc

def body_l0_attn(nc, ctx, io, stage=99):
    H = 16
    xs = io["xs"]; wqkvg = io["wqkvg"]; wout = io["wout"]; ctri = io["ctri"]; cones = io["cones"]; conesr = io["conesr"]; pmask = io["pmask"]; cflt = io["cflt"]; xo = io["xo"]; ag_in = io["ag_in"]; ag_out = io["ag_out"]; ybuf = io["ybuf"]
    with contextlib.ExitStack() as es:
        old_es, ctx.es = ctx.es, es
        T_, V, S, SY, G, CC = ctx.T, ctx.V, ctx.S, ctx.SY, ctx.G, ctx.CC
        sb, ps = ctx.sb, ctx.ps

        x_bf = sb("x_bf", (128, DC, TOK), BF16)
        W = sb("W", (128, DC, 4096), BF16)
        WO = sb("WO", (128, DC, 1024), BF16)
        u0 = sb("u0", (128, 8, 512), BF16)
        vtm = sb("vtm", (128, 4, 1024), BF16)
        kfm = sb("kfm", (128, 8, 512), BF16)
        gat = sb("gat", (128, 8, 512), BF16)
        oall = sb("oall", (128, 8, 512), BF16)
        stA = sb("stA", (128, 32, 65), BF16)
        stR = sb("stR", (128, 16 * 65), F32)
        stRB = sb("stRB", (128, 16 * 65), BF16)
        st_sb = [sb(f"st{i}", (128, 512), BF16) for i in range(4)]
        scrA = sb("scrA", (128, 512), F32)
        scrB = sb("scrB", (128, 512), F32)
        sq_bf = sb("sq_bf", (128, 512), BF16)
        y_bf = sb("y_bf", (128, 512), BF16)
        xchunk = [sb(f"xchunk{i}", (128, 512), F32) for i in range(2)]
        ychunk = [sb(f"ychunk{i}", (128, 512), F32) for i in range(2)]
        dden = sb("dden", (1, 512), F32)
        tri = sb("tri", (128, 128), BF16)
        onec = sb("onec", (128, 2), BF16)
        onerw = sb("onerw", (2, 512), BF16)
        onerf = sb("onerf", (2, 512), F32)
        pmk = sb("pmk", (128, 8), F32)
        cfl = sb("cfl", (128, 4), F32)
        arow = sb("arow", (1, 4, 512), F32)
        brow = sb("brow", (1, 4, 512), F32)

        P = [ps(f"P{i}") for i in range(8)]
        pPA, pPB = P[0], P[1]
        pST = P[2]
        pO = P[3]
        pDen = P[4]
        pBc = P[5]
        pSt = P[6]
        pSt2 = P[7]
        pSum, pSq = pST, pO

        nv = nc.vector; nt = nc.tensor; nsy = nc.sync; ng = nc.gpsimd

        G.mark(ng.dma_start(out=x_bf[:], in_=xs), 16)
        G.mark(ng.dma_start(out=W[:], in_=wqkvg.rearrange("c p f -> p c f")), 16)
        G.mark(ng.dma_start(out=WO[:], in_=wout.rearrange("c p f -> p c f")), 16)
        G.mark(ng.dma_start(out=tri[:], in_=ctri), 16)
        G.mark(ng.dma_start(out=onec[:], in_=cones), 16)
        G.mark(ng.dma_start(out=onerw[:], in_=conesr), 16)
        SY.mark(nsy.dma_start(out=onerf[:], in_=conesr), 16)
        SY.mark(nsy.dma_start(out=pmk[:], in_=pmask), 16)
        SY.mark(nsy.dma_start(out=cfl[:], in_=cflt), 16)
        for i in range(4):
            V.mark(nv.memset(st_sb[i][:], 0.0))
        T_.wait(G)
        T_.e.wait_ge(SY.sem, 48)
        V.wait(G)
        S.wait(SY)
        S.wait(G)

        for b in range(4):
            t0 = b * 512
            for ti in range(4):
                tok0 = t0 + ti * 128
                for of in range(2):
                    for half in range(2):
                        base = 1024 + half * 1024 + of * 512
                        dst_ps = pPA if (ti + of + half) % 2 == 0 else pPB
                        for dc in range(DC):
                            inst = nt.matmul(dst_ps[:], x_bf[:, dc, tok0:tok0 + 128],
                                             W[:, dc, base:base + 512],
                                             start=(dc == 0), stop=(dc == DC - 1))
                        T_.mark(inst)
                        V.wait(T_)
                        if half == 0:
                            elu1_evac(ctx, dst_ps[:], u0[:, ti * 2 + of, :], scrA, scrB)
                        else:
                            S.wait(T_)
                            S.mark(S.e.copy(vtm[:, ti, of * 512:(of + 1) * 512], dst_ps[:]))
                        T_.wait(V); T_.wait(S)
            for h in range(H):
                u2 = b * 8 + h // 2
                hp = (h % 2) * 64
                for kti in range(4):
                    ksl = u0[:, 2 * kti + h // 8, (h % 8) * 64:(h % 8) * 64 + 64]
                    i1 = nt.matmul(pSt[hp:hp + 64, 0:64], ksl,
                                   vtm[:, kti, h * 64:(h + 1) * 64],
                                   start=(kti == 0), stop=(kti == 3))
                    i2 = nt.matmul(pSt2[hp:hp + 64, 0:1], ksl,
                                   onec[:, 0:1], start=(kti == 0), stop=(kti == 3))
                T_.mark(i1); T_.mark(i2)
                V.wait(T_)
                V.mark(nv.tensor_copy(stA[hp:hp + 64, u2, 0:64], pSt[hp:hp + 64, 0:64]))
                V.mark(nv.tensor_copy(stA[hp:hp + 64, u2, 64:65], pSt2[hp:hp + 64, 0:1]))
                T_.wait(V)
        if stage <= 0:
            ng.wait_ge(SY.sem, SY.n); ng.wait_ge(G.sem, G.n)
            return nc
        G.wait(V)
        G.mark(ng.dma_start(out=ag_in.ap(), in_=stA[:].rearrange("p u c -> p (u c)")), 16)
        G.wait(G)
        CC.mark(ng.collective_compute(
            "AllGather", ALU.bypass, replica_groups=[list(range(NCORES))],
            ins=[ag_in.ap().opt()], outs=[ag_out.ap().opt()]))
        G.wait(CC)
        V.wait(SY)
        HALF = 16 * 65
        for hh in range(2):
            V.fence()
            V.mark(nv.memset(stR[:], 0.0))
            for r in range(NCORES):
                G.e.wait_ge(V.sem, V.n)
                G.mark(ng.dma_start(out=stRB[:],
                                    in_=ag_out.ap()[r * 128:(r + 1) * 128, hh * HALF:(hh + 1) * HALF]), 16)
                V.wait(G)
                V.fence()
                V.mark(nv.scalar_tensor_tensor(stR[:], stRB[:], pmk[:, r:r + 1], stR[:],
                                               ALU.mult, ALU.add))
            V.fence()
            V.mark(nv.tensor_copy(
                stA[:].rearrange("p u c -> p (u c)")[:, hh * HALF:(hh + 1) * HALF], stR[:]))

        if stage <= 1:
            ng.wait_ge(SY.sem, SY.n); ng.wait_ge(G.sem, G.n)
            return nc
        for b in range(4):
            t0 = b * 512
            for oc in range(8):
                for tgt, cbase, dst, kind in ((0, 0, u0, 'elu'), (1, 1024, kfm, 'elu'),
                                              (2, 3072, gat, 'sig')):
                    dst_ps = pPA if (oc + tgt) % 2 == 0 else pPB
                    for dc in range(DC):
                        inst = nt.matmul(dst_ps[:],
                                         W[:, dc, cbase + oc * 128:cbase + (oc + 1) * 128],
                                         x_bf[:, dc, t0:t0 + 512],
                                         start=(dc == 0), stop=(dc == DC - 1))
                    T_.mark(inst)
                    dsl = dst[:, oc, :]
                    V.wait(T_)
                    if kind == 'elu':
                        elu1_evac(ctx, dst_ps[:], dsl, scrA, scrB)
                    else:
                        S.wait(T_)
                        S.mark(S.e.activation(dsl, dst_ps[:], AF.Sigmoid))
                    T_.wait(V); T_.wait(S)
            for ti in range(4):
                tok0 = t0 + ti * 128
                for of in range(2):
                    base = 2048 + of * 512
                    dst_ps = pPA if (ti + of) % 2 == 0 else pPB
                    for dc in range(DC):
                        inst = nt.matmul(dst_ps[:], x_bf[:, dc, tok0:tok0 + 128],
                                         W[:, dc, base:base + 512],
                                         start=(dc == 0), stop=(dc == DC - 1))
                    T_.mark(inst)
                    S.wait(T_)
                    S.mark(S.e.copy(vtm[:, ti, of * 512:(of + 1) * 512], dst_ps[:]))
                    T_.wait(S)
            if stage <= 2:
                continue
            for h in range(H):
                u2 = b * 8 + h // 2
                hp = (h % 2) * 64
                hc = h // 2
                q_sl = u0[hp:hp + 64, hc, :]
                for kt in range(4):
                    ksl = kfm[hp:hp + 64, hc, kt * 128:(kt + 1) * 128]
                    T_.mark(nt.matmul(pST[:], ksl, q_sl, start=True, stop=True))
                    S.wait(T_)
                    S.mark(S.e.copy(st_sb[kt][:, kt * 128:512], pST[:, kt * 128:512]))
                    V.wait(S)
                    V.mark(nv.tensor_tensor(st_sb[kt][:, kt * 128:(kt + 1) * 128],
                                            st_sb[kt][:, kt * 128:(kt + 1) * 128], tri[:], ALU.mult))
                    T_.wait(V); T_.wait(S)
                    nt.matmul(pO[hp:hp + 64, :], vtm[:, kt, h * 64:(h + 1) * 64],
                              st_sb[kt][:], start=(kt == 0), stop=False)
                    nt.matmul(pDen[0:1, :], onec[:, 0:1], st_sb[kt][:],
                              start=(kt == 0), stop=False)
                i1 = nt.matmul(pO[hp:hp + 64, :], stA[hp:hp + 64, u2, 0:64], q_sl,
                               start=False, stop=True)
                T_.mark(i1)
                nt.matmul(pDen[0:1, :], stA[hp:hp + 64, u2, 64:65], q_sl, start=False, stop=False)
                i2 = nt.matmul(pDen[0:1, :], onec[0:1, 1:2], onerw[0:1, :], start=False, stop=True)
                T_.mark(i2)
                V.wait(T_)
                V.mark(nv.reciprocal(dden[:], pDen[0:1, :]))
                T_.wait(V)
                T_.mark(nt.matmul(pBc[hp:hp + 64, :], onerf[0:1, 0:64], dden[:], start=True, stop=True))
                V.wait(T_)
                V.mark(nv.tensor_tensor(scrA[hp:hp + 64, :], pO[hp:hp + 64, :],
                                        gat[hp:hp + 64, hc, :], ALU.mult))
                V.fence()
                V.mark(nv.tensor_tensor(oall[hp:hp + 64, hc, :],
                                        scrA[hp:hp + 64, :], pBc[hp:hp + 64, :], ALU.mult))
                T_.wait(V)
            if stage <= 3:
                continue
            for oc in range(DC):
                dst_ps = pPA if oc % 2 == 0 else pPB
                for fc in range(DC):
                    inst = nt.matmul(dst_ps[:], WO[:, fc, oc * 128:(oc + 1) * 128],
                                     oall[:, fc, :], start=(fc == 0), stop=(fc == DC - 1))
                T_.mark(inst)
                xb = xchunk[oc % 2]
                SY.wait(V)
                SY.mark(nsy.dma_start(out=xb[:], in_=xs[:, oc, t0:t0 + 512]), 16)
                V.wait(T_); V.wait(SY); V.wait(S)
                yb = ychunk[oc % 2]
                V.mark(nv.tensor_tensor(yb[:], dst_ps[:], xb[:], ALU.add))
                T_.wait(V)
                S.wait(V)
                S.mark(S.e.copy(y_bf[:], yb[:]))
                S.mark(S.e.activation(sq_bf[:], yb[:], AF.Square))
                SY.wait(V)
                SY.mark(nsy.dma_start(out=ybuf.ap()[:, oc, t0:t0 + 512], in_=yb[:]), 16)
                T_.wait(S)
                i1 = nt.matmul(pSum[0:1, :], onec[:, 0:1], y_bf[:],
                               start=(oc == 0), stop=(oc == DC - 1))
                i2 = nt.matmul(pSq[0:1, :], onec[:, 0:1], sq_bf[:],
                               start=(oc == 0), stop=(oc == DC - 1))
                T_.mark(i1); T_.mark(i2)
                S.wait(T_)
            V.wait(T_)
            V.fence()
            V.mark(nv.tensor_scalar(dden[:], pSum[0:1, :], 1.0 / D, None, ALU.mult))
            V.mark(nv.tensor_scalar(scrA[0:1, :], pSq[0:1, :], 1.0 / D, None, ALU.mult))
            V.fence()
            V.mark(nv.tensor_tensor(scrB[0:1, :], dden[:], dden[:], ALU.mult))
            V.fence()
            V.mark(nv.tensor_tensor(scrB[0:1, :], scrA[0:1, :], scrB[0:1, :], ALU.subtract))
            S.wait(V)
            S.mark(S.e.activation(scrA[0:1, :], scrB[0:1, :], AF.Sqrt, bias=cfl[0:1, 0:1]))
            V.wait(S)
            V.mark(nv.reciprocal(arow[:, b, :], scrA[0:1, :]))
            V.fence()
            V.mark(nv.tensor_tensor(scrB[0:1, :], dden[:], arow[:, b, :], ALU.mult))
            V.fence()
            V.mark(nv.tensor_scalar(brow[:, b, :], scrB[0:1, :], -1.0, None, ALU.mult))
            T_.wait(V)
        if stage <= 4:
            ng.wait_ge(SY.sem, SY.n); ng.wait_ge(G.sem, G.n)
            return nc
        for trow in range(4):
            tok0 = trow * 512
            T_.wait(V)
            ia = nt.matmul(pDen[:], onerf[0:1, 0:128], arow[:, trow, :], start=True, stop=True)
            ib = nt.matmul(pBc[:], onerf[0:1, 0:128], brow[:, trow, :], start=True, stop=True)
            T_.mark(ia); T_.mark(ib)
            for oc in range(DC):
                yb = ychunk[oc % 2]
                ob = xchunk[oc % 2]
                SY.wait(V)
                SY.mark(nsy.dma_start(out=yb[:], in_=ybuf.ap()[:, oc, tok0:tok0 + 512]), 16)
                V.wait(SY); V.wait(T_)
                V.mark(nv.tensor_tensor(ob[:], yb[:], pDen[:], ALU.mult))
                V.fence()
                V.mark(nv.tensor_tensor(ob[:], ob[:], pBc[:], ALU.add))
                SY.wait(V)
                SY.mark(nsy.dma_start(out=xo[:, oc, tok0:tok0 + 512], in_=ob[:]), 16)
            T_.wait(V)
        ng.wait_ge(SY.sem, SY.n)
        ng.wait_ge(G.sem, G.n)

        ctx.es = old_es
def host_l0_attn(ins, x_shards):
    wq = ins["L0_attn_w_qkv"].astype(np.float32)
    wg = ins["L0_attn_w_gate"].astype(np.float32)
    wo = ins["L0_attn_w_out"].astype(np.float32)
    wqkvg = np.ascontiguousarray(np.concatenate([wq, wg], axis=1).reshape(DC, 128, 4096))
    wof = np.ascontiguousarray(wo.reshape(DC, 128, 1024))
    tri = np.ascontiguousarray(np.tril(np.ones((128, 128), np.float32)).T)
    cones = np.zeros((128, 2), np.float32); cones[:, 0] = 1.0; cones[0, 1] = 1e-6
    conesr = np.zeros((2, 512), np.float32); conesr[0, :] = 1.0
    maps = []
    for c in range(NCORES):
        pm = np.zeros((128, 8), np.float32)
        pm[:, :c] = 1.0
        cf = np.zeros((128, 4), np.float32); cf[:, 0] = 1e-5
        maps.append(dict(xs=to_fm(x_shards[c]), wqkvg=wqkvg, wout=wof, ctri=tri,
                         cones=cones, conesr=conesr, pmask=pm, cflt=cf))
    return maps




def build_l0_ffn(nc):
    xs = nc.dram_tensor("xs", [128, DC, TOK], F32, kind="ExternalInput").ap()
    wqkv = nc.dram_tensor("wqkv", [DC, 128, 3072], F32, kind="ExternalInput").ap()
    wproj = nc.dram_tensor("wproj", [DC, 128, 1024], F32, kind="ExternalInput").ap()
    wrout = nc.dram_tensor("wrout", [128, DC * 8], F32, kind="ExternalInput").ap()
    w1 = nc.dram_tensor("w1", [8, DC, 128, 2048], F32, kind="ExternalInput").ap()
    w2 = nc.dram_tensor("w2", [8, 16, 128, 1024], F32, kind="ExternalInput").ap()
    lnw = nc.dram_tensor("lnw", [128, 4 * DC], F32, kind="ExternalInput").ap()
    cmh = nc.dram_tensor("cmh", [128, DC * 16], F32, kind="ExternalInput").ap()
    cmhT = nc.dram_tensor("cmhT", [16, DC * 128], F32, kind="ExternalInput").ap()
    cidt = nc.dram_tensor("cidt", [128, 128], F32, kind="ExternalInput").ap()
    conesr = nc.dram_tensor("conesr", [2, 512], F32, kind="ExternalInput").ap()
    cones = nc.dram_tensor("cones", [128, 2], F32, kind="ExternalInput").ap()
    cflt = nc.dram_tensor("cflt", [128, 4], F32, kind="ExternalInput").ap()
    csel = nc.dram_tensor("csel", [8, 8 * 128], F32, kind="ExternalInput").ap()
    xo = nc.dram_tensor("xo", [128, DC, TOK], F32, kind="ExternalOutput").ap()
    ybuf = nc.dram_tensor("ybuf", [128, DC, TOK], F32)

    io = dict(xs=xs, wqkv=wqkv, wproj=wproj, wrout=wrout, w1=w1, w2=w2, lnw=lnw, cmh=cmh, cmhT=cmhT, cidt=cidt, conesr=conesr, cones=cones, cflt=cflt, csel=csel, xo=xo, ybuf=ybuf)
    with contextlib.ExitStack() as es0:
        ctx = Ctx(nc, es0)
        es0.enter_context(nc.Block())
        body_l0_ffn(nc, ctx, io)
    return nc

def body_l0_ffn(nc, ctx, io):
    xs = io["xs"]; wqkv = io["wqkv"]; wproj = io["wproj"]; wrout = io["wrout"]; w1 = io["w1"]; w2 = io["w2"]; lnw = io["lnw"]; cmh = io["cmh"]; cmhT = io["cmhT"]; cidt = io["cidt"]; conesr = io["conesr"]; cones = io["cones"]; cflt = io["cflt"]; csel = io["csel"]; xo = io["xo"]; ybuf = io["ybuf"]
    with contextlib.ExitStack() as es:
        old_es, ctx.es = ctx.es, es
        T_, V, S, SY, G, CC = ctx.T, ctx.V, ctx.S, ctx.SY, ctx.G, ctx.CC
        sb, ps = ctx.sb, ctx.ps
        nv = nc.vector; nt = nc.tensor; nsy = nc.sync; ng = nc.gpsimd

        A = sb("A", (128, 16384), BF16)
        Bl = sb("Bl", (128, 16384), BF16)
        Q = sb("Q", (128, 16384), BF16)
        K = sb("K", (128, 16384), BF16)
        moe = sb("moe", (128, 8, 512), F32)
        wsl = sb("wsl", (128, 2, 2048), BF16)
        wrt = sb("wrt", (128, DC, 8), BF16)
        scrA = sb("scrA", (128, 512), F32)
        scrB = sb("scrB", (128, 512), F32)
        sq_bf = sb("sq_bf", (128, 512), BF16)
        y_bf = sb("y_bf", (128, 512), BF16)
        _xc = sb("xchunk0", (128, 512), F32)
        xchunk = [_xc, _xc]
        ychunk = [sb(f"ychunk{i}", (128, 512), F32) for i in range(2)]
        wbc = sb("wbc", (128, 512), BF16)
        sc_sb = sb("sc_sb", (16, 4, 512), F32)
        prb = sb("prb", (16, 4, 512), BF16)
        tm8 = sb("tm8", (128, 8), F32)
        tm8b = sb("tm8b", (128, 8), F32)
        tm8m = sb("tm8m", (128, 1), F32)
        tm8c = sb("tm8c", (128, 8), BF16)
        wrows = sb("wrows", (8, 2048), BF16)
        mh = sb("mh", (128, DC, 16), BF16)
        mhT = sb("mhT", (16, DC, 128), BF16)
        idt = sb("idt", (128, 128), BF16)
        sel = sb("sel", (8, 8, 128), BF16)
        onerw = sb("onerw", (2, 512), BF16)
        onerf = sb("onerf", (2, 512), F32)
        onec = sb("onec", (128, 2), BF16)
        cfl = sb("cfl", (128, 4), F32)
        lnww = sb("lnww", (128, 4, DC), F32)
        arow = sb("arow", (1, 4, 512), F32)
        brow = sb("brow", (1, 4, 512), F32)

        P = [ps(f"P{i}") for i in range(7)]
        P7b = ps("P7b", (128, 1024), BF16)
        pPA, pPB = P[0], P[1]
        pSum, pSq = P[2], P[3]
        pBa, pBb = P[4], P[5]
        pX = P[6]


        G.mark(ng.dma_start(out=mh[:].rearrange("p c f -> p (c f)"), in_=cmh), 16)
        G.mark(ng.dma_start(out=mhT[:].rearrange("p c f -> p (c f)"), in_=cmhT), 16)
        G.mark(ng.dma_start(out=idt[:], in_=cidt), 16)
        G.mark(ng.dma_start(out=sel[:].rearrange("p c f -> p (c f)"), in_=csel), 16)
        G.mark(ng.dma_start(out=onerw[:], in_=conesr), 16)
        G.mark(ng.dma_start(out=onec[:], in_=cones), 16)
        G.mark(ng.dma_start(out=wrt[:].rearrange("p c f -> p (c f)"), in_=wrout), 16)
        SD = ctx.sd_eng()
        SD.mark(nc.scalar.dma_start(out=onerf[:], in_=conesr), 16)
        SD.mark(nc.scalar.dma_start(out=cfl[:], in_=cflt), 16)
        SD.mark(nc.scalar.dma_start(out=lnww[:].rearrange("p c f -> p (c f)"), in_=lnw), 16)
        T_.wait(G); V.wait(G); S.wait(G)
        T_.wait(SD)
        V.wait(SD); S.wait(SD)
        G.fence()

        wps = [ctx.pool("wsl0"), ctx.pool("wsl1")]

        def wsl_load(slot, src_ap, n):
            G.e.wait_ge(T_.sem, wps[slot].cons)
            c = src_ap.shape[0]
            dst = wsl[:, slot, 0:n].rearrange("p (c f) -> p c f", c=c)
            ng.dma_start(out=dst, in_=src_ap.rearrange("c p f -> p c f")).then_inc(wps[slot].sem, 16)
            wps[slot].cnt += 16

        def wsl_wait(slot):
            T_.e.wait_ge(wps[slot].sem, wps[slot].cnt)

        def wsl_done(slot):
            wps[slot].cons = T_.n

        def stats_finish(b):
            V.wait(T_)
            V.wait(SY)
            V.fence()
            V.mark(nv.tensor_scalar(ychunk[1][0:1, :], pSum[0:1, :], 1.0 / D, None, ALU.mult))
            V.mark(nv.tensor_scalar(scrA[0:1, :], pSq[0:1, :], 1.0 / D, None, ALU.mult))
            V.fence()
            V.mark(nv.tensor_tensor(scrB[0:1, :], ychunk[1][0:1, :], ychunk[1][0:1, :], ALU.mult))
            V.fence()
            V.mark(nv.tensor_tensor(scrB[0:1, :], scrA[0:1, :], scrB[0:1, :], ALU.subtract))
            S.wait(V)
            S.mark(S.e.activation(scrA[0:1, :], scrB[0:1, :], AF.Sqrt, bias=cfl[0:1, 0:1]))
            V.wait(S)
            V.mark(nv.reciprocal(arow[:, b, :], scrA[0:1, :]))
            V.fence()
            V.mark(nv.tensor_tensor(scrB[0:1, :], ychunk[1][0:1, :], arow[:, b, :], ALU.mult))
            V.fence()
            V.mark(nv.tensor_scalar(brow[:, b, :], scrB[0:1, :], -1.0, None, ALU.mult))
            T_.wait(V)

        def bcast_rows(tr):
            T_.wait(V)
            ia = nt.matmul(pBa[:], onerf[0:1, 0:128], arow[:, tr, :], start=True, stop=True)
            ib = nt.matmul(pBb[:], onerf[0:1, 0:128], brow[:, tr, :], start=True, stop=True)
            T_.mark(ia)
            T_.mark(ib)

        for b in range(4):
            t0 = b * 512
            for oc in range(DC):
                xb = xchunk[oc % 2]
                SY.wait(S)
                SY.wait(V)
                SY.mark(nsy.dma_start(out=xb[:], in_=xs[:, oc, t0:t0 + 512]), 16)
                S.wait(SY)
                S.wait(T_)
                S.mark(S.e.copy(y_bf[:], xb[:]))
                S.mark(S.e.activation(sq_bf[:], xb[:], AF.Square))
                T_.wait(S)
                i1 = nt.matmul(pSum[0:1, :], onec[:, 0:1], y_bf[:], start=(oc == 0), stop=(oc == DC - 1))
                i2 = nt.matmul(pSq[0:1, :], onec[:, 0:1], sq_bf[:], start=(oc == 0), stop=(oc == DC - 1))
                T_.mark(i1)
                T_.mark(i2)
            stats_finish(b)
            bcast_rows(b)
            for oc in range(DC):
                xb = xchunk[oc % 2]
                SY.wait(V)
                SY.wait(S)
                SY.mark(nsy.dma_start(out=xb[:], in_=xs[:, oc, t0:t0 + 512]), 16)
                V.wait(SY)
                V.wait(T_)
                V.fence()
                V.mark(nv.tensor_tensor(scrA[:], xb[:], pBa[:], ALU.mult))
                V.fence()
                V.mark(nv.tensor_tensor(scrA[:], scrA[:], pBb[:], ALU.add))
                V.fence()
                V.mark(nv.tensor_scalar(Bl[:, oc * 2048 + t0: oc * 2048 + t0 + 512], scrA[:],
                                        lnww[:, 0, oc:oc + 1], lnww[:, 1, oc:oc + 1],
                                        ALU.mult, ALU.add))
            T_.wait(V)
        wsl_load(0, wqkv[:, :, 0:128], 1024)
        for oc24 in range(24):
            tgt = Q if oc24 < 8 else (K if oc24 < 16 else A)
            ocl = oc24 % 8
            slot = oc24 % 2
            if oc24 + 1 < 24:
                wsl_load((oc24 + 1) % 2, wqkv[:, :, (oc24 + 1) * 128:(oc24 + 2) * 128], 1024)
            wsl_wait(slot)
            for b in range(4):
                t0 = b * 512
                dst_ps = pPA if (oc24 + b) % 2 == 0 else pPB
                for dc in range(DC):
                    inst = nt.matmul(dst_ps[:], wsl[:, slot, dc * 128:(dc + 1) * 128],
                                     Bl[:, dc * 2048 + t0: dc * 2048 + t0 + 512],
                                     start=(dc == 0), stop=(dc == DC - 1))
                T_.mark(inst)
                S.wait(T_)
                S.mark(S.e.copy(tgt[:, ocl * 2048 + t0: ocl * 2048 + t0 + 512], dst_ps[:]))
                T_.wait(S)
            wsl_done(slot)
        for s in range(4):
            for t in range(s + 1):
                for hc in range(DC):
                    V.wait(T_)
                    V.fence()
                    V.mark(nv.tensor_tensor(sq_bf[:], Q[:, hc * 2048 + s * 512: hc * 2048 + (s + 1) * 512],
                                            K[:, hc * 2048 + t * 512: hc * 2048 + (t + 1) * 512], ALU.mult))
                    T_.wait(V)
                    T_.mark(nt.matmul(P[2 + t][0:16, :], mh[:, hc, :], sq_bf[:],
                                      start=(hc == 0), stop=(hc == DC - 1)))
                V.wait(T_)
                V.fence()
                V.mark(nv.tensor_scalar(sc_sb[:, t, :], P[2 + t][0:16, :], 0.125, None, ALU.mult))
            V.fence()
            V.mark(nv.tensor_copy(scrA[0:16, :], sc_sb[:, 0, :]))
            for t in range(1, s + 1):
                V.fence()
                V.mark(nv.tensor_tensor(scrA[0:16, :], scrA[0:16, :], sc_sb[:, t, :], ALU.max))
            for t in range(s + 1):
                V.fence()
                V.mark(nv.tensor_tensor(sc_sb[:, t, :], sc_sb[:, t, :], scrA[0:16, :], ALU.subtract))
            S.wait(V)
            for t in range(s + 1):
                S.mark(S.e.activation(sc_sb[:, t, :], sc_sb[:, t, :], AF.Exp))
            V.wait(S)
            V.mark(nv.tensor_copy(scrB[0:16, :], sc_sb[:, 0, :]))
            for t in range(1, s + 1):
                V.fence()
                V.mark(nv.tensor_tensor(scrB[0:16, :], scrB[0:16, :], sc_sb[:, t, :], ALU.add))
            V.fence()
            V.mark(nv.reciprocal(scrB[0:16, :], scrB[0:16, :]))
            for t in range(s + 1):
                V.fence()
                V.mark(nv.tensor_tensor(prb[:, t, :], sc_sb[:, t, :], scrB[0:16, :], ALU.mult))
            for hc in range(DC):
                for t in range(s + 1):
                    T_.wait(V)
                    T_.mark(nt.matmul(pX[:], mhT[:, hc, :], prb[:, t, :], start=True, stop=True))
                    V.wait(T_)
                    dsl = Bl[:, hc * 2048 + s * 512: hc * 2048 + (s + 1) * 512]
                    vsl = A[:, hc * 2048 + t * 512: hc * 2048 + (t + 1) * 512]
                    if t == 0:
                        V.mark(nv.tensor_tensor(dsl, pX[:], vsl, ALU.mult))
                    else:
                        V.mark(nv.tensor_tensor(scrA[:], pX[:], vsl, ALU.mult))
                        V.fence()
                        V.mark(nv.tensor_tensor(dsl, dsl, scrA[:], ALU.add))
        wsl_load(0, wproj[:, :, 0:128], 1024)
        for b in range(4):
            t0 = b * 512
            for oc in range(DC):
                li = b * 8 + oc
                slot = li % 2
                if li + 1 < 32:
                    noc = (li + 1) % 8
                    wsl_load((li + 1) % 2, wproj[:, :, noc * 128:(noc + 1) * 128], 1024)
                wsl_wait(slot)
                dst_ps = pPA if oc % 2 == 0 else pPB
                for fc in range(DC):
                    inst = nt.matmul(dst_ps[:], wsl[:, slot, fc * 128:(fc + 1) * 128],
                                     Bl[:, fc * 2048 + t0: fc * 2048 + t0 + 512],
                                     start=(fc == 0), stop=(fc == DC - 1))
                T_.mark(inst)
                wsl_done(slot)
                xb = xchunk[oc % 2]
                SY.wait(V)
                SY.wait(S)
                SY.mark(nsy.dma_start(out=xb[:], in_=xs[:, oc, t0:t0 + 512]), 16)
                V.wait(T_)
                V.wait(SY)
                V.wait(S)
                yb = ychunk[oc % 2]
                V.mark(nv.tensor_tensor(yb[:], dst_ps[:], xb[:], ALU.add))
                T_.wait(V)
                S.wait(V)
                S.mark(S.e.copy(y_bf[:], yb[:]))
                S.mark(S.e.activation(sq_bf[:], yb[:], AF.Square))
                SY.wait(V)
                SY.mark(nsy.dma_start(out=ybuf.ap()[:, oc, t0:t0 + 512], in_=yb[:]), 16)
                T_.wait(S)
                i1 = nt.matmul(pSum[0:1, :], onec[:, 0:1], y_bf[:], start=(oc == 0), stop=(oc == DC - 1))
                i2 = nt.matmul(pSq[0:1, :], onec[:, 0:1], sq_bf[:], start=(oc == 0), stop=(oc == DC - 1))
                T_.mark(i1)
                T_.mark(i2)
                S.wait(T_)
            stats_finish(b)
            bcast_rows(b)
            for oc in range(DC):
                yb = ychunk[oc % 2]
                SY.wait(V)
                SY.mark(nsy.dma_start(out=yb[:], in_=ybuf.ap()[:, oc, t0:t0 + 512]), 16)
                V.wait(SY)
                V.wait(T_)
                V.fence()
                V.mark(nv.tensor_tensor(scrA[:], yb[:], pBa[:], ALU.mult))
                V.fence()
                V.mark(nv.tensor_tensor(scrA[:], scrA[:], pBb[:], ALU.add))
                V.fence()
                V.mark(nv.tensor_scalar(A[:, oc * 2048 + t0: oc * 2048 + t0 + 512], scrA[:],
                                        lnww[:, 2, oc:oc + 1], lnww[:, 3, oc:oc + 1],
                                        ALU.mult, ALU.add))
            T_.wait(V)
        for tt in range(4):
            t0 = tt * 512
            for dc in range(DC):
                inst = nt.matmul(pX[0:8, :], wrt[:, dc, :],
                                 A[:, dc * 2048 + t0: dc * 2048 + t0 + 512],
                                 start=(dc == 0), stop=(dc == DC - 1))
            T_.mark(inst)
            S.wait(T_)
            S.mark(S.e.copy(prb[0:8, 0, :], pX[0:8, :]))
            T_.wait(S)
            for q4 in range(4):
                T_.wait(V)
                T_.mark(nt.transpose(P7b[:, 0:8], prb[0:8, 0, q4 * 128:(q4 + 1) * 128], idt[0:8, 0:8]))
                V.wait(T_)
                V.mark(nv.tensor_copy(tm8[:], P7b[:, 0:8]))
                V.fence()
                V.mark(nv.reduce_max(tm8m[:], tm8[:], axis=AX.X, negate=True))
                V.fence()
                V.mark(nv.tensor_scalar(tm8[:], tm8[:], tm8m[:, 0:1], None, ALU.add))
                S.wait(V)
                S.mark(S.e.activation(tm8[:], tm8[:], AF.Exp))
                V.wait(S)
                V.mark(nv.reduce_sum(tm8m[:], tm8[:], axis=AX.X))
                V.fence()
                V.mark(nv.reciprocal(tm8m[:], tm8m[:]))
                V.fence()
                V.mark(nv.tensor_scalar(tm8[:], tm8[:], tm8m[:, 0:1], None, ALU.mult))
                V.fence()
                V.mark(nv.reduce_max(tm8m[:], tm8[:], axis=AX.X))
                V.fence()
                V.mark(nv.tensor_scalar(tm8b[:], tm8[:], tm8m[:, 0:1], None, ALU.is_ge))
                V.fence()
                V.mark(nv.scalar_tensor_tensor(tm8b[:], tm8b[:], 2.0, tm8[:], ALU.mult, ALU.subtract))
                V.fence()
                V.mark(nv.tensor_scalar(tm8b[:], tm8b[:], -1.0, None, ALU.mult))
                V.fence()
                V.mark(nv.reduce_max(tm8m[:], tm8b[:], axis=AX.X))
                V.fence()
                V.mark(nv.tensor_scalar(tm8b[:], tm8[:], tm8m[:, 0:1], None, ALU.is_ge))
                V.fence()
                V.mark(nv.tensor_tensor(tm8c[:], tm8[:], tm8b[:], ALU.mult))
                T_.wait(V)
                T_.mark(nt.transpose(P7b[0:8, 0:128], tm8c[:], idt[:]))
                V.wait(T_)
                V.mark(nv.tensor_copy(wrows[:, t0 + q4 * 128: t0 + (q4 + 1) * 128], P7b[0:8, 0:128]))
        for slab in range(4):
            s0 = slab * 512
            V.fence()
            V.mark(nv.memset(moe[:], 0.0))
            for e in range(8):
                T_.wait(V)
                T_.wait(S)
                T_.mark(nt.matmul(pBa[:], sel[:, e, :], wrows[:, s0:s0 + 512],
                                  start=True, stop=True))
                S.wait(T_)
                S.mark(S.e.copy(wbc[:], pBa[:]))
                wsl_load(0, w1[e, :, :, 0:256], 2048)
                for fj in range(8):
                    slot = fj % 2
                    if fj + 1 < 8:
                        wsl_load((fj + 1) % 2, w1[e, :, :, (fj + 1) * 256:(fj + 2) * 256], 2048)
                    wsl_wait(slot)
                    for fi in range(2):
                        fc = fj * 2 + fi
                        dst_ps = pPA if fc % 2 == 0 else pPB
                        for dc in range(DC):
                            inst = nt.matmul(dst_ps[:], wsl[:, slot, dc * 256 + fi * 128: dc * 256 + (fi + 1) * 128],
                                             A[:, dc * 2048 + s0: dc * 2048 + s0 + 512],
                                             start=(dc == 0), stop=(dc == DC - 1))
                        T_.mark(inst)
                        S.wait(T_)
                        S.mark(S.e.activation(Bl[:, fc * 512: (fc + 1) * 512], dst_ps[:], AF.Gelu_apprx_tanh))
                        T_.wait(S)
                    wsl_done(slot)
                wsl_load(0, w2[e, :, :, 0:128], 2048)
                for oc in range(8):
                    slot = oc % 2
                    if oc + 1 < 8:
                        wsl_load((oc + 1) % 2, w2[e, :, :, (oc + 1) * 128:(oc + 2) * 128], 2048)
                    wsl_wait(slot)
                    dst_ps = pPA if oc % 2 == 0 else pPB
                    for fc in range(16):
                        inst = nt.matmul(dst_ps[:], wsl[:, slot, fc * 128:(fc + 1) * 128],
                                         Bl[:, fc * 512: (fc + 1) * 512],
                                         start=(fc == 0), stop=(fc == 15))
                    T_.mark(inst)
                    wsl_done(slot)
                    V.wait(T_)
                    V.wait(S)
                    V.fence()
                    V.mark(nv.tensor_tensor(scrA[:], dst_ps[:], wbc[:], ALU.mult))
                    V.fence()
                    V.mark(nv.tensor_tensor(moe[:, oc, :], moe[:, oc, :], scrA[:], ALU.add))
                    T_.wait(V)
            for oc in range(DC):
                xb = xchunk[oc % 2]
                SY.wait(V)
                SY.wait(S)
                SY.mark(nsy.dma_start(out=xb[:], in_=ybuf.ap()[:, oc, s0:s0 + 512]), 16)
                SY.mark(nsy.dma_start(out=scrB[:], in_=xs[:, oc, s0:s0 + 512]), 16)
                V.wait(SY)
                V.wait(S)
                yb = ychunk[oc % 2]
                V.fence()
                V.mark(nv.tensor_tensor(yb[:], moe[:, oc, :], xb[:], ALU.add))
                V.fence()
                V.mark(nv.tensor_tensor(yb[:], yb[:], scrB[:], ALU.add))
                S.wait(V)
                S.mark(S.e.copy(y_bf[:], yb[:]))
                S.mark(S.e.activation(sq_bf[:], yb[:], AF.Square))
                SY.wait(V)
                SY.mark(nsy.dma_start(out=ybuf.ap()[:, oc, s0:s0 + 512], in_=yb[:]), 16)
                T_.wait(S)
                i1 = nt.matmul(pSum[0:1, :], onec[:, 0:1], y_bf[:], start=(oc == 0), stop=(oc == DC - 1))
                i2 = nt.matmul(pSq[0:1, :], onec[:, 0:1], sq_bf[:], start=(oc == 0), stop=(oc == DC - 1))
                T_.mark(i1)
                T_.mark(i2)
                S.wait(T_)
            stats_finish(slab)
        for b in range(4):
            t0 = b * 512
            bcast_rows(b)
            for oc in range(DC):
                yb = ychunk[oc % 2]
                ob = xchunk[oc % 2]
                SY.wait(V)
                SY.mark(nsy.dma_start(out=yb[:], in_=ybuf.ap()[:, oc, t0:t0 + 512]), 16)
                V.wait(SY)
                V.wait(T_)
                V.fence()
                V.mark(nv.tensor_tensor(ob[:], yb[:], pBa[:], ALU.mult))
                V.fence()
                V.mark(nv.tensor_tensor(ob[:], ob[:], pBb[:], ALU.add))
                SY.wait(V)
                SY.mark(nsy.dma_start(out=xo[:, oc, t0:t0 + 512], in_=ob[:]), 16)
            T_.wait(V)
        ng.wait_ge(SY.sem, SY.n)
        ng.wait_ge(G.sem, G.n)
        ng.wait_ge(wps[0].sem, wps[0].cnt)
        ng.wait_ge(wps[1].sem, wps[1].cnt)


        ctx.es = old_es
def host_l0_ffn(ins):
    NH, NKV = 16, 4
    qkvw = ins["L0_ffn_qkv_weight"].astype(np.float32)
    wq = qkvw[0:1024].T
    idx = np.repeat(np.arange(NKV), NH // NKV)
    colk = (idx[:, None] * 64 + np.arange(64)[None, :]).reshape(-1)
    wk = qkvw[1024:1280].T[:, colk]
    wv = qkvw[1280:1536].T[:, colk]
    wqkv = np.ascontiguousarray(np.concatenate([wq, wk, wv], 1).reshape(DC, 128, 3072))
    wproj = np.ascontiguousarray(ins["L0_ffn_proj_weight"].astype(np.float32).T.reshape(DC, 128, 1024))
    wrout = np.ascontiguousarray(ins["L0_ffn_router_weight"].astype(np.float32).T.reshape(DC, 128, 8).transpose(1, 0, 2).reshape(128, DC * 8))
    w1 = np.ascontiguousarray(ins["L0_ffn_moe_w1"].astype(np.float32).reshape(8, DC, 128, 2048))
    w2 = np.ascontiguousarray(ins["L0_ffn_moe_w2"].astype(np.float32).reshape(8, 16, 128, 1024))
    lnw = np.stack([ins["L0_ffn_ln1_weight"], ins["L0_ffn_ln1_bias"],
                    ins["L0_ffn_ln2_weight"], ins["L0_ffn_ln2_bias"]], 0).astype(np.float32)
    lnw = np.ascontiguousarray(lnw.reshape(4, DC, 128).transpose(2, 0, 1).reshape(128, 4 * DC))
    mh = np.zeros((128, DC, 16), np.float32)
    for hc in range(DC):
        mh[0:64, hc, 2 * hc] = 1.0
        mh[64:128, hc, 2 * hc + 1] = 1.0
    mhT = np.ascontiguousarray(mh.transpose(2, 1, 0).reshape(16, DC * 128))
    idt = np.eye(128, dtype=np.float32)
    conesr = np.zeros((2, 512), np.float32); conesr[0] = 1.0
    cones = np.zeros((128, 2), np.float32); cones[:, 0] = 1.0
    cflt = np.zeros((128, 4), np.float32); cflt[:, 0] = 1e-5
    csel = np.zeros((8, 8, 128), np.float32)
    for e in range(8):
        csel[e, e, :] = 1.0
    csel = csel.reshape(8, 8 * 128)
    return dict(wqkv=wqkv, wproj=wproj, wrout=wrout, w1=w1, w2=w2, lnw=lnw,
                cmh=np.ascontiguousarray(mh.reshape(128, DC * 16)), cmhT=mhT, cidt=idt,
                conesr=conesr, cones=cones, cflt=cflt, csel=csel)




def std_setup(ctx, nc, es, lite=False):
    class NS: pass
    h = NS()
    sb, ps = ctx.sb, ctx.ps
    nv = nc.vector
    T_, V, S, SY, G = ctx.T, ctx.V, ctx.S, ctx.SY, ctx.G
    h.scrA = sb("scrA", (128, 512), F32)
    h.scrB = sb("scrB", (128, 512), F32)
    h.sq_bf = sb("sq_bf", (128, 512), BF16)
    h.y_bf = sb("y_bf", (128, 512), BF16)
    if not lite:
        h.xchunk = [sb(f"xchunk{i}", (128, 512), F32) for i in range(2)]
        h.ychunk = [sb(f"ychunk{i}", (128, 512), F32) for i in range(2)]
    h.onerw = sb("onerw", (2, 512), BF16)
    h.onerf = sb("onerf", (2, 512), F32)
    h.onec = sb("onec", (128, 2), BF16)
    h.cfl = sb("cfl", (128, 4), F32)
    if not lite:
        h.arow = sb("arow", (1, 4, 512), F32)
        h.brow = sb("brow", (1, 4, 512), F32)
    h.P = [ps(f"P{i}") for i in range(7)]
    h.pPA, h.pPB = h.P[0], h.P[1]
    h.pSum, h.pSq = h.P[2], h.P[3]
    h.pBa, h.pBb = h.P[4], h.P[5]
    h.pX = h.P[6]
    h.wsl = None if lite else sb("wsl", (128, 2, 4096), BF16)
    h.wps = [ctx.pool("wsl0"), ctx.pool("wsl1")]

    def wsl_load(slot, src_ap, n):
        ctx.G.e.wait_ge(ctx.T.sem, h.wps[slot].cons)
        c = src_ap.shape[0]
        dst = h.wsl[:, slot, 0:n].rearrange("p (c f) -> p c f", c=c)
        nc.gpsimd.dma_start(out=dst, in_=src_ap.rearrange("c p f -> p c f")).then_inc(h.wps[slot].sem, 16)
        h.wps[slot].cnt += 16

    def wsl_wait(slot):
        ctx.T.e.wait_ge(h.wps[slot].sem, h.wps[slot].cnt)

    def wsl_done(slot):
        h.wps[slot].cons = ctx.T.n

    def wsl_finish(eng):
        eng.wait_ge(h.wps[0].sem, h.wps[0].cnt)
        eng.wait_ge(h.wps[1].sem, h.wps[1].cnt)

    h.wsl_load, h.wsl_wait, h.wsl_done, h.wsl_finish = wsl_load, wsl_wait, wsl_done, wsl_finish

    def load_std_consts(conesr, cones, cflt):
        SD = ctx.sd_eng()
        h.SD = SD
        ctx.G.mark(nc.gpsimd.dma_start(out=h.onerw[:], in_=conesr), 16)
        ctx.G.mark(nc.gpsimd.dma_start(out=h.onec[:], in_=cones), 16)
        SD.mark(nc.scalar.dma_start(out=h.onerf[:], in_=conesr), 16)
        SD.mark(nc.scalar.dma_start(out=h.cfl[:], in_=cflt), 16)
    h.load_std_consts = load_std_consts

    def stats_finish(b):
        V.wait(T_); V.wait(SY)
        V.fence()
        V.mark(nv.tensor_scalar(h.ychunk[1][0:1, :], h.pSum[0:1, :], 1.0 / D, None, ALU.mult))
        V.mark(nv.tensor_scalar(h.scrA[0:1, :], h.pSq[0:1, :], 1.0 / D, None, ALU.mult))
        V.fence()
        V.mark(nv.tensor_tensor(h.scrB[0:1, :], h.ychunk[1][0:1, :], h.ychunk[1][0:1, :], ALU.mult))
        V.fence()
        V.mark(nv.tensor_tensor(h.scrB[0:1, :], h.scrA[0:1, :], h.scrB[0:1, :], ALU.subtract))
        S.wait(V)
        S.mark(S.e.activation(h.scrA[0:1, :], h.scrB[0:1, :], AF.Sqrt, bias=h.cfl[0:1, 0:1]))
        V.wait(S)
        V.mark(nv.reciprocal(h.arow[:, b, :], h.scrA[0:1, :]))
        V.fence()
        V.mark(nv.tensor_tensor(h.scrB[0:1, :], h.ychunk[1][0:1, :], h.arow[:, b, :], ALU.mult))
        V.fence()
        V.mark(nv.tensor_scalar(h.brow[:, b, :], h.scrB[0:1, :], -1.0, None, ALU.mult))
        T_.wait(V)
    h.stats_finish = stats_finish

    def bcast_rows(tr):
        T_.wait(V)
        ia = nc.tensor.matmul(h.pBa[:], h.onerf[0:1, 0:128], h.arow[:, tr, :], start=True, stop=True)
        ib = nc.tensor.matmul(h.pBb[:], h.onerf[0:1, 0:128], h.brow[:, tr, :], start=True, stop=True)
        T_.mark(ia); T_.mark(ib)
    h.bcast_rows = bcast_rows

    def resid_stats_chunk(oc, n_oc, y_from, xs_ap, t0, ybuf_ap, extra_ap=None):
        nsy = nc.sync
        xb = h.xchunk[oc % 2]
        SY.wait(V); SY.wait(S)
        SY.mark(nsy.dma_start(out=xb[:], in_=xs_ap[:, oc, t0:t0 + 512]), 16)
        if extra_ap is not None:
            SY.mark(nsy.dma_start(out=h.scrB[:], in_=extra_ap[:, oc, t0:t0 + 512]), 16)
        V.wait(T_); V.wait(SY); V.wait(S)
        yb = h.ychunk[oc % 2]
        V.fence()
        V.mark(nv.tensor_tensor(yb[:], y_from, xb[:], ALU.add))
        if extra_ap is not None:
            V.fence()
            V.mark(nv.tensor_tensor(yb[:], yb[:], h.scrB[:], ALU.add))
        T_.wait(V)
        S.wait(V)
        S.mark(S.e.copy(h.y_bf[:], yb[:]))
        S.mark(S.e.activation(h.sq_bf[:], yb[:], AF.Square))
        SY.wait(V)
        SY.mark(nsy.dma_start(out=ybuf_ap[:, oc, t0:t0 + 512], in_=yb[:]), 16)
        T_.wait(S)
        i1 = nc.tensor.matmul(h.pSum[0:1, :], h.onec[:, 0:1], h.y_bf[:],
                              start=(oc == 0), stop=(oc == n_oc - 1))
        i2 = nc.tensor.matmul(h.pSq[0:1, :], h.onec[:, 0:1], h.sq_bf[:],
                              start=(oc == 0), stop=(oc == n_oc - 1))
        T_.mark(i1); T_.mark(i2)
        S.wait(T_)
    h.resid_stats_chunk = resid_stats_chunk

    def ln_apply_pass(ybuf_ap, xo_ap, n_rows=4):
        nsy = nc.sync
        for trow in range(n_rows):
            t0 = trow * 512
            h.bcast_rows(trow)
            for oc in range(DC):
                yb = h.ychunk[oc % 2]
                ob = h.xchunk[oc % 2]
                SY.wait(V)
                SY.mark(nsy.dma_start(out=yb[:], in_=ybuf_ap[:, oc, t0:t0 + 512]), 16)
                V.wait(SY); V.wait(T_)
                V.fence()
                V.mark(nv.tensor_tensor(ob[:], yb[:], h.pBa[:], ALU.mult))
                V.fence()
                V.mark(nv.tensor_tensor(ob[:], ob[:], h.pBb[:], ALU.add))
                SY.wait(V)
                SY.mark(nsy.dma_start(out=xo_ap[:, oc, t0:t0 + 512], in_=ob[:]), 16)
            T_.wait(V)
    h.ln_apply_pass = ln_apply_pass
    return h

def std_host_consts():
    conesr = np.zeros((2, 512), np.float32); conesr[0] = 1.0
    cones = np.zeros((128, 2), np.float32); cones[:, 0] = 1.0; cones[0, 1] = 1e-6
    cflt = np.zeros((128, 4), np.float32); cflt[:, 0] = 1e-5
    return dict(conesr=conesr, cones=cones, cflt=cflt)


def build_l1_ffn(nc):
    xs = nc.dram_tensor("xs", [128, DC, TOK], F32, kind="ExternalInput").ap()
    wfc = nc.dram_tensor("wfc", [DC, 128, 4096], F32, kind="ExternalInput").ap()
    wpr = nc.dram_tensor("wpr", [32, 128, 1024], F32, kind="ExternalInput").ap()
    conesr = nc.dram_tensor("conesr", [2, 512], F32, kind="ExternalInput").ap()
    cones = nc.dram_tensor("cones", [128, 2], F32, kind="ExternalInput").ap()
    cflt = nc.dram_tensor("cflt", [128, 4], F32, kind="ExternalInput").ap()
    xo = nc.dram_tensor("xo", [128, DC, TOK], F32, kind="ExternalOutput").ap()
    ybuf = nc.dram_tensor("ybuf", [128, DC, TOK], F32)

    io = dict(xs=xs, wfc=wfc, wpr=wpr, conesr=conesr, cones=cones, cflt=cflt, xo=xo, ybuf=ybuf)
    with contextlib.ExitStack() as es0:
        ctx = Ctx(nc, es0)
        es0.enter_context(nc.Block())
        body_l1_ffn(nc, ctx, io)
    return nc

def body_l1_ffn(nc, ctx, io):
    xs = io["xs"]; wfc = io["wfc"]; wpr = io["wpr"]; conesr = io["conesr"]; cones = io["cones"]; cflt = io["cflt"]; xo = io["xo"]; ybuf = io["ybuf"]
    with contextlib.ExitStack() as es:
        old_es, ctx.es = ctx.es, es
        T_, V, S, SY, G, CC = ctx.T, ctx.V, ctx.S, ctx.SY, ctx.G, ctx.CC
        sb = ctx.sb
        nv = nc.vector; nt = nc.tensor; ng = nc.gpsimd
        h = std_setup(ctx, nc, es)
        x_bf = sb("x_bf", (128, DC, TOK), BF16)
        WFC = sb("WFC", (128, DC, 4096), BF16)
        hbf = sb("hbf", (128, 32, 512), BF16)
        scrY = sb("scrY", (128, 512), F32)
        mem = sb("mem", (128, 512), F32)
        ymin = sb("ymin", (128, 512), F32)
        spk = sb("spk", (128, 512), F32)
        acc = sb("acc", (128, 512), F32)

        G.mark(ng.dma_start(out=x_bf[:], in_=xs), 16)
        G.mark(ng.dma_start(out=WFC[:], in_=wfc.rearrange("c p f -> p c f")), 16)
        h.load_std_consts(conesr, cones, cflt)
        T_.wait(G); V.wait(G); S.wait(G)
        T_.wait(h.SD); V.wait(h.SD); S.wait(h.SD)
        G.fence()

        for b in range(4):
            t0 = b * 512
            for fc in range(32):
                dst_ps = h.pPA if fc % 2 == 0 else h.pPB
                for dc in range(DC):
                    inst = nt.matmul(dst_ps[:], WFC[:, dc, fc * 128:(fc + 1) * 128],
                                     x_bf[:, dc, t0:t0 + 512],
                                     start=(dc == 0), stop=(dc == DC - 1))
                T_.mark(inst)
                V.wait(T_); V.wait(S)
                V.fence()
                V.mark(nv.tensor_single_scalar(h.scrA[:], dst_ps[:], 0.0, ALU.max))
                S.wait(V)
                S.mark(S.e.activation(hbf[:, fc, :], h.scrA[:], AF.Square))
                T_.wait(S)
            if b == 0:
                h.wsl_load(0, wpr[:, :, 0:128], 4096)
            for oc in range(DC):
                li = b * 8 + oc
                slot = li % 2
                if li + 1 < 32:
                    noc = (li + 1) % 8
                    h.wsl_load((li + 1) % 2, wpr[:, :, noc * 128:(noc + 1) * 128], 4096)
                h.wsl_wait(slot)
                dst_ps = h.pPA if oc % 2 == 0 else h.pPB
                for fc in range(32):
                    inst = nt.matmul(dst_ps[:], h.wsl[:, slot, fc * 128:(fc + 1) * 128],
                                     hbf[:, fc, :], start=(fc == 0), stop=(fc == 31))
                T_.mark(inst)
                h.wsl_done(slot)
                V.wait(T_); V.wait(S)
                V.fence()
                V.mark(nv.tensor_copy(scrY[:], dst_ps[:]))
                V.fence()
                V.mark(nv.tensor_copy(mem[:], scrY[:]))
                V.fence()
                V.mark(nv.tensor_single_scalar(acc[:], mem[:], 1.0, ALU.is_gt))
                for step in range(4):
                    V.fence()
                    V.mark(nv.tensor_single_scalar(spk[:], mem[:], 1.0, ALU.is_gt))
                    V.fence()
                    V.mark(nv.tensor_tensor(ymin[:], scrY[:], spk[:], ALU.subtract))
                    V.fence()
                    V.mark(nv.scalar_tensor_tensor(mem[:], mem[:], 0.9, ymin[:], ALU.mult, ALU.add))
                    V.fence()
                    V.mark(nv.tensor_single_scalar(spk[:], mem[:], 1.0, ALU.is_gt))
                    V.fence()
                    V.mark(nv.tensor_tensor(acc[:], acc[:], spk[:], ALU.add))
                V.fence()
                V.mark(nv.tensor_scalar(acc[:], acc[:], 0.2, None, ALU.mult))
                h.resid_stats_chunk(oc, DC, acc[:], xs, t0, ybuf.ap())
            h.stats_finish(b)
        h.ln_apply_pass(ybuf.ap(), xo)
        ng.wait_ge(SY.sem, SY.n)
        ng.wait_ge(G.sem, G.n)
        h.wsl_finish(ng)

        ctx.es = old_es
def host_l1_ffn(ins):
    wfc = np.ascontiguousarray(ins["L1_ffn_W_fc"].astype(np.float32).T.reshape(DC, 128, 4096))
    wpr = np.ascontiguousarray(ins["L1_ffn_W_proj"].astype(np.float32).T.reshape(32, 128, 1024))
    return dict(wfc=wfc, wpr=wpr, **std_host_consts())



def build_l2_ffn(nc):
    xs = nc.dram_tensor("xs", [128, DC, TOK], F32, kind="ExternalInput").ap()
    atm = nc.dram_tensor("atm", [2, 4, 1024], F32, kind="ExternalInput").ap()
    btm = nc.dram_tensor("btm", [2, 4, 256], F32, kind="ExternalInput").ap()
    bias = nc.dram_tensor("bias", [128, 8], F32, kind="ExternalInput").ap()
    conesr = nc.dram_tensor("conesr", [2, 512], F32, kind="ExternalInput").ap()
    cones = nc.dram_tensor("cones", [128, 2], F32, kind="ExternalInput").ap()
    cflt = nc.dram_tensor("cflt", [128, 4], F32, kind="ExternalInput").ap()
    xo = nc.dram_tensor("xo", [128, DC, TOK], F32, kind="ExternalOutput").ap()
    ybuf = nc.dram_tensor("ybuf", [128, DC, TOK], F32)
    wd1 = nc.dram_tensor("wd1", [2, 1024, 256], BF16)
    wd2 = nc.dram_tensor("wd2", [2, 512, 512], BF16)

    io = dict(xs=xs, atm=atm, btm=btm, bias=bias, conesr=conesr, cones=cones, cflt=cflt, xo=xo, ybuf=ybuf, wd1=wd1, wd2=wd2)
    with contextlib.ExitStack() as es0:
        ctx = Ctx(nc, es0)
        es0.enter_context(nc.Block())
        body_l2_ffn(nc, ctx, io)
    return nc

def body_l2_ffn(nc, ctx, io):
    xs = io["xs"]; atm = io["atm"]; btm = io["btm"]; bias = io["bias"]; conesr = io["conesr"]; cones = io["cones"]; cflt = io["cflt"]; xo = io["xo"]; ybuf = io["ybuf"]; wd1 = io["wd1"]; wd2 = io["wd2"]
    with contextlib.ExitStack() as es:
        old_es, ctx.es = ctx.es, es
        T_, V, S, SY, G, CC = ctx.T, ctx.V, ctx.S, ctx.SY, ctx.G, ctx.CC
        sb = ctx.sb
        nv = nc.vector; nt = nc.tensor; ng = nc.gpsimd; nsy = nc.sync
        h = std_setup(ctx, nc, es)
        x_bf = sb("x_bf", (128, DC, TOK), BF16)
        WT = sb("WT", (128, 2, 4, 512), BF16)
        afac = sb("afac", (4, 2, 1024), BF16)
        bfac = sb("bfac", (4, 2, 256), BF16)
        bia = sb("bia", (128, 8), F32)
        wbb = sb("wbb", (128, 256), BF16)

        G.mark(ng.dma_start(out=x_bf[:], in_=xs), 16)
        for m in range(2):
            G.mark(ng.dma_start(out=afac[:, m, :], in_=atm[m]), 16)
            G.mark(ng.dma_start(out=bfac[:, m, :], in_=btm[m]), 16)
        h.load_std_consts(conesr, cones, cflt)
        h.SD.mark(nc.scalar.dma_start(out=bia[:], in_=bias), 16)
        T_.wait(G); V.wait(G); S.wait(G)
        T_.wait(h.SD); V.wait(h.SD); S.wait(h.SD)
        G.fence()
        for m in range(2):
            for ch in range(8):
                T_.wait(S)
                T_.mark(nt.matmul(h.pX[:, 0:256], afac[:, m, ch * 128:(ch + 1) * 128],
                                  bfac[:, m, :], start=True, stop=True))
                S.wait(T_); S.wait(SY)
                S.mark(S.e.copy(wbb[:], h.pX[:, 0:256]))
                SY.wait(S)
                SY.mark(nsy.dma_start(out=wd1.ap()[m, ch * 128:(ch + 1) * 128, :], in_=wbb[:]), 16)
        SY.fence()
        G.e.wait_ge(SY.sem, SY.n)
        for m in range(2):
            srcv = wd1.ap()[m].rearrange("(b d) (e g) -> b e d g", b=32, e=16)
            dstv = wd2.ap()[m].rearrange("(b e) (d g) -> b e d g", b=32, d=32)
            for bb in range(32):
                G.mark(ng.dma_start(out=dstv[bb], in_=srcv[bb]), 16)
        G.fence()
        SY.wait(G)
        for m in range(2):
            for ic in range(4):
                SY.mark(nsy.dma_start(out=WT[:, m, ic, :],
                                      in_=wd2.ap()[m, :, ic * 128:(ic + 1) * 128],
                                      transpose=True), 16)
        T_.e.wait_ge(SY.sem, SY.n)
        for b in range(4):
            t0 = b * 512
            for oc in range(DC):
                dst_ps = h.pPA if oc % 2 == 0 else h.pPB
                if oc < 4:
                    for ic in range(4):
                        i1 = nt.matmul(dst_ps[:], WT[:, 0, ic, oc * 128:(oc + 1) * 128],
                                       x_bf[:, ic, t0:t0 + 512], start=(ic == 0), stop=(ic == 3))
                    T_.mark(i1)
                    T_.wait(V)
                    for ic in range(4):
                        i2 = nt.matmul(h.pX[:], WT[:, 1, ic, oc * 128:(oc + 1) * 128],
                                       x_bf[:, 4 + ic, t0:t0 + 512], start=(ic == 0), stop=(ic == 3))
                    T_.mark(i2)
                    V.wait(T_)
                    V.fence()
                    V.mark(nv.tensor_scalar(h.scrA[:], dst_ps[:], bia[:, oc:oc + 1], None, ALU.add))
                    V.fence()
                    V.mark(nv.tensor_tensor(h.scrA[:], h.scrA[:], h.pX[:], ALU.subtract))
                else:
                    for ic in range(4):
                        nt.matmul(dst_ps[:], WT[:, 1, ic, (oc - 4) * 128:(oc - 3) * 128],
                                  x_bf[:, ic, t0:t0 + 512], start=(ic == 0), stop=False)
                    for ic in range(4):
                        i1 = nt.matmul(dst_ps[:], WT[:, 0, ic, (oc - 4) * 128:(oc - 3) * 128],
                                       x_bf[:, 4 + ic, t0:t0 + 512], start=False, stop=(ic == 3))
                    T_.mark(i1)
                    V.wait(T_)
                    V.fence()
                    V.mark(nv.tensor_scalar(h.scrA[:], dst_ps[:], bia[:, oc:oc + 1], None, ALU.add))
                h.resid_stats_chunk(oc, DC, h.scrA[:], xs, t0, ybuf.ap())
            h.stats_finish(b)
        h.ln_apply_pass(ybuf.ap(), xo)
        ng.wait_ge(SY.sem, SY.n)
        ng.wait_ge(G.sem, G.n)

        ctx.es = old_es
def host_l2_ffn(ins):
    G1R = ins["L2_ffn_G1_R"].astype(np.float32)
    G2R = ins["L2_ffn_G2_R"].astype(np.float32)
    G1I = ins["L2_ffn_G1_I"].astype(np.float32)
    G2I = ins["L2_ffn_G2_I"].astype(np.float32)
    atm = np.stack([G1R[0].transpose(1, 0, 2).reshape(4, 1024),
                    G1I[0].transpose(1, 0, 2).reshape(4, 1024)], 0)
    btm = np.stack([G2R[:, :, 0, :].reshape(4, 256),
                    G2I[:, :, 0, :].reshape(4, 256)], 0)
    bias = np.concatenate([ins["L2_ffn_bias_real"], ins["L2_ffn_bias_imag"]]).astype(np.float32)
    bias = np.ascontiguousarray(bias.reshape(8, 128).T)
    return dict(atm=np.ascontiguousarray(atm), btm=np.ascontiguousarray(btm), bias=bias,
                **std_host_consts())



def build_l3_ffn(nc):
    xs = nc.dram_tensor("xs", [128, DC, TOK], F32, kind="ExternalInput").ap()
    wgr = nc.dram_tensor("wgr", [16, DC, 128, 256], F32, kind="ExternalInput").ap()
    wur = nc.dram_tensor("wur", [16, DC, 128, 256], F32, kind="ExternalInput").ap()
    wdr = nc.dram_tensor("wdr", [16, 2, 128, 1024], F32, kind="ExternalInput").ap()
    wgs = nc.dram_tensor("wgs", [2, DC, 128, 512], F32, kind="ExternalInput").ap()
    wus = nc.dram_tensor("wus", [2, DC, 128, 512], F32, kind="ExternalInput").ap()
    wds = nc.dram_tensor("wds", [2, 4, 128, 1024], F32, kind="ExternalInput").ap()
    wrt_in = nc.dram_tensor("wrt_in", [128, DC * 16], F32, kind="ExternalInput").ap()
    csel = nc.dram_tensor("csel", [16, 16 * 128], F32, kind="ExternalInput").ap()
    cidt = nc.dram_tensor("cidt", [128, 128], F32, kind="ExternalInput").ap()
    conesr = nc.dram_tensor("conesr", [2, 512], F32, kind="ExternalInput").ap()
    cones = nc.dram_tensor("cones", [128, 2], F32, kind="ExternalInput").ap()
    cflt = nc.dram_tensor("cflt", [128, 4], F32, kind="ExternalInput").ap()
    xo = nc.dram_tensor("xo", [128, DC, TOK], F32, kind="ExternalOutput").ap()
    ybuf = nc.dram_tensor("ybuf", [128, DC, TOK], F32)

    io = dict(xs=xs, wgr=wgr, wur=wur, wdr=wdr, wgs=wgs, wus=wus, wds=wds, wrt_in=wrt_in, csel=csel, cidt=cidt, conesr=conesr, cones=cones, cflt=cflt, xo=xo, ybuf=ybuf)
    with contextlib.ExitStack() as es0:
        ctx = Ctx(nc, es0)
        es0.enter_context(nc.Block())
        body_l3_ffn(nc, ctx, io)
    return nc

def body_l3_ffn(nc, ctx, io):
    xs = io["xs"]; wgr = io["wgr"]; wur = io["wur"]; wdr = io["wdr"]; wgs = io["wgs"]; wus = io["wus"]; wds = io["wds"]; wrt_in = io["wrt_in"]; csel = io["csel"]; cidt = io["cidt"]; conesr = io["conesr"]; cones = io["cones"]; cflt = io["cflt"]; xo = io["xo"]; ybuf = io["ybuf"]
    with contextlib.ExitStack() as es:
        old_es, ctx.es = ctx.es, es
        T_, V, S, SY, G, CC = ctx.T, ctx.V, ctx.S, ctx.SY, ctx.G, ctx.CC
        sb = ctx.sb
        nv = nc.vector; nt = nc.tensor; ng = nc.gpsimd; nsy = nc.sync
        h = std_setup(ctx, nc, es)
        x_bf = sb("x_bf", (128, DC, TOK), BF16)
        moe = sb("moe", (128, DC, 1024), F32)
        hb = sb("hb", (128, 4, 1024), BF16)
        sil = sb("sil", (128, 512), BF16)
        wbc = sb("wbc", (128, 1024), BF16)
        wrows = sb("wrows", (16, 2048), BF16)
        wrt = sb("wrt", (128, DC, 16), BF16)
        sel = sb("sel", (16, 16, 128), BF16)
        idt = sb("idt", (128, 128), BF16)
        tm = sb("tm", (128, 16), F32)
        tmb = sb("tmb", (128, 16), F32)
        tmm = sb("tmm", (128, 1), F32)
        tmc = sb("tmc", (128, 16), BF16)
        P7b = ctx.ps("P7b", (128, 1024), BF16)
        wsl2 = sb("wsl2", (128, 2, 2048), BF16)
        w2ps = [ctx.pool("w2sl0"), ctx.pool("w2sl1")]

        def wsl2_load(slot, src_ap, n):
            G.e.wait_ge(T_.sem, w2ps[slot].cons)
            c = src_ap.shape[0]
            dst = wsl2[:, slot, 0:n].rearrange("p (c f) -> p c f", c=c)
            ng.dma_start(out=dst, in_=src_ap.rearrange("c p f -> p c f")).then_inc(w2ps[slot].sem, 16)
            w2ps[slot].cnt += 16
        def wsl2_wait(slot):
            T_.e.wait_ge(w2ps[slot].sem, w2ps[slot].cnt)
        def wsl2_done(slot):
            w2ps[slot].cons = T_.n

        G.mark(ng.dma_start(out=x_bf[:], in_=xs), 16)
        G.mark(ng.dma_start(out=wrt[:].rearrange("p c f -> p (c f)"), in_=wrt_in), 16)
        G.mark(ng.dma_start(out=sel[:].rearrange("p c f -> p (c f)"), in_=csel), 16)
        G.mark(ng.dma_start(out=idt[:], in_=cidt), 16)
        h.load_std_consts(conesr, cones, cflt)
        T_.wait(G); V.wait(G); S.wait(G)
        T_.wait(h.SD); V.wait(h.SD); S.wait(h.SD)
        G.fence()

        for tt in range(4):
            t0 = tt * 512
            for dc in range(DC):
                inst = nt.matmul(h.pX[0:16, :], wrt[:, dc, :],
                                 x_bf[:, dc, t0:t0 + 512], start=(dc == 0), stop=(dc == DC - 1))
            T_.mark(inst)
            S.wait(T_)
            S.mark(S.e.copy(h.sq_bf[0:16, :], h.pX[0:16, :]))
            T_.wait(S)
            for q4 in range(4):
                T_.wait(V)
                T_.mark(nt.transpose(P7b[:, 0:16], h.sq_bf[0:16, q4 * 128:(q4 + 1) * 128], idt[0:16, 0:16]))
                V.wait(T_)
                V.mark(nv.tensor_copy(tm[:], P7b[:, 0:16]))
                V.fence()
                V.mark(nv.reduce_max(tmm[:], tm[:], axis=AX.X, negate=True))
                V.fence()
                V.mark(nv.tensor_scalar(tm[:], tm[:], tmm[:, 0:1], None, ALU.add))
                S.wait(V)
                S.mark(S.e.activation(tm[:], tm[:], AF.Exp))
                V.wait(S)
                V.mark(nv.reduce_sum(tmm[:], tm[:], axis=AX.X))
                V.fence()
                V.mark(nv.reciprocal(tmm[:], tmm[:]))
                V.fence()
                V.mark(nv.tensor_scalar(tm[:], tm[:], tmm[:, 0:1], None, ALU.mult))
                V.fence()
                V.mark(nv.reduce_max(tmm[:], tm[:], axis=AX.X))
                V.fence()
                V.mark(nv.tensor_scalar(tmb[:], tm[:], tmm[:, 0:1], None, ALU.is_ge))
                V.fence()
                V.mark(nv.scalar_tensor_tensor(tmb[:], tmb[:], 2.0, tm[:], ALU.mult, ALU.subtract))
                V.fence()
                V.mark(nv.tensor_scalar(tmb[:], tmb[:], -1.0, None, ALU.mult))
                V.fence()
                V.mark(nv.reduce_max(tmm[:], tmb[:], axis=AX.X))
                V.fence()
                V.mark(nv.tensor_scalar(tmb[:], tm[:], tmm[:, 0:1], None, ALU.is_ge))
                V.fence()
                V.mark(nv.tensor_tensor(tm[:], tm[:], tmb[:], ALU.mult))
                V.fence()
                V.mark(nv.reduce_sum(tmm[:], tm[:], axis=AX.X))
                V.fence()
                V.mark(nv.tensor_scalar(tmm[:], tmm[:], 1e-9, None, ALU.add))
                V.fence()
                V.mark(nv.reciprocal(tmm[:], tmm[:]))
                V.fence()
                V.mark(nv.tensor_scalar(tmc[:], tm[:], tmm[:, 0:1], None, ALU.mult))
                T_.wait(V)
                T_.mark(nt.transpose(P7b[0:16, 0:128], tmc[:], idt[:]))
                V.wait(T_)
                V.mark(nv.tensor_copy(wrows[:, t0 + q4 * 128: t0 + (q4 + 1) * 128], P7b[0:16, 0:128]))
        for slab in range(2):
            s0 = slab * 1024
            V.fence()
            V.mark(nv.memset(moe[:], 0.0))
            for n in range(2):
                for fj in range(2):
                    sl = (n * 2 + fj) % 2
                    h.wsl_load(sl, wgs[n, :, :, fj * 256:(fj + 1) * 256], 2048)
                    wsl2_load(sl, wus[n, :, :, fj * 256:(fj + 1) * 256], 2048)
                    h.wsl_wait(sl); wsl2_wait(sl)
                    for fc2 in range(2):
                        fc = fj * 2 + fc2
                        for sub in range(2):
                            t0 = s0 + sub * 512
                            pg = h.pPA if (fc + sub) % 2 == 0 else h.pPB
                            pu = h.P[2] if (fc + sub) % 2 == 0 else h.P[3]
                            for dc in range(DC):
                                ig = nt.matmul(pg[:], h.wsl[:, sl, dc * 256 + fc2 * 128: dc * 256 + (fc2 + 1) * 128],
                                               x_bf[:, dc, t0:t0 + 512], start=(dc == 0), stop=(dc == DC - 1))
                            T_.mark(ig)
                            for dc in range(DC):
                                iu = nt.matmul(pu[:], wsl2[:, sl, dc * 256 + fc2 * 128: dc * 256 + (fc2 + 1) * 128],
                                               x_bf[:, dc, t0:t0 + 512], start=(dc == 0), stop=(dc == DC - 1))
                            T_.mark(iu)
                            S.wait(T_); S.wait(V)
                            S.mark(S.e.activation(sil[:], pg[:], AF.Silu))
                            V.wait(S)
                            V.fence()
                            V.mark(nv.tensor_tensor(hb[:, fc, sub * 512:sub * 512 + 512], sil[:], pu[:], ALU.mult))
                            T_.wait(V)
                    h.wsl_done(sl); wsl2_done(sl)
                for oh in range(2):
                    sl = oh % 2
                    h.wsl_load(sl, wds[n, :, :, oh * 512:(oh + 1) * 512], 2048)
                    h.wsl_wait(sl)
                    for oc2 in range(4):
                        oc = oh * 4 + oc2
                        for sub in range(2):
                            dst_ps = h.pPA if (oc + sub) % 2 == 0 else h.pPB
                            for fc in range(4):
                                inst = nt.matmul(dst_ps[:], h.wsl[:, sl, fc * 512 + oc2 * 128: fc * 512 + (oc2 + 1) * 128],
                                                 hb[:, fc, sub * 512:sub * 512 + 512],
                                                 start=(fc == 0), stop=(fc == 3))
                            T_.mark(inst)
                            V.wait(T_)
                            V.fence()
                            msl = moe[:, oc, sub * 512:sub * 512 + 512]
                            V.mark(nv.tensor_tensor(msl, msl, dst_ps[:], ALU.add))
                            T_.wait(V)
                    h.wsl_done(sl)
            for e in range(16):
                h.wsl_load(e % 2, wgr[e], 2048)
                wsl2_load(e % 2, wur[e], 2048)
                h.wsl_wait(e % 2); wsl2_wait(e % 2)
                for fc in range(2):
                    for sub in range(2):
                        t0 = s0 + sub * 512
                        pg = h.pPA if (fc + sub) % 2 == 0 else h.pPB
                        pu = h.P[2] if (fc + sub) % 2 == 0 else h.P[3]
                        for dc in range(DC):
                            ig = nt.matmul(pg[:], h.wsl[:, e % 2, dc * 256 + fc * 128: dc * 256 + (fc + 1) * 128],
                                           x_bf[:, dc, t0:t0 + 512], start=(dc == 0), stop=(dc == DC - 1))
                        T_.mark(ig)
                        for dc in range(DC):
                            iu = nt.matmul(pu[:], wsl2[:, e % 2, dc * 256 + fc * 128: dc * 256 + (fc + 1) * 128],
                                           x_bf[:, dc, t0:t0 + 512], start=(dc == 0), stop=(dc == DC - 1))
                        T_.mark(iu)
                        S.wait(T_); S.wait(V)
                        S.mark(S.e.activation(sil[:], pg[:], AF.Silu))
                        V.wait(S)
                        V.fence()
                        V.mark(nv.tensor_tensor(hb[:, fc, sub * 512:sub * 512 + 512], sil[:], pu[:], ALU.mult))
                        T_.wait(V)
                h.wsl_done(e % 2); wsl2_done(e % 2)
                for sub in range(2):
                    T_.wait(V); T_.wait(S)
                    T_.mark(nt.matmul(h.pBa[:], sel[:, e, :], wrows[:, s0 + sub * 512: s0 + sub * 512 + 512],
                                      start=True, stop=True))
                    S.wait(T_)
                    S.mark(S.e.copy(wbc[:, sub * 512:sub * 512 + 512], h.pBa[:]))
                h.wsl_load(e % 2, wdr[e], 2048)
                h.wsl_wait(e % 2)
                for oc in range(DC):
                    for sub in range(2):
                        dst_ps = h.pPA if (oc + sub) % 2 == 0 else h.pPB
                        for fc in range(2):
                            inst = nt.matmul(dst_ps[:], h.wsl[:, e % 2, fc * 1024 + oc * 128: fc * 1024 + (oc + 1) * 128],
                                             hb[:, fc, sub * 512:sub * 512 + 512],
                                             start=(fc == 0), stop=(fc == 1))
                        T_.mark(inst)
                        V.wait(T_); V.wait(S)
                        V.fence()
                        V.mark(nv.tensor_tensor(h.scrA[:], dst_ps[:], wbc[:, sub * 512:sub * 512 + 512], ALU.mult))
                        V.fence()
                        msl = moe[:, oc, sub * 512:sub * 512 + 512]
                        V.mark(nv.tensor_tensor(msl, msl, h.scrA[:], ALU.add))
                        T_.wait(V)
                h.wsl_done(e % 2)
            for sub in range(2):
                t0 = s0 + sub * 512
                b4 = slab * 2 + sub
                for oc in range(DC):
                    h.resid_stats_chunk(oc, DC, moe[:, oc, sub * 512:sub * 512 + 512],
                                        xs, t0, ybuf.ap())
                h.stats_finish(b4)
        h.ln_apply_pass(ybuf.ap(), xo)
        ng.wait_ge(SY.sem, SY.n)
        ng.wait_ge(G.sem, G.n)
        h.wsl_finish(ng)
        ng.wait_ge(w2ps[0].sem, w2ps[0].cnt)
        ng.wait_ge(w2ps[1].sem, w2ps[1].cnt)

        ctx.es = old_es
def host_l3_ffn(ins):
    wgr = np.ascontiguousarray(ins["L3_ffn_wg_r"].astype(np.float32).reshape(16, DC, 128, 256))
    wur = np.ascontiguousarray(ins["L3_ffn_wu_r"].astype(np.float32).reshape(16, DC, 128, 256))
    wdr = np.ascontiguousarray(ins["L3_ffn_wd_r"].astype(np.float32).reshape(16, 2, 128, 1024))
    wgs = np.ascontiguousarray(ins["L3_ffn_wg_s"].astype(np.float32).reshape(2, DC, 128, 512))
    wus = np.ascontiguousarray(ins["L3_ffn_wu_s"].astype(np.float32).reshape(2, DC, 128, 512))
    wds = np.ascontiguousarray(ins["L3_ffn_wd_s"].astype(np.float32).reshape(2, 4, 128, 1024))
    wrt_in = np.ascontiguousarray(ins["L3_ffn_router_w"].astype(np.float32)
                                  .reshape(DC, 128, 16).transpose(1, 0, 2).reshape(128, DC * 16))
    csel = np.zeros((16, 16, 128), np.float32)
    for e in range(16):
        csel[e, e, :] = 1.0
    idt = np.eye(128, dtype=np.float32)
    return dict(wgr=wgr, wur=wur, wdr=wdr, wgs=wgs, wus=wus, wds=wds, wrt_in=wrt_in,
                csel=csel.reshape(16, 16 * 128), cidt=idt, **std_host_consts())



def build_proj_resid_ln(nc):
    xs = nc.dram_tensor("xs", [128, DC, TOK], F32, kind="ExternalInput").ap()
    osd = nc.dram_tensor("osd", [128, DC, TOK], F32, kind="ExternalInput").ap()
    wp = nc.dram_tensor("wp", [DC, 128, 1024], F32, kind="ExternalInput").ap()
    wbias = nc.dram_tensor("wbias", [128, DC], F32, kind="ExternalInput").ap()
    conesr = nc.dram_tensor("conesr", [2, 512], F32, kind="ExternalInput").ap()
    cones = nc.dram_tensor("cones", [128, 2], F32, kind="ExternalInput").ap()
    cflt = nc.dram_tensor("cflt", [128, 4], F32, kind="ExternalInput").ap()
    xo = nc.dram_tensor("xo", [128, DC, TOK], F32, kind="ExternalOutput").ap()
    ybuf = nc.dram_tensor("ybuf", [128, DC, TOK], F32)

    io = dict(xs=xs, osd=osd, wp=wp, wbias=wbias, conesr=conesr, cones=cones, cflt=cflt, xo=xo, ybuf=ybuf)
    with contextlib.ExitStack() as es0:
        ctx = Ctx(nc, es0)
        es0.enter_context(nc.Block())
        body_proj_resid_ln(nc, ctx, io)
    return nc

def body_proj_resid_ln(nc, ctx, io):
    xs = io["xs"]; osd = io["osd"]; wp = io["wp"]; wbias = io["wbias"]; conesr = io["conesr"]; cones = io["cones"]; cflt = io["cflt"]; xo = io["xo"]; ybuf = io["ybuf"]
    with contextlib.ExitStack() as es:
        old_es, ctx.es = ctx.es, es
        T_, V, S, SY, G, CC = ctx.T, ctx.V, ctx.S, ctx.SY, ctx.G, ctx.CC
        sb = ctx.sb
        nv = nc.vector; nt = nc.tensor; ng = nc.gpsimd
        h = std_setup(ctx, nc, es)
        o_bf = sb("o_bf", (128, DC, TOK), BF16)
        wbs = sb("wbs", (128, DC), F32)
        G.mark(ng.dma_start(out=o_bf[:], in_=osd), 16)
        h.load_std_consts(conesr, cones, cflt)
        h.SD.mark(nc.scalar.dma_start(out=wbs[:], in_=wbias), 16)
        T_.wait(G); V.wait(G); S.wait(G)
        T_.wait(h.SD); V.wait(h.SD); S.wait(h.SD)
        G.fence()
        h.wsl_load(0, wp[:, :, 0:128], 1024)
        for b in range(4):
            t0 = b * 512
            for oc in range(DC):
                li = b * 8 + oc
                slot = li % 2
                if li + 1 < 32:
                    noc = (li + 1) % 8
                    h.wsl_load((li + 1) % 2, wp[:, :, noc * 128:(noc + 1) * 128], 1024)
                h.wsl_wait(slot)
                dst_ps = h.pPA if oc % 2 == 0 else h.pPB
                for fc in range(DC):
                    inst = nt.matmul(dst_ps[:], h.wsl[:, slot, fc * 128:(fc + 1) * 128],
                                     o_bf[:, fc, t0:t0 + 512],
                                     start=(fc == 0), stop=(fc == DC - 1))
                T_.mark(inst)
                h.wsl_done(slot)
                V.wait(T_); V.wait(S)
                V.fence()
                V.mark(nv.tensor_scalar(h.scrA[:], dst_ps[:], wbs[:, oc:oc + 1], None, ALU.add))
                h.resid_stats_chunk(oc, DC, h.scrA[:], xs, t0, ybuf.ap())
            h.stats_finish(b)
        h.ln_apply_pass(ybuf.ap(), xo)
        ng.wait_ge(SY.sem, SY.n)
        ng.wait_ge(G.sem, G.n)
        h.wsl_finish(ng)

        ctx.es = old_es

def build_l2_attn_heads(nc):
    xb_in = nc.dram_tensor("xb_in", [128, DC, T], F32, kind="ExternalInput").ap()
    wqkv = nc.dram_tensor("wqkv", [DC, 128, 1536], F32, kind="ExternalInput").ap()
    ctab = nc.dram_tensor("ctab", [128, 4, T], F32, kind="ExternalInput").ap()
    cpsw = nc.dram_tensor("cpsw", [128, 128], F32, kind="ExternalInput").ap()
    ctri = nc.dram_tensor("ctri", [128, 128], F32, kind="ExternalInput").ap()
    conesr = nc.dram_tensor("conesr", [2, 512], F32, kind="ExternalInput").ap()
    cones = nc.dram_tensor("cones", [128, 2], F32, kind="ExternalInput").ap()
    cflt = nc.dram_tensor("cflt", [128, 4], F32, kind="ExternalInput").ap()
    oh = nc.dram_tensor("oh", [8, 64, T], BF16, kind="ExternalOutput")
    vstash = nc.dram_tensor("vstash", [32, 128, 512], BF16)

    io = dict(xb_in=xb_in, wqkv=wqkv, ctab=ctab, cpsw=cpsw, ctri=ctri, conesr=conesr, cones=cones, cflt=cflt, oh=oh, vstash=vstash)
    with contextlib.ExitStack() as es0:
        ctx = Ctx(nc, es0)
        es0.enter_context(nc.Block())
        body_l2_attn_heads(nc, ctx, io)
    return nc

def body_l2_attn_heads(nc, ctx, io):
    xb_in = io["xb_in"]; wqkv = io["wqkv"]; ctab = io["ctab"]; cpsw = io["cpsw"]; ctri = io["ctri"]; conesr = io["conesr"]; cones = io["cones"]; cflt = io["cflt"]; oh = io["oh"]; vstash = io["vstash"]
    with contextlib.ExitStack() as es:
        old_es, ctx.es = ctx.es, es
        T_, V, S, SY, G, CC = ctx.T, ctx.V, ctx.S, ctx.SY, ctx.G, ctx.CC
        sb, ps = ctx.sb, ctx.ps
        nv = nc.vector; nt = nc.tensor; ng = nc.gpsimd; nsy = nc.sync
        h = std_setup(ctx, nc, es, lite=True)
        qa = sb("qa", (65, 8, T), BF16)
        ka = sb("ka", (65, 8, T), BF16)
        vhead = sb("vhead", (128, 32, 64), BF16)
        wqkv_sb = sb("wqkv_sb", (128, DC, 1536), BF16)
        xtile = sb("xtile", (128, DC, 512), BF16)
        tabs = sb("tabs", (128, 4, 512), BF16)
        psw = sb("psw", (128, 128), BF16)
        tri = sb("tri", (128, 128), BF16)
        nrm = sb("nrm", (1, 2, 512), F32)
        bcell = sb("bcell", (1, 16), F32)
        dden = sb("dden", (1, 512), F32)
        ost = sb("ost", (128, 512), BF16)
        obuf = sb("obuf", (64, 1, 512), BF16)

        G.mark(ng.dma_start(out=psw[:], in_=cpsw), 16)
        G.mark(ng.dma_start(out=wqkv_sb[:], in_=wqkv.rearrange("c p f -> p c f")), 16)
        G.mark(ng.dma_start(out=tri[:], in_=ctri), 16)
        h.load_std_consts(conesr, cones, cflt)
        T_.wait(G); V.wait(G); S.wait(G)
        T_.wait(h.SD); V.wait(h.SD); S.wait(h.SD)
        G.fence()
        for tt in range(8):
            t0 = tt * 512
            G.e.wait_ge(T_.sem, ctx.T.n)
            G.e.wait_ge(V.sem, ctx.V.n)
            G.mark(ng.dma_start(out=xtile[:], in_=xb_in[:, :, t0:t0 + 512]), 16)
            for cc4 in range(4):
                G.mark(ng.dma_start(out=tabs[:, cc4, :], in_=ctab[:, cc4, t0:t0 + 512]), 16)
            T_.wait(G)
            V.wait(G)
            for part, cbase, dst in ((0, 0, qa), (1, 512, ka)):
                for hh in range(8):
                    dst_ps = h.pPA if hh % 2 == 0 else h.pPB
                    for dc in range(DC):
                        inst = nt.matmul(dst_ps[0:64, :],
                                         wqkv_sb[:, dc, cbase + hh * 64:cbase + (hh + 1) * 64],
                                         xtile[:, dc, :], start=(dc == 0), stop=(dc == DC - 1))
                    T_.mark(inst)
                    S.wait(T_); S.wait(V)
                    S.mark(S.e.copy(h.scrA[0:64, :], dst_ps[0:64, :]))
                    S.mark(S.e.copy(h.y_bf[0:64, :], dst_ps[0:64, :]))
                    T_.wait(S)
                    T_.mark(nt.matmul(h.pX[0:64, :], psw[0:64, 0:64], h.y_bf[0:64, :],
                                      start=True, stop=True))
                    ci = 2 if part == 0 else 0
                    V.wait(T_)
                    V.fence()
                    V.mark(nv.tensor_tensor(h.scrB[0:64, :], h.scrA[0:64, :], tabs[0:64, ci, :], ALU.mult))
                    V.fence()
                    V.mark(nv.tensor_tensor(h.scrA[0:64, :], h.pX[0:64, :], tabs[0:64, ci + 1, :], ALU.mult))
                    V.fence()
                    V.mark(nv.tensor_tensor(dst[0:64, hh, t0:t0 + 512], h.scrA[0:64, :],
                                            h.scrB[0:64, :], ALU.add))
                    T_.wait(V)
            for st4 in range(4):
                ts0 = t0 + st4 * 128
                dst_ps = h.pPA if st4 % 2 == 0 else h.pPB
                for dc in range(DC):
                    inst = nt.matmul(dst_ps[:], xtile[:, dc, st4 * 128:(st4 + 1) * 128],
                                     wqkv_sb[:, dc, 1024:1536], start=(dc == 0), stop=(dc == DC - 1))
                T_.mark(inst)
                S.wait(T_); S.wait(SY)
                S.mark(S.e.copy(h.sq_bf[:], dst_ps[:]))
                SY.wait(S)
                SY.mark(nsy.dma_start(out=vstash.ap()[tt * 4 + st4], in_=h.sq_bf[:]), 16)
                T_.wait(S)
        V.fence()
        V.mark(nv.memset(ka[64:65, :, :].rearrange("p c t -> p (c t)"), 1.0))
        for hh in range(8):
            for part, buf in ((0, qa), (1, ka)):
                first = True
                for tt in range(8):
                    t0 = tt * 512
                    S.wait(T_); S.wait(V); S.wait(SY)
                    S.mark(S.e.activation(h.sq_bf[0:64, :], buf[0:64, hh, t0:t0 + 512], AF.Square))
                    T_.wait(S)
                    i1 = nt.matmul(h.pSum[0:1, :], h.onec[0:64, 0:1], h.sq_bf[0:64, :],
                                   start=True, stop=True)
                    T_.mark(i1)
                    V.wait(T_)
                    V.fence()
                    V.mark(nv.reduce_max(nrm[:, 0, 0:1], h.pSum[0:1, :], axis=AX.X))
                    V.fence()
                    if first:
                        V.mark(nv.tensor_copy(bcell[:, part * 8 + hh:part * 8 + hh + 1], nrm[:, 0, 0:1]))
                    else:
                        V.mark(nv.tensor_tensor(bcell[:, part * 8 + hh:part * 8 + hh + 1],
                                                bcell[:, part * 8 + hh:part * 8 + hh + 1],
                                                nrm[:, 0, 0:1], ALU.max))
                    first = False
            V.fence()
            V.mark(nv.tensor_tensor(nrm[:, 0, 0:1], bcell[:, hh:hh + 1], bcell[:, 8 + hh:8 + hh + 1], ALU.mult))
            S.wait(V)
            S.mark(S.e.activation(nrm[:, 0, 0:1], nrm[:, 0, 0:1], AF.Sqrt))
            V.wait(S)
            V.mark(nv.tensor_scalar(nrm[:, 0, 0:1], nrm[:, 0, 0:1], -1.0, None, ALU.mult))
            V.fence()
            V.mark(nv.memset(qa[64:65, hh, :], 1.0))
            V.fence()
            V.mark(nv.tensor_scalar(qa[64:65, hh, :], qa[64:65, hh, :], nrm[0:1, 0, 0:1], None, ALU.mult))
        SY.fence()
        for hh in range(8):
            G.e.wait_ge(T_.sem, ctx.T.n)
            G.mark(ng.dma_start(out=vhead[:],
                                in_=vstash.ap()[:, :, hh * 64:(hh + 1) * 64]
                                .rearrange("c p f -> p c f")), 16)
            T_.wait(G)
            for qt in range(8):
                q0 = qt * 512
                pO = h.P[3]
                pD = h.P[4]
                for kt in range(4 * (qt + 1)):
                    k0 = kt * 128
                    T_.wait(V); T_.wait(S)
                    T_.mark(nt.matmul(h.pX[:], ka[:, hh, k0:k0 + 128], qa[:, hh, q0:q0 + 512],
                                      start=True, stop=True))
                    S.wait(T_)
                    S.mark(S.e.activation(ost[:], h.pX[:], AF.Exp))
                    diag = (kt // 4 == qt)
                    if diag:
                        V.wait(S)
                        j = kt % 4
                        V.mark(nv.tensor_tensor(ost[:, j * 128:(j + 1) * 128],
                                                ost[:, j * 128:(j + 1) * 128], tri[:], ALU.mult))
                        if j > 0:
                            V.fence()
                            V.mark(nv.memset(ost[:, 0:j * 128], 0.0))
                        T_.wait(V)
                    else:
                        T_.wait(S)
                    last = (kt == 4 * (qt + 1) - 1)
                    io = nt.matmul(pO[0:64, :], vhead[:, kt, :], ost[:],
                                   start=(kt == 0), stop=last)
                    instd = nt.matmul(pD[0:1, :], h.onec[:, 0:1], ost[:],
                                      start=(kt == 0), stop=last)
                    if last:
                        T_.mark(io); T_.mark(instd)
                V.wait(T_)
                V.fence()
                V.mark(nv.reciprocal(dden[:], pD[0:1, :]))
                T_.wait(V)
                T_.mark(nt.matmul(h.pBa[0:64, :], h.onerf[0:1, 0:64], dden[:], start=True, stop=True))
                S.wait(T_)
                S.mark(S.e.copy(h.y_bf[0:64, :], h.pBa[0:64, :]))
                V.wait(S); V.wait(SY)
                V.fence()
                V.mark(nv.tensor_tensor(obuf[:, 0, :], pO[0:64, :], h.y_bf[0:64, :], ALU.mult))
                SY.wait(V)
                SY.mark(nsy.dma_start(out=oh.ap()[hh, :, q0:q0 + 512], in_=obuf[:, 0, :]), 16)
                T_.wait(V)
        ng.wait_ge(SY.sem, SY.n)
        ng.wait_ge(G.sem, G.n)

        ctx.es = old_es
def host_l2_attn_heads(ins, x_full):
    H, dk = 16, 64
    wq = ins["L2_attn_wq"].astype(np.float32)
    wk = ins["L2_attn_wk"].astype(np.float32)
    wv = ins["L2_attn_wv"].astype(np.float32)
    half = dk // 2
    inv = 10000.0 ** (-np.arange(half, dtype=np.float32) / half)
    ang = np.arange(T, dtype=np.float32)[:, None] * inv
    cos1, sin1 = np.cos(ang).astype(np.float32), np.sin(ang).astype(np.float32)
    cosr = np.zeros((64, T), np.float32); sinr = np.zeros((64, T), np.float32)
    cosr[0::2] = cos1.T; cosr[1::2] = cos1.T
    sinr[0::2] = -sin1.T; sinr[1::2] = sin1.T
    ctab = np.zeros((128, 4, T), np.float32)
    ctab[:, 0] = np.concatenate([cosr, cosr], 0)
    ctab[:, 1] = np.concatenate([sinr, sinr], 0)
    ctab[:, 2] = ctab[:, 0] / 8.0
    ctab[:, 3] = ctab[:, 1] / 8.0
    psw = np.zeros((128, 128), np.float32)
    for i in range(0, 128, 2):
        psw[i + 1, i] = 1.0
        psw[i, i + 1] = 1.0
    tri = np.ascontiguousarray(np.tril(np.ones((128, 128), np.float32)).T)
    maps = []
    for c in range(NCORES):
        b = c // 2
        h0 = (c % 2) * 8
        cols = []
        for w in (wq, wk):
            cols.append(w[h0 * 64:(h0 + 8) * 64].T)
        cols.append(wv[h0 * 64:(h0 + 8) * 64].T)
        wqkv = np.ascontiguousarray(np.concatenate(cols, 1).reshape(DC, 128, 1536))
        xb = np.ascontiguousarray(to_fm(x_full[b].astype(np.float32)))
        maps.append(dict(xb_in=xb, wqkv=wqkv, ctab=ctab, cpsw=psw, ctri=tri,
                         **std_host_consts()))
    return maps


def run_launch(build_fn, common, per_core_xs):
    nc = bass.Bass("TRN2", target_bir_lowering=False)
    build_fn(nc)
    maps = [dict(common, xs=per_core_xs[c]) for c in range(NCORES)]
    res = run_bass_kernel_spmd(nc, maps, core_ids=list(range(NCORES)))
    return [np.asarray(r["xo"]).reshape(128, DC, TOK) for r in res.results]




def build_l3_qkv(nc):
    xs = nc.dram_tensor("xs", [128, DC, TOK], F32, kind="ExternalInput").ap()
    wq = nc.dram_tensor("wq", [DC, 128, 1024], F32, kind="ExternalInput").ap()
    wk = nc.dram_tensor("wk", [DC, 128, 1024], F32, kind="ExternalInput").ap()
    wv = nc.dram_tensor("wv", [DC, 128, 1024], F32, kind="ExternalInput").ap()
    brows = nc.dram_tensor("brows", [128, 3, DC], F32, kind="ExternalInput").ap()
    bvrow = nc.dram_tensor("bvrow", [2, 1024], F32, kind="ExternalInput").ap()
    conesr = nc.dram_tensor("conesr", [2, 512], F32, kind="ExternalInput").ap()
    cones = nc.dram_tensor("cones", [128, 2], F32, kind="ExternalInput").ap()
    cflt = nc.dram_tensor("cflt", [128, 4], F32, kind="ExternalInput").ap()
    qs = nc.dram_tensor("qs", [128, DC, TOK], BF16, kind="ExternalOutput")
    ks = nc.dram_tensor("ks", [128, DC, TOK], BF16, kind="ExternalOutput")
    vs = nc.dram_tensor("vs", [16, 128, 1024], BF16, kind="ExternalOutput")
    knorm = nc.dram_tensor("knorm", [1, 1], F32, kind="ExternalOutput")

    io = dict(xs=xs, wq=wq, wk=wk, wv=wv, brows=brows, bvrow=bvrow, conesr=conesr, cones=cones, cflt=cflt, qs=qs, ks=ks, vs=vs, knorm=knorm)
    with contextlib.ExitStack() as es0:
        ctx = Ctx(nc, es0)
        es0.enter_context(nc.Block())
        body_l3_qkv(nc, ctx, io)
    return nc

def body_l3_qkv(nc, ctx, io):
    xs = io["xs"]; wq = io["wq"]; wk = io["wk"]; wv = io["wv"]; brows = io["brows"]; bvrow = io["bvrow"]; conesr = io["conesr"]; cones = io["cones"]; cflt = io["cflt"]; qs = io["qs"]; ks = io["ks"]; vs = io["vs"]; knorm = io["knorm"]
    with contextlib.ExitStack() as es:
        old_es, ctx.es = ctx.es, es
        T_, V, S, SY, G, CC = ctx.T, ctx.V, ctx.S, ctx.SY, ctx.G, ctx.CC
        sb = ctx.sb
        nv = nc.vector; nt = nc.tensor; ng = nc.gpsimd; nsy = nc.sync
        h = std_setup(ctx, nc, es)
        x_bf = sb("x_bf", (128, DC, TOK), BF16)
        W3 = sb("W3", (128, 3, DC, 1024), BF16)
        brw = sb("brw", (128, 3, DC), F32)
        bvb = sb("bvb", (128, 1024), F32)
        kcell = sb("kcell", (1, 2), F32)
        G.mark(ng.dma_start(out=x_bf[:], in_=xs), 16)
        G.mark(ng.dma_start(out=W3[:, 0].rearrange("p c f -> p c f"), in_=wq.rearrange("c p f -> p c f")), 16)
        G.mark(ng.dma_start(out=W3[:, 1], in_=wk.rearrange("c p f -> p c f")), 16)
        G.mark(ng.dma_start(out=W3[:, 2], in_=wv.rearrange("c p f -> p c f")), 16)
        h.load_std_consts(conesr, cones, cflt)
        h.SD.mark(nc.scalar.dma_start(out=brw[:].rearrange("p a b -> p (a b)"),
                                      in_=brows.rearrange("p a b -> p (a b)")), 16)
        T_.wait(G); V.wait(G); S.wait(G)
        T_.wait(h.SD); V.wait(h.SD); S.wait(h.SD)
        G.fence()
        bvr = sb("bvr", (2, 1024), F32)
        SY.mark(nsy.dma_start(out=bvr[:], in_=bvrow), 16)
        T_.wait(SY)
        for half in range(2):
            T_.wait(V)
            T_.mark(nt.matmul(h.pBa[:, 0:512], h.onerf[0:1, 0:128], bvr[0:1, half * 512:(half + 1) * 512],
                              start=True, stop=True))
            V.wait(T_)
            V.mark(nv.tensor_copy(bvb[:, half * 512:(half + 1) * 512], h.pBa[:, 0:512]))
        T_.wait(V)
        V.fence()
        V.mark(nv.memset(kcell[:], 0.0))
        for w3i, outd in ((0, qs), (1, ks)):
            for b in range(4):
                t0 = b * 512
                for oc in range(DC):
                    dst_ps = h.pPA if oc % 2 == 0 else h.pPB
                    T_.wait(V)
                    for dc in range(DC):
                        inst = nt.matmul(dst_ps[:], W3[:, w3i, dc, oc * 128:(oc + 1) * 128],
                                         x_bf[:, dc, t0:t0 + 512], start=(dc == 0), stop=(dc == DC - 1))
                    T_.mark(inst)
                    V.wait(T_); V.wait(S); V.wait(SY)
                    V.fence()
                    V.mark(nv.tensor_scalar(h.scrA[:], dst_ps[:], brw[:, w3i, oc:oc + 1], None, ALU.add))
                    S.wait(V)
                    S.mark(S.e.copy(h.y_bf[:], h.scrA[:]))
                    SY.wait(S)
                    SY.mark(nsy.dma_start(out=outd.ap()[:, oc, t0:t0 + 512], in_=h.y_bf[:]), 16)
                    if w3i == 1:
                        S.mark(S.e.activation(h.sq_bf[:], h.scrA[:], AF.Square))
                        T_.wait(S)
                        i1 = nt.matmul(h.pSum[0:1, :], h.onec[:, 0:1], h.sq_bf[:],
                                       start=(oc == 0), stop=(oc == DC - 1))
                        T_.mark(i1)
                        if oc == DC - 1:
                            V.wait(T_)
                            V.fence()
                            V.mark(nv.reduce_max(kcell[:, 1:2], h.pSum[0:1, :], axis=AX.X))
                            V.fence()
                            V.mark(nv.tensor_tensor(kcell[:, 0:1], kcell[:, 0:1], kcell[:, 1:2], ALU.max))
                            T_.wait(V)
        for ti in range(16):
            t0 = ti * 128
            dst_ps = h.pPA if ti % 2 == 0 else h.pPB
            for half in range(2):
                pd = dst_ps if half == 0 else (h.pPB if ti % 2 == 0 else h.pPA)
                T_.wait(V)
                for dc in range(DC):
                    inst = nt.matmul(pd[:], x_bf[:, dc, t0:t0 + 128],
                                     W3[:, 2, dc, half * 512:(half + 1) * 512],
                                     start=(dc == 0), stop=(dc == DC - 1))
                T_.mark(inst)
                V.wait(T_); V.wait(SY)
                V.fence()
                V.mark(nv.tensor_tensor(h.scrA[:], pd[:], bvb[:, half * 512:(half + 1) * 512], ALU.add))
                S.wait(V)
                S.mark(S.e.copy(h.y_bf[:], h.scrA[:]))
                SY.wait(S)
                SY.mark(nsy.dma_start(out=vs.ap()[ti, :, half * 512:(half + 1) * 512], in_=h.y_bf[:]), 16)
        SY.wait(V)
        SY.mark(nsy.dma_start(out=knorm.ap(), in_=kcell[:, 0:1]), 16)
        ng.wait_ge(SY.sem, SY.n)
        ng.wait_ge(G.sem, G.n)

        ctx.es = old_es
def build_l3_attn(nc):
    xs = nc.dram_tensor("xs", [128, DC, TOK], F32, kind="ExternalInput").ap()
    qsh = nc.dram_tensor("qsh", [128, DC, TOK], BF16, kind="ExternalInput").ap()
    kfull = nc.dram_tensor("kfull", [4, 128, DC, T], BF16, kind="ExternalInput").ap()
    vfull = nc.dram_tensor("vfull", [4, 32, 128, 1024], BF16, kind="ExternalInput").ap()
    knorms = nc.dram_tensor("knorms", [1, 8], F32, kind="ExternalInput").ap()
    wo = nc.dram_tensor("wo", [DC, 128, 1024], F32, kind="ExternalInput").ap()
    borow = nc.dram_tensor("borow", [128, DC], F32, kind="ExternalInput").ap()
    cam = nc.dram_tensor("cam", [128, 32, 512], F32, kind="ExternalInput").ap()
    cumask = nc.dram_tensor("cumask", [128, 4], F32, kind="ExternalInput").ap()
    conesr = nc.dram_tensor("conesr", [2, 512], F32, kind="ExternalInput").ap()
    cones = nc.dram_tensor("cones", [128, 2], F32, kind="ExternalInput").ap()
    cflt = nc.dram_tensor("cflt", [128, 4], F32, kind="ExternalInput").ap()
    xo = nc.dram_tensor("xo", [128, DC, TOK], F32, kind="ExternalOutput").ap()
    ybuf = nc.dram_tensor("ybuf", [128, DC, TOK], F32)
    CBASE = 512

    io = dict(xs=xs, qsh=qsh, kfull=kfull, vfull=vfull, knorms=knorms, wo=wo, borow=borow, cam=cam, cumask=cumask, conesr=conesr, cones=cones, cflt=cflt, xo=xo, ybuf=ybuf)
    with contextlib.ExitStack() as es0:
        ctx = Ctx(nc, es0)
        es0.enter_context(nc.Block())
        body_l3_attn(nc, ctx, io)
    return nc

def body_l3_attn(nc, ctx, io):
    CBASE = 512
    xs = io["xs"]; qsh = io["qsh"]; kfull = io["kfull"]; vfull = io["vfull"]; knorms = io["knorms"]; wo = io["wo"]; borow = io["borow"]; cam = io["cam"]; cumask = io["cumask"]; conesr = io["conesr"]; cones = io["cones"]; cflt = io["cflt"]; xo = io["xo"]; ybuf = io["ybuf"]
    with contextlib.ExitStack() as es:
        old_es, ctx.es = ctx.es, es
        T_, V, S, SY, G, CC = ctx.T, ctx.V, ctx.S, ctx.SY, ctx.G, ctx.CC
        sb = ctx.sb
        nv = nc.vector; nt = nc.tensor; ng = nc.gpsimd; nsy = nc.sync
        h = std_setup(ctx, nc, es)
        q2 = sb("q2", (128, DC, TOK), BF16)
        st_all = sb("st_all", (128, 32, 512), BF16)
        ktile = [sb(f"ktile{i}", (128, DC, 128), BF16) for i in range(2)]
        vtile = [sb(f"vtile{i}", (128, 512), BF16) for i in range(2)]
        WO = sb("WO", (128, DC, 1024), BF16)
        o_sb = sb("o_sb", (128, DC, 512), BF16)
        bor = sb("bor", (128, DC), F32)
        am = sb("am", (128, 32, 512), BF16)
        um = sb("um", (128, 4), F32)
        kn = sb("kn", (1, 8), F32)
        negB = sb("negB", (1, 4, 512), BF16)
        denr = sb("denr", (1, 512), F32)
        denb = sb("denb", (1, 512), BF16)
        uni = sb("uni", (1, 2, 512), BF16)
        pT = ctx.ps("pT", (128, 16), BF16)
        unif = sb("unif", (128, DC), F32)
        kps = [ctx.pool("kt0"), ctx.pool("kt1")]
        vps = [ctx.pool("vt0"), ctx.pool("vt1")]

        G.mark(ng.dma_start(out=q2[:], in_=qsh), 16)
        G.mark(ng.dma_start(out=am[:].rearrange("p a c -> p (a c)"), in_=cam.rearrange("p a c -> p (a c)")), 16)
        G.mark(ng.dma_start(out=WO[:], in_=wo.rearrange("c p f -> p c f")), 16)
        h.load_std_consts(conesr, cones, cflt)
        h.SD.mark(nc.scalar.dma_start(out=bor[:], in_=borow), 16)
        h.SD.mark(nc.scalar.dma_start(out=um[:], in_=cumask), 16)
        h.SD.mark(nc.scalar.dma_start(out=kn[:], in_=knorms), 16)
        T_.wait(G); V.wait(G); S.wait(G)
        T_.wait(h.SD); V.wait(h.SD); S.wait(h.SD)
        G.fence()
        V.fence()
        V.mark(nv.reduce_max(denr[:, 0:1], kn[:], axis=AX.X))
        S.wait(V)
        S.mark(S.e.activation(denr[:, 0:1], denr[:, 0:1], AF.Sqrt))
        V.wait(S)
        for dc in range(DC):
            V.fence()
            V.mark(nv.tensor_scalar(q2[:, dc, :], q2[:, dc, :], 1.0 / 32.0, None, ALU.mult))
        for b in range(4):
            t0 = b * 512
            for oc in range(DC):
                S.wait(V); S.wait(T_)
                S.mark(S.e.activation(h.sq_bf[:], q2[:, oc, t0:t0 + 512], AF.Square))
                T_.wait(S)
                i1 = nt.matmul(h.pSum[0:1, :], h.onec[:, 0:1], h.sq_bf[:],
                               start=(oc == 0), stop=(oc == DC - 1))
                T_.mark(i1)
            S.wait(T_)
            S.mark(S.e.activation(h.scrA[0:1, :], h.pSum[0:1, :], AF.Sqrt))
            V.wait(S)
            V.mark(nv.tensor_scalar(h.scrA[0:1, :], h.scrA[0:1, :], denr[0:1, 0:1], None, ALU.mult))
            V.fence()
            V.mark(nv.tensor_scalar(negB[:, b, :], h.scrA[0:1, :], -1.0, None, ALU.mult))
            T_.wait(V)

        def kload(slot, b, kt):
            G.e.wait_ge(T_.sem, kps[slot].cons)
            ng.dma_start(out=ktile[slot][:], in_=kfull[b, :, :, kt * 128:(kt + 1) * 128]).then_inc(kps[slot].sem, 16)
            kps[slot].cnt += 16

        def vload(slot, b, kt, half):
            G.e.wait_ge(T_.sem, vps[slot].cons)
            ng.dma_start(out=vtile[slot][:], in_=vfull[b, kt, :, half * 512:(half + 1) * 512]).then_inc(vps[slot].sem, 16)
            vps[slot].cnt += 16

        for b in range(4):
            kload(0, b, 0)
            for kt in range(32):
                slot = kt % 2
                if kt + 1 < 32:
                    kload((kt + 1) % 2, b, kt + 1)
                T_.e.wait_ge(kps[slot].sem, kps[slot].cnt)
                T_.wait(S)
                for dc in range(DC):
                    nt.matmul(h.pX[:], ktile[slot][:, dc, :], q2[:, dc, b * 512:(b + 1) * 512],
                              start=(dc == 0), stop=False)
                ib = nt.matmul(h.pX[:], h.onerw[0:1, 0:128], negB[:, b, :], start=False, stop=True)
                T_.mark(ib)
                kps[slot].cons = T_.n
                S.wait(T_); S.wait(V)
                S.mark(S.e.activation(st_all[:, kt, :], h.pX[:], AF.Exp))
                V.wait(S)
                V.fence()
                V.mark(nv.tensor_tensor(st_all[:, kt, :], st_all[:, kt, :],
                                        am[:, kt, :], ALU.mult))
            T_.wait(V)
            for kt in range(32):
                iD = nt.matmul(h.pSum[0:1, :], h.onec[:, 0:1], st_all[:, kt, :],
                               start=(kt == 0), stop=(kt == 31))
            T_.mark(iD)
            V.wait(T_)
            V.fence()
            V.mark(nv.tensor_scalar(denr[:], h.pSum[0:1, :], 1e-30, None, ALU.add))
            V.fence()
            V.mark(nv.reciprocal(denr[:], denr[:]))
            V.fence()
            V.mark(nv.tensor_copy(denb[:], denr[:]))
            T_.wait(V)
            T_.mark(nt.matmul(h.pBa[:], h.onerw[0:1, 0:128], denb[:], start=True, stop=True))
            S.wait(T_)
            S.mark(S.e.copy(h.y_bf[:], h.pBa[:]))
            for half in range(2):
                vload(0, b, 0, half)
                for kt in range(32):
                    slot = kt % 2
                    if kt + 1 < 32:
                        vload((kt + 1) % 2, b, kt + 1, half)
                    T_.e.wait_ge(vps[slot].sem, vps[slot].cnt)
                    for dvc in range(4):
                        io = nt.matmul(h.P[dvc][:], vtile[slot][:, dvc * 128:(dvc + 1) * 128],
                                       st_all[:, kt, :], start=(kt == 0), stop=(kt == 31))
                    iu = nt.matmul(h.P[5][0:1, :], h.onec[:, 0:1], vtile[slot][:],
                                   start=(kt == 0), stop=(kt == 31))
                    T_.mark(iu)
                    if kt == 31:
                        T_.mark(io)
                    vps[slot].cons = T_.n
                V.wait(T_)
                V.fence()
                V.wait(S)
                V.mark(nv.tensor_scalar(uni[:, half, :], h.P[5][0:1, :], 1.0 / 4096.0, None, ALU.mult))
                for dvc in range(4):
                    oc = half * 4 + dvc
                    V.fence()
                    V.wait(S)
                    V.mark(nv.tensor_tensor(h.scrA[:], h.P[dvc][:], h.y_bf[:], ALU.mult))
                    T_.wait(V)
                    T_.mark(nt.transpose(pT[:, 0:1], uni[0:1, half, dvc * 128:(dvc + 1) * 128],
                                         h.onerw[0:1, 0:1]))
                    V.wait(T_)
                    for j in range(2):
                        V.fence()
                        V.mark(nv.tensor_scalar(h.scrA[:, 510 + j:511 + j], h.scrA[:, 510 + j:511 + j],
                                                um[:, 2 + j:3 + j], None, ALU.mult))
                        V.fence()
                        V.mark(nv.tensor_scalar(h.scrB[:, 0:1], pT[:, 0:1],
                                                um[:, j:j + 1], None, ALU.mult))
                        V.fence()
                        V.mark(nv.tensor_tensor(h.scrA[:, 510 + j:511 + j], h.scrA[:, 510 + j:511 + j],
                                                h.scrB[:, 0:1], ALU.add))
                    S.wait(V)
                    S.mark(S.e.copy(o_sb[:, oc, :], h.scrA[:]))
                    T_.wait(S)
            t0 = b * 512
            for oc in range(DC):
                dst_ps = h.pPA if oc % 2 == 0 else h.pPB
                for fc in range(DC):
                    inst = nt.matmul(dst_ps[:], WO[:, fc, oc * 128:(oc + 1) * 128],
                                     o_sb[:, fc, :], start=(fc == 0), stop=(fc == DC - 1))
                T_.mark(inst)
                V.wait(T_); V.wait(S)
                V.fence()
                V.mark(nv.tensor_scalar(h.scrA[:], dst_ps[:], bor[:, oc:oc + 1], None, ALU.add))
                h.resid_stats_chunk(oc, DC, h.scrA[:], xs, t0, ybuf.ap())
            h.stats_finish(b)
        h.ln_apply_pass(ybuf.ap(), xo)
        ng.wait_ge(SY.sem, SY.n)
        ng.wait_ge(G.sem, G.n)
        ng.wait_ge(kps[0].sem, kps[0].cnt); ng.wait_ge(kps[1].sem, kps[1].cnt)
        ng.wait_ge(vps[0].sem, vps[0].cnt); ng.wait_ge(vps[1].sem, vps[1].cnt)


        ctx.es = old_es
def host_l3_qkv(ins):
    wq = np.ascontiguousarray(ins["L3_attn_wq"].astype(np.float32).reshape(DC, 128, 1024))
    wk = np.ascontiguousarray(ins["L3_attn_wk"].astype(np.float32).reshape(DC, 128, 1024))
    wv = np.ascontiguousarray(ins["L3_attn_wv"].astype(np.float32).reshape(DC, 128, 1024))
    br = np.stack([ins["L3_attn_bq"], ins["L3_attn_bk"], ins["L3_attn_bv"]], 0).astype(np.float32)
    brows = np.ascontiguousarray(br.reshape(3, DC, 128).transpose(2, 0, 1))
    bvrow = np.zeros((2, 1024), np.float32)
    bvrow[0] = ins["L3_attn_bv"].astype(np.float32)
    return dict(wq=wq, wk=wk, wv=wv, brows=brows, bvrow=bvrow, **std_host_consts())

def host_l3_attn(ins, c, kfull, vfull, knorms):
    C0 = c * TPC
    p = np.arange(128)[:, None, None]
    kt = np.arange(32)[None, :, None]
    col = np.arange(512)[None, None, :]
    am = (kt * 128 + p >= C0 + col + 2).astype(np.float32)
    cum = np.zeros((128, 4), np.float32)
    own = 1.0 if c == NCORES - 1 else 0.0
    cum[:, 0:2] = own
    cum[:, 2:4] = 1.0 - own
    wo = np.ascontiguousarray(ins["L3_attn_wo"].astype(np.float32).reshape(DC, 128, 1024))
    borow = np.ascontiguousarray(ins["L3_attn_bo"].astype(np.float32).reshape(DC, 128).T)
    return dict(kfull=kfull, vfull=vfull, knorms=knorms, wo=wo, borow=borow,
                cam=np.ascontiguousarray(am), cumask=cum, **std_host_consts())

def run_l3_attn(ins, xs_fm_shards, runner):
    nc = bass.Bass("TRN2", target_bir_lowering=False)
    build_l3_qkv(nc)
    com = host_l3_qkv(ins)
    maps = [dict(com, xs=xs_fm_shards[c]) for c in range(NCORES)]
    res = runner(nc, maps)
    kfull = np.zeros((4, 128, DC, T), dtype=BF)
    vfull = np.zeros((4, 32, 128, 1024), dtype=BF)
    qs_sh = []
    knorms = np.zeros((1, 8), np.float32)
    for c in range(NCORES):
        ksx = np.asarray(res[c]["ks"]).reshape(128, DC, TOK)
        vsx = np.asarray(res[c]["vs"]).reshape(16, 128, 1024)
        qs_sh.append(np.ascontiguousarray(np.asarray(res[c]["qs"]).reshape(128, DC, TOK)))
        for b in range(4):
            kfull[b, :, :, c * TPC:(c + 1) * TPC] = ksx[:, :, b * TPC:(b + 1) * TPC]
            vfull[b, c * 4:(c + 1) * 4] = vsx[b * 4:(b + 1) * 4]
        knorms[0, c] = np.asarray(res[c]["knorm"]).astype(np.float32).flatten()[0]
    kfull = np.ascontiguousarray(kfull)
    vfull = np.ascontiguousarray(vfull)
    nc2 = bass.Bass("TRN2", target_bir_lowering=False)
    build_l3_attn(nc2)
    maps2 = [dict(host_l3_attn(ins, c, kfull, vfull, knorms),
                  xs=xs_fm_shards[c], qsh=qs_sh[c]) for c in range(NCORES)]
    res2 = runner(nc2, maps2)
    return [np.ascontiguousarray(np.asarray(r["xo"], dtype=np.float32).reshape(128, DC, TOK)) for r in res2]


def run_l2_attn(ins, xs_fm_shards, runner):
    full = unshard_tokens([from_fm(s) for s in xs_fm_shards], D)
    nc = bass.Bass("TRN2", target_bir_lowering=False)
    build_l2_attn_heads(nc)
    maps = host_l2_attn_heads(ins, full)
    res = runner(nc, maps)
    o_full = np.zeros((B, T, D), np.float32)
    for c in range(NCORES):
        b = c // 2
        h0 = (c % 2) * 8
        ohh = np.asarray(res[c]["oh"], dtype=np.float32)
        o_full[b, :, h0 * 64:(h0 + 8) * 64] = ohh.transpose(2, 0, 1).reshape(T, 512)
    o_shards = [to_fm(s) for s in shard_tokens(o_full)]
    wo = ins["L2_attn_wo"].astype(np.float32)
    wp = np.ascontiguousarray(wo.T.reshape(DC, 128, 1024))
    nc2 = bass.Bass("TRN2", target_bir_lowering=False)
    build_proj_resid_ln(nc2)
    com = dict(wp=wp, wbias=np.zeros((128, DC), np.float32), **std_host_consts())
    maps2 = [dict(com, xs=xs_fm_shards[c], osd=o_shards[c]) for c in range(NCORES)]
    res2 = runner(nc2, maps2)
    return [np.ascontiguousarray(np.asarray(r["xo"], dtype=np.float32).reshape(128, DC, TOK)) for r in res2]



def build_l1_proj(nc):
    xs = nc.dram_tensor("xs", [128, DC, TOK], F32, kind="ExternalInput").ap()
    xh = nc.dram_tensor("xh", [128, DC, 12], F32, kind="ExternalInput").ap()
    w3 = nc.dram_tensor("w3", [3, DC, 128, 1024], F32, kind="ExternalInput").ap()
    wbg = nc.dram_tensor("wbg", [DC, 128, 8], F32, kind="ExternalInput").ap()
    wmx = nc.dram_tensor("wmx", [DC, 128, 4], F32, kind="ExternalInput").ap()
    cvw = nc.dram_tensor("cvw", [128, 3, DC, 4], F32, kind="ExternalInput").ap()
    cmixw = nc.dram_tensor("cmixw", [4, 4], F32, kind="ExternalInput").ap()
    conesr = nc.dram_tensor("conesr", [2, 512], F32, kind="ExternalInput").ap()
    cones = nc.dram_tensor("cones", [128, 2], F32, kind="ExternalInput").ap()
    cflt = nc.dram_tensor("cflt", [128, 4], F32, kind="ExternalInput").ap()
    qs = nc.dram_tensor("qs", [128, DC, TOK], BF16, kind="ExternalOutput")
    ks = nc.dram_tensor("ks", [128, DC, TOK], BF16, kind="ExternalOutput")
    vs = nc.dram_tensor("vs", [128, DC, TOK], BF16, kind="ExternalOutput")
    bgm = nc.dram_tensor("bgm", [12, TOK], F32, kind="ExternalOutput")

    io = dict(xs=xs, xh=xh, w3=w3, wbg=wbg, wmx=wmx, cvw=cvw, cmixw=cmixw, conesr=conesr, cones=cones, cflt=cflt, qs=qs, ks=ks, vs=vs, bgm=bgm)
    with contextlib.ExitStack() as es0:
        ctx = Ctx(nc, es0)
        es0.enter_context(nc.Block())
        body_l1_proj(nc, ctx, io)
    return nc

def body_l1_proj(nc, ctx, io):
    xs = io["xs"]; xh = io["xh"]; w3 = io["w3"]; wbg = io["wbg"]; wmx = io["wmx"]; cvw = io["cvw"]; cmixw = io["cmixw"]; conesr = io["conesr"]; cones = io["cones"]; cflt = io["cflt"]; qs = io["qs"]; ks = io["ks"]; vs = io["vs"]; bgm = io["bgm"]
    with contextlib.ExitStack() as es:
        old_es, ctx.es = ctx.es, es
        T_, V, S, SY, G, CC = ctx.T, ctx.V, ctx.S, ctx.SY, ctx.G, ctx.CC
        sb = ctx.sb
        nv = nc.vector; nt = nc.tensor; ng = nc.gpsimd; nsy = nc.sync
        h = std_setup(ctx, nc, es, lite=True)
        x_bf = sb("x_bf", (128, DC, TOK), BF16)
        xh_bf = sb("xh_bf", (128, DC, 12), BF16)
        W3 = sb("W3", (128, 3, DC, 1024), BF16)
        wbgb = sb("wbgb", (128, DC, 8), BF16)
        wmxb = sb("wmxb", (128, DC, 4), BF16)
        cvf = sb("cvf", (128, 3, DC, 4), F32)
        cmx = sb("cmx", (4, 4), F32)
        ppad = sb("ppad", (128, 516), F32)
        ppadM = sb("ppadM", (4, 516), F32)
        scrF = sb("scrF", (128, 512), F32)
        bgm_sb = sb("bgm_sb", (36, TOK), F32)
        G.mark(ng.dma_start(out=x_bf[:], in_=xs), 16)
        G.mark(ng.dma_start(out=xh_bf[:].rearrange("p c f -> p (c f)"),
                            in_=xh.rearrange("p c f -> p (c f)")), 16)
        for w3i in range(3):
            G.mark(ng.dma_start(out=W3[:, w3i], in_=w3[w3i].rearrange("c p f -> p c f")), 16)
        G.mark(ng.dma_start(out=wbgb[:], in_=wbg.rearrange("c p f -> p c f")), 16)
        G.mark(ng.dma_start(out=wmxb[:], in_=wmx.rearrange("c p f -> p c f")), 16)
        h.load_std_consts(conesr, cones, cflt)
        h.SD.mark(nc.scalar.dma_start(out=cvf[:].rearrange("p a b c -> p (a b c)"),
                                      in_=cvw.rearrange("p a b c -> p (a b c)")), 16)
        h.SD.mark(nc.scalar.dma_start(out=cmx[:], in_=cmixw), 16)
        T_.wait(G); V.wait(G); S.wait(G)
        T_.wait(h.SD); V.wait(h.SD); S.wait(h.SD)
        G.fence()
        for w3i, outd in ((0, qs), (1, ks), (2, vs)):
            for b in range(4):
                t0 = b * 512
                for oc in range(DC):
                    dst_ps = h.pPA if oc % 2 == 0 else h.pPB
                    T_.wait(V)
                    for dc in range(DC):
                        im = nt.matmul(dst_ps[:], W3[:, w3i, dc, oc * 128:(oc + 1) * 128],
                                       x_bf[:, dc, t0:t0 + 512], start=(dc == 0), stop=(dc == DC - 1))
                    T_.mark(im)
                    for dc in range(DC):
                        ih = nt.matmul(h.pX[:, 0:3], W3[:, w3i, dc, oc * 128:(oc + 1) * 128],
                                       xh_bf[:, dc, b * 3:b * 3 + 3], start=(dc == 0), stop=(dc == DC - 1))
                    T_.mark(ih)
                    V.wait(T_); V.wait(S)
                    V.fence()
                    V.mark(nv.tensor_copy(ppad[:, 3:515], dst_ps[:]))
                    V.mark(nv.tensor_copy(ppad[:, 0:3], h.pX[:, 0:3]))
                    V.fence()
                    V.mark(nv.tensor_scalar(h.scrA[:], ppad[:, 3:515], cvf[:, w3i, oc, 3:4], None, ALU.mult))
                    for k in range(3):
                        V.fence()
                        V.mark(nv.scalar_tensor_tensor(h.scrA[:], ppad[:, k:k + 512], cvf[:, w3i, oc, k:k + 1],
                                                       h.scrA[:], ALU.mult, ALU.add))
                    S.wait(V); S.wait(SY)
                    if w3i < 2:
                        S.mark(S.e.activation(scrF[:], h.scrA[:], AF.Silu))
                        S.fence()
                        S.mark(S.e.activation(h.y_bf[:], scrF[:], AF.Silu))
                    else:
                        S.mark(S.e.activation(h.y_bf[:], h.scrA[:], AF.Silu))
                    SY.wait(S)
                    SY.mark(nsy.dma_start(out=outd.ap()[:, oc, t0:t0 + 512], in_=h.y_bf[:]), 16)
        for b in range(4):
            t0 = b * 512
            T_.wait(V); T_.wait(S)
            for dc in range(DC):
                ib = nt.matmul(h.pSum[0:8, :], wbgb[:, dc, :], x_bf[:, dc, t0:t0 + 512],
                               start=(dc == 0), stop=(dc == DC - 1))
            T_.mark(ib)
            for dc in range(DC):
                imx = nt.matmul(h.pSq[0:4, :], wmxb[:, dc, :], x_bf[:, dc, t0:t0 + 512],
                                start=(dc == 0), stop=(dc == DC - 1))
            T_.mark(imx)
            for dc in range(DC):
                imh = nt.matmul(h.pBa[0:4, 0:3], wmxb[:, dc, :], xh_bf[:, dc, b * 3:b * 3 + 3],
                                start=(dc == 0), stop=(dc == DC - 1))
            T_.mark(imh)
            S.wait(T_)
            S.mark(S.e.activation(bgm_sb[0:8, t0:t0 + 512], h.pSum[0:8, :], AF.Sigmoid))
            V.wait(T_); V.wait(S)
            V.fence()
            V.mark(nv.tensor_copy(ppadM[:, 3:515], h.pSq[0:4, :]))
            V.mark(nv.tensor_copy(ppadM[:, 0:3], h.pBa[0:4, 0:3]))
            V.fence()
            V.mark(nv.tensor_scalar(scrF[0:4, :], ppadM[:, 3:515], cmx[:, 3:4], None, ALU.mult))
            for k in range(3):
                V.fence()
                V.mark(nv.scalar_tensor_tensor(scrF[0:4, :], ppadM[:, k:k + 512], cmx[:, k:k + 1],
                                               scrF[0:4, :], ALU.mult, ALU.add))
            S.wait(V)
            S.mark(S.e.activation(bgm_sb[32:36, t0:t0 + 512], scrF[0:4, :], AF.Silu))
        SY.wait(S)
        SY.mark(nsy.dma_start(out=bgm.ap()[0:8, :], in_=bgm_sb[0:8, :]), 16)
        SY.mark(nsy.dma_start(out=bgm.ap()[8:12, :], in_=bgm_sb[32:36, :]), 16)
        ng.wait_ge(SY.sem, SY.n)
        ng.wait_ge(G.sem, G.n)


        ctx.es = old_es
def host_l1_proj(ins, xs_fm_shards):
    w3 = np.stack([ins["L1_attn_Wq"], ins["L1_attn_Wk"], ins["L1_attn_Wv"]], 0)
    w3 = np.ascontiguousarray(w3.astype(np.float32).reshape(3, DC, 128, 1024))
    wbg = np.concatenate([ins["L1_attn_Wb"], ins["L1_attn_Wdec"]], 1).astype(np.float32)
    wbg = np.ascontiguousarray(wbg.reshape(DC, 128, 8))
    wmx = np.ascontiguousarray(ins["L1_attn_Wmix"].astype(np.float32).reshape(DC, 128, 4))
    cv = np.stack([ins["L1_attn_conv_q"], ins["L1_attn_conv_k"], ins["L1_attn_conv_v"]], 0)
    cvw = np.ascontiguousarray(cv.astype(np.float32).reshape(3, DC, 128, 4).transpose(2, 0, 1, 3))
    com = dict(w3=w3, wbg=wbg, wmx=wmx, cvw=cvw,
               cmixw=np.ascontiguousarray(ins["L1_attn_conv_mix"].astype(np.float32)),
               **std_host_consts())
    maps = []
    for c in range(NCORES):
        xhl = np.zeros((128, DC, 12), np.float32)
        if c > 0:
            prev = xs_fm_shards[c - 1]
            for b in range(4):
                xhl[:, :, b * 3:b * 3 + 3] = prev[:, :, b * 512 + 509:b * 512 + 512]
        maps.append(dict(com, xs=xs_fm_shards[c], xh=np.ascontiguousarray(xhl)))
    return maps



def build_l1_scan(nc):
    qf = nc.dram_tensor("qf", [128, 2, 2, 4096], BF16, kind="ExternalInput").ap()
    kf = nc.dram_tensor("kf", [128, 2, 2, 4096], BF16, kind="ExternalInput").ap()
    vf = nc.dram_tensor("vf", [128, 2, 2, 4096], BF16, kind="ExternalInput").ap()
    ktm = nc.dram_tensor("ktm", [2, 32, 128, 256], BF16, kind="ExternalInput").ap()
    vtm = nc.dram_tensor("vtm", [2, 32, 128, 256], BF16, kind="ExternalInput").ap()
    bcol = nc.dram_tensor("bcol", [2, 128, 32], F32, kind="ExternalInput").ap()
    brw = nc.dram_tensor("brw", [2, 4096], F32, kind="ExternalInput").ap()
    grw = nc.dram_tensor("grw", [2, 4096], F32, kind="ExternalInput").ap()
    mrw = nc.dram_tensor("mrw", [2, 4096], F32, kind="ExternalInput").ap()
    mbias2 = nc.dram_tensor("mbias2", [1, 2], F32, kind="ExternalInput").ap()
    rmswf = nc.dram_tensor("rmswf", [128, 2], F32, kind="ExternalInput").ap()
    mtrin = nc.dram_tensor("mtrin", [128, 128], F32, kind="ExternalInput").ap()
    mtriun = nc.dram_tensor("mtriun", [128, 128], F32, kind="ExternalInput").ap()
    mtui = nc.dram_tensor("mtui", [128, 128], F32, kind="ExternalInput").ap()
    idtf = nc.dram_tensor("idtf", [128, 128], F32, kind="ExternalInput").ap()
    cellk = nc.dram_tensor("cellk", [128, 2], F32, kind="ExternalInput").ap()
    conesr = nc.dram_tensor("conesr", [2, 512], F32, kind="ExternalInput").ap()
    cones = nc.dram_tensor("cones", [128, 2], F32, kind="ExternalInput").ap()
    cflt = nc.dram_tensor("cflt", [128, 4], F32, kind="ExternalInput").ap()
    od = nc.dram_tensor("od", [128, 2, 2, 4096], F32, kind="ExternalOutput").ap()

    io = dict(qf=qf, kf=kf, vf=vf, ktm=ktm, vtm=vtm, bcol=bcol, brw=brw, grw=grw, mrw=mrw, mbias2=mbias2, rmswf=rmswf, mtrin=mtrin, mtriun=mtriun, mtui=mtui, idtf=idtf, cellk=cellk, conesr=conesr, cones=cones, cflt=cflt, od=od)
    with contextlib.ExitStack() as es0:
        ctx = Ctx(nc, es0)
        es0.enter_context(nc.Block())
        body_l1_scan(nc, ctx, io)
    return nc

def body_l1_scan(nc, ctx, io):
    qf = io["qf"]; kf = io["kf"]; vf = io["vf"]; ktm = io["ktm"]; vtm = io["vtm"]; bcol = io["bcol"]; brw = io["brw"]; grw = io["grw"]; mrw = io["mrw"]; mbias2 = io["mbias2"]; rmswf = io["rmswf"]; mtrin = io["mtrin"]; mtriun = io["mtriun"]; mtui = io["mtui"]; idtf = io["idtf"]; cellk = io["cellk"]; conesr = io["conesr"]; cones = io["cones"]; cflt = io["cflt"]; od = io["od"]
    with contextlib.ExitStack() as es:
        old_es, ctx.es = ctx.es, es
        T_, V, S, SY, G, CC = ctx.T, ctx.V, ctx.S, ctx.SY, ctx.G, ctx.CC
        sb = ctx.sb
        nv = nc.vector; nt = nc.tensor; ng = nc.gpsimd; nsy = nc.sync
        h = std_setup(ctx, nc, es, lite=True)
        q_fm = sb("q_fm", (128, 2, 4096), BF16)
        k_fm = sb("k_fm", (128, 2, 4096), BF16)
        o_sb = sb("o_sb", (128, 2, 4096), F32)
        gb = sb("gb", (128, 4096), F32)
        in1b = sb("in1b", (128, 4096), F32)
        emab = sb("emab", (128, 4096), F32)
        vfb = sb("vfb", (128, 4096), BF16)
        S_sb = sb("S_sb", (128, 2, 256), F32)
        S_bf = sb("S_bf", (128, 2, 256), BF16)
        ktt = [sb(f"ktt{i}", (128, 256), BF16) for i in range(2)]
        vtt = [sb(f"vtt{i}", (128, 256), BF16) for i in range(2)]
        ktn = sb("ktn", (128, 256), BF16)
        vbn = sb("vbn", (128, 256), BF16)
        vpr = sb("vpr", (128, 256), BF16)
        uadj = sb("uadj", (128, 256), BF16)
        sqk = sb("sqk", (128, 256), BF16)
        nrm = sb("nrm", (128, 2), F32)
        scrG = sb("scrG", (128, 128), F32)
        P_sb = sb("P_sb", (128, 128), BF16)
        Pt_sb = sb("Pt_sb", (128, 128), BF16)
        M_sb = sb("M_sb", (128, 128), BF16)
        Mt_sb = sb("Mt_sb", (128, 128), BF16)
        attnT = sb("attnT", (128, 128), BF16)
        mtl = sb("mtl", (128, 128), F32)
        mtu = sb("mtu", (128, 128), F32)
        mti = sb("mti", (128, 128), F32)
        idt = sb("idt", (128, 128), F32)
        ck = sb("ck", (128, 2), F32)
        bcolu = sb("bcolu", (128, 32), F32)
        rowA = sb("rowA", (1, 4096), F32)
        rowB = sb("rowB", (1, 4096), F32)
        rmsw = sb("rmsw", (128, 2), F32)
        ktps = [ctx.pool("ktm0"), ctx.pool("ktm1")]
        vtps = [ctx.pool("vtm0"), ctx.pool("vtm1")]

        h.load_std_consts(conesr, cones, cflt)
        h.SD.mark(nc.scalar.dma_start(out=mtl[:], in_=mtrin), 16)
        h.SD.mark(nc.scalar.dma_start(out=mtu[:], in_=mtriun), 16)
        h.SD.mark(nc.scalar.dma_start(out=mti[:], in_=mtui), 16)
        h.SD.mark(nc.scalar.dma_start(out=idt[:], in_=idtf), 16)
        h.SD.mark(nc.scalar.dma_start(out=ck[:], in_=cellk), 16)
        h.SD.mark(nc.scalar.dma_start(out=rmsw[:], in_=rmswf), 16)
        T_.wait(G); V.wait(G); S.wait(G)
        T_.wait(h.SD); V.wait(h.SD); S.wait(h.SD)
        G.fence()

        def kloadB(slot, u, c):
            G.e.wait_ge(V.sem, ktps[slot].cons)
            ng.dma_start(out=ktt[slot][:], in_=ktm[u, c]).then_inc(ktps[slot].sem, 16)
            ktps[slot].cnt += 16

        def vloadB(slot, u, c):
            G.e.wait_ge(V.sem, vtps[slot].cons)
            ng.dma_start(out=vtt[slot][:], in_=vtm[u, c]).then_inc(vtps[slot].sem, 16)
            vtps[slot].cnt += 16

        for u in range(2):
            SY.wait(V); SY.wait(T_); SY.wait(S)
            SY.mark(nsy.dma_start(out=q_fm[:], in_=qf[:, u]), 16)
            SY.mark(nsy.dma_start(out=k_fm[:], in_=kf[:, u]), 16)
            SY.mark(nsy.dma_start(out=bcolu[:], in_=bcol[u]), 16)
            SY.mark(nsy.dma_start(out=rowB[:], in_=brw[u:u + 1, :]), 16)
            SY.mark(nsy.dma_start(out=rowA[:], in_=grw[u:u + 1, :]), 16)
            V.wait(SY); S.wait(SY); T_.wait(SY)
            V.fence()
            for fc in range(2):
                V.mark(nv.memset(S_sb[:, fc, :], 0.0))
                V.mark(nv.memset(S_bf[:, fc, :], 0.0))
            for tf in (q_fm, k_fm):
                for cc in range(8):
                    t0 = cc * 512
                    for fc in range(2):
                        S.wait(V)
                        if fc == 1:
                            S.wait(T_)
                        S.mark(S.e.activation(h.sq_bf[:], tf[:, fc, t0:t0 + 512], AF.Square))
                        T_.wait(S)
                        i1 = nt.matmul(h.pSum[0:1, :], h.onec[:, 0:1], h.sq_bf[:],
                                       start=(fc == 0), stop=(fc == 1))
                        T_.mark(i1)
                    S.wait(T_)
                    S.mark(S.e.activation(h.scrA[0:1, :], h.pSum[0:1, :], AF.Sqrt, bias=ck[0:1, 0:1]))
                    V.wait(S)
                    V.mark(nv.reciprocal(h.scrA[0:1, :], h.scrA[0:1, :]))
                    V.fence()
                    V.mark(nv.tensor_copy(h.y_bf[0:1, :], h.scrA[0:1, :]))
                    T_.wait(V)
                    T_.mark(nt.matmul(h.pBa[:], h.onerw[0:1, 0:128], h.y_bf[0:1, :], start=True, stop=True))
                    V.wait(T_)
                    V.fence()
                    for fc in range(2):
                        V.mark(nv.tensor_tensor(tf[:, fc, t0:t0 + 512], tf[:, fc, t0:t0 + 512],
                                                h.pBa[:], ALU.mult))
            kloadB(0, u, 0)
            vloadB(0, u, 0)
            for c in range(32):
                slot = c % 2
                if c + 1 < 32:
                    kloadB((c + 1) % 2, u, c + 1)
                    vloadB((c + 1) % 2, u, c + 1)
                t0 = c * 128
                S.e.wait_ge(ktps[slot].sem, ktps[slot].cnt)
                S.wait(V)
                S.mark(S.e.activation(sqk[:], ktt[slot][:], AF.Square))
                V.wait(S)
                V.fence()
                V.mark(nv.reduce_sum(nrm[:, 0:1], sqk[:], axis=AX.X))
                S.wait(V)
                S.mark(S.e.activation(nrm[:, 1:2], nrm[:, 0:1], AF.Sqrt, bias=ck[:, 0:1]))
                V.wait(S)
                V.fence()
                V.mark(nv.reciprocal(nrm[:, 1:2], nrm[:, 1:2]))
                V.fence()
                V.mark(nv.tensor_scalar(ktn[:], ktt[slot][:], nrm[:, 1:2], None, ALU.mult))
                ktps[slot].cons = V.n
                V.e.wait_ge(vtps[slot].sem, vtps[slot].cnt)
                V.fence()
                V.mark(nv.tensor_scalar(vbn[:], vtt[slot][:], bcolu[:, c:c + 1], None, ALU.mult))
                vtps[slot].cons = V.n
                T_.wait(V); T_.wait(S)
                for fc in range(2):
                    ig = nt.matmul(h.pX[:, 0:128], k_fm[:, fc, t0:t0 + 128], k_fm[:, fc, t0:t0 + 128],
                                   start=(fc == 0), stop=(fc == 1))
                T_.mark(ig)
                for fc in range(2):
                    ia = nt.matmul(h.pPB[:, 0:128], k_fm[:, fc, t0:t0 + 128], q_fm[:, fc, t0:t0 + 128],
                                   start=(fc == 0), stop=(fc == 1))
                T_.mark(ia)
                T_.mark(nt.matmul(h.pBa[:, 0:128], h.onerf[0:1, 0:128], rowB[0:1, t0:t0 + 128],
                                  start=True, stop=True))
                for fc in range(2):
                    iks = nt.matmul(h.pSq[:, 0:256], k_fm[:, fc, t0:t0 + 128], S_bf[:, fc, :],
                                    start=(fc == 0), stop=(fc == 1))
                T_.mark(iks)
                V.wait(T_)
                V.fence()
                V.mark(nv.tensor_copy(scrG[:], h.pX[:, 0:128]))
                V.fence()
                V.mark(nv.tensor_tensor(attnT[:], h.pPB[:, 0:128], mti[:], ALU.mult))
                V.mark(nv.tensor_scalar(h.scrB[:, 0:128], scrG[:], bcolu[:, c:c + 1], None, ALU.mult))
                V.fence()
                V.mark(nv.tensor_tensor(P_sb[:], h.scrB[:, 0:128], mtl[:], ALU.mult))
                V.mark(nv.tensor_tensor(h.scrA[:, 0:128], scrG[:], h.pBa[:, 0:128], ALU.mult))
                V.fence()
                V.mark(nv.tensor_tensor(Pt_sb[:], h.scrA[:, 0:128], mtu[:], ALU.mult))
                V.fence()
                V.mark(nv.tensor_tensor(M_sb[:], P_sb[:], idt[:], ALU.add))
                V.mark(nv.tensor_tensor(Mt_sb[:], Pt_sb[:], idt[:], ALU.add))
                V.mark(nv.tensor_scalar(h.scrA[:, 0:256], h.pSq[:, 0:256], bcolu[:, c:c + 1], None, ALU.mult))
                V.fence()
                V.mark(nv.tensor_tensor(vpr[:], vbn[:], h.scrA[:, 0:256], ALU.subtract))
                for j in range(6):
                    T_.wait(V)
                    i1 = nt.matmul(h.pPA[:, 0:128], Pt_sb[:], P_sb[:], start=True, stop=True)
                    i2 = nt.matmul(h.pPB[:, 0:128], P_sb[:], Pt_sb[:], start=True, stop=True)
                    T_.mark(i1); T_.mark(i2)
                    V.wait(T_)
                    V.fence()
                    V.mark(nv.tensor_copy(P_sb[:], h.pPA[:, 0:128]))
                    V.mark(nv.tensor_copy(Pt_sb[:], h.pPB[:, 0:128]))
                    T_.wait(V)
                    i3 = nt.matmul(h.pPA[:, 0:128], Mt_sb[:], P_sb[:], start=True, stop=True)
                    i4 = nt.matmul(h.pPB[:, 0:128], P_sb[:], Mt_sb[:], start=True, stop=True)
                    T_.mark(i3); T_.mark(i4)
                    V.wait(T_)
                    V.fence()
                    V.mark(nv.tensor_tensor(M_sb[:], M_sb[:], h.pPA[:, 0:128], ALU.add))
                    V.mark(nv.tensor_tensor(Mt_sb[:], Mt_sb[:], h.pPB[:, 0:128], ALU.add))
                T_.wait(V)
                T_.mark(nt.matmul(h.pSum[:, 0:256], Mt_sb[:], vpr[:], start=True, stop=True))
                V.wait(T_)
                V.fence()
                V.mark(nv.tensor_copy(uadj[:], h.pSum[:, 0:256]))
                T_.wait(V)
                for dc2 in range(2):
                    i1 = nt.matmul(h.pSq[:, 0:128], S_bf[:, 0, dc2 * 128:(dc2 + 1) * 128],
                                   q_fm[:, 0, t0:t0 + 128], start=True, stop=False)
                    i2 = nt.matmul(h.pSq[:, 0:128], S_bf[:, 1, dc2 * 128:(dc2 + 1) * 128],
                                   q_fm[:, 1, t0:t0 + 128], start=False, stop=False)
                    i3 = nt.matmul(h.pSq[:, 0:128], uadj[:, dc2 * 128:(dc2 + 1) * 128],
                                   attnT[:], start=False, stop=True)
                    T_.mark(i3)
                    V.wait(T_)
                    V.fence()
                    V.mark(nv.tensor_copy(o_sb[:, dc2, t0:t0 + 128], h.pSq[:, 0:128]))
                    T_.wait(V)
                for fcf in range(2):
                    ids = nt.matmul(h.P[5 + fcf][:, 0:256], ktn[:, fcf * 128:(fcf + 1) * 128],
                                    uadj[:], start=True, stop=True)
                    T_.mark(ids)
                V.wait(T_)
                V.fence()
                for fcf in range(2):
                    V.mark(nv.tensor_tensor(S_sb[:, fcf, :], S_sb[:, fcf, :],
                                            h.P[5 + fcf][:, 0:256], ALU.add))
                S.wait(V)
                for fcf in range(2):
                    S.mark(S.e.copy(S_bf[:, fcf, :], S_sb[:, fcf, :]))
                T_.wait(S)
            for cc in range(8):
                t0 = cc * 512
                T_.wait(V)
                T_.mark(nt.matmul(h.pBa[:], h.onerf[0:1, 0:128], rowA[0:1, t0:t0 + 512],
                                  start=True, stop=True))
                V.wait(T_)
                V.fence()
                V.mark(nv.tensor_copy(gb[:, t0:t0 + 512], h.pBa[:]))
            SY.wait(T_); SY.wait(S)
            SY.mark(nsy.dma_start(out=rowA[:], in_=mrw[u:u + 1, :]), 16)
            SY.mark(nsy.dma_start(out=nrm[0:1, 0:2].rearrange("p f -> p f"), in_=mbias2), 16)
            S.wait(SY); S.wait(V)
            S.mark(S.e.activation(rowA[:], rowA[:], AF.Sigmoid, bias=nrm[0:1, u:u + 1]))
            for fc in range(2):
                SY.wait(V); SY.wait(S)
                SY.mark(nsy.dma_start(out=vfb[:], in_=vf[:, u, fc, :]), 16)
                V.wait(SY)
                for cc in range(8):
                    t0 = cc * 512
                    V.fence()
                    V.mark(nv.tensor_tensor(h.scrA[:], gb[:, t0:t0 + 512], vfb[:, t0:t0 + 512], ALU.mult))
                    V.fence()
                    V.mark(nv.tensor_tensor(in1b[:, t0:t0 + 512], vfb[:, t0:t0 + 512], h.scrA[:], ALU.subtract))
                V.fence()
                V.mark(nv.tensor_tensor_scan(emab[:, :], gb[:, :], in1b[:, :], 0.0, ALU.mult, ALU.add))
                for cc in range(8):
                    t0 = cc * 512
                    T_.wait(V); T_.wait(S)
                    T_.mark(nt.matmul(h.pBb[:], h.onerf[0:1, 0:128], rowA[0:1, t0:t0 + 512],
                                      start=True, stop=True))
                    V.wait(T_); V.wait(S)
                    V.fence()
                    V.mark(nv.tensor_tensor(h.scrA[:], emab[:, t0:t0 + 512], o_sb[:, fc, t0:t0 + 512],
                                            ALU.subtract))
                    V.fence()
                    V.mark(nv.tensor_tensor(h.scrA[:], h.scrA[:], h.pBb[:], ALU.mult))
                    V.fence()
                    V.mark(nv.tensor_tensor(o_sb[:, fc, t0:t0 + 512], o_sb[:, fc, t0:t0 + 512],
                                            h.scrA[:], ALU.add))
            for cc in range(8):
                t0 = cc * 512
                for fc in range(2):
                    S.wait(V)
                    if fc == 1:
                        S.wait(T_)
                    S.mark(S.e.activation(h.sq_bf[:], o_sb[:, fc, t0:t0 + 512], AF.Square))
                    T_.wait(S)
                    i1 = nt.matmul(h.pSum[0:1, :], h.onec[:, 0:1], h.sq_bf[:],
                                   start=(fc == 0), stop=(fc == 1))
                    T_.mark(i1)
                V.wait(T_)
                V.fence()
                V.mark(nv.tensor_scalar(h.scrA[0:1, :], h.pSum[0:1, :], 1.0 / 256.0, None, ALU.mult))
                S.wait(V)
                S.mark(S.e.activation(h.scrA[0:1, :], h.scrA[0:1, :], AF.Sqrt, bias=ck[0:1, 1:2]))
                V.wait(S)
                V.fence()
                V.mark(nv.reciprocal(h.scrA[0:1, :], h.scrA[0:1, :]))
                V.fence()
                V.mark(nv.tensor_copy(h.y_bf[0:1, :], h.scrA[0:1, :]))
                T_.wait(V)
                T_.mark(nt.matmul(h.pBa[:], h.onerw[0:1, 0:128], h.y_bf[0:1, :], start=True, stop=True))
                V.wait(T_)
                for fc in range(2):
                    V.fence()
                    V.mark(nv.tensor_tensor(o_sb[:, fc, t0:t0 + 512], o_sb[:, fc, t0:t0 + 512],
                                            h.pBa[:], ALU.mult))
                    V.fence()
                    V.mark(nv.tensor_scalar(o_sb[:, fc, t0:t0 + 512], o_sb[:, fc, t0:t0 + 512],
                                            rmsw[:, fc:fc + 1], None, ALU.mult))
                    SY.wait(V)
                    SY.mark(nsy.dma_start(out=od[:, u, fc, t0:t0 + 512], in_=o_sb[:, fc, t0:t0 + 512]), 16)
        ng.wait_ge(SY.sem, SY.n)
        ng.wait_ge(G.sem, G.n)
        ng.wait_ge(ktps[0].sem, ktps[0].cnt); ng.wait_ge(ktps[1].sem, ktps[1].cnt)
        ng.wait_ge(vtps[0].sem, vtps[0].cnt); ng.wait_ge(vtps[1].sem, vtps[1].cnt)


        ctx.es = old_es
def host_l1_scan(ins, qs_sh, ks_sh, vs_sh, bgm_sh):
    BFd = BF
    def full_fm(shards):
        out = np.zeros((128, DC, 4, T), dtype=np.asarray(shards[0]).dtype)
        for c in range(NCORES):
            s = np.asarray(shards[c]).reshape(128, DC, 4, TPC)
            out[:, :, :, c * TPC:(c + 1) * TPC] = s
        return out
    qfull = full_fm(qs_sh); kfull = full_fm(ks_sh); vfull = full_fm(vs_sh)
    bg_full = np.zeros((12, 4, T), np.float32)
    for c in range(NCORES):
        bg_full[:, :, c * TPC:(c + 1) * TPC] = np.asarray(bgm_sh[c]).reshape(12, 4, TPC)
    mtl = -np.tril(np.ones((128, 128), np.float32), -1)
    mtu = -np.triu(np.ones((128, 128), np.float32), 1)
    mti = np.triu(np.ones((128, 128), np.float32), 0)
    idtf = np.eye(128, dtype=np.float32)
    cellk = np.zeros((128, 2), np.float32); cellk[:, 0] = 1e-6; cellk[:, 1] = 1e-5
    rmswf = np.ascontiguousarray(ins["L1_attn_rms_w"].astype(np.float32).reshape(2, 128).T)
    com = dict(mtrin=mtl, mtriun=mtu, mtui=mti, idtf=idtf, cellk=cellk, rmswf=rmswf,
               **std_host_consts())
    maps = []
    for c in range(NCORES):
        b = c // 2
        hs = [2 * (c % 2), 2 * (c % 2) + 1]
        qfu = np.stack([qfull[:, 2 * hh:2 * hh + 2, b, :] for hh in hs], 0)
        qfa = np.stack([qfull[:, 2 * hh:2 * hh + 2, b, :] for hh in hs], 1)
        kfa = np.stack([kfull[:, 2 * hh:2 * hh + 2, b, :] for hh in hs], 1)
        vfa = np.stack([vfull[:, 2 * hh:2 * hh + 2, b, :] for hh in hs], 1)
        def tmof(full, hh):
            f = full[:, 2 * hh:2 * hh + 2, b, :]
            ftm = f.transpose(2, 1, 0).reshape(T, 256)
            return ftm.reshape(32, 128, 256)
        ktma = np.stack([tmof(kfull, hh) for hh in hs], 0)
        vtma = np.stack([tmof(vfull, hh) for hh in hs], 0)
        bcola = np.stack([np.ascontiguousarray(
            bg_full[hh, b, :].reshape(32, 128).T) for hh in hs], 0)
        brwa = np.stack([bg_full[hh, b, :] for hh in hs], 0)
        grwa = np.stack([bg_full[4 + hh, b, :] for hh in hs], 0)
        mrwa = np.stack([bg_full[8 + hh, b, :] for hh in hs], 0)
        mb2 = ins["L1_attn_mix_bias"].astype(np.float32)[hs].reshape(1, 2)
        maps.append(dict(com,
                         qf=np.ascontiguousarray(qfa), kf=np.ascontiguousarray(kfa),
                         vf=np.ascontiguousarray(vfa),
                         ktm=np.ascontiguousarray(ktma), vtm=np.ascontiguousarray(vtma),
                         bcol=np.ascontiguousarray(bcola),
                         brw=np.ascontiguousarray(brwa), grw=np.ascontiguousarray(grwa),
                         mrw=np.ascontiguousarray(mrwa), mbias2=np.ascontiguousarray(mb2)))
    return maps


def run_l1_attn(ins, xs_fm_shards, runner):
    nc = bass.Bass("TRN2", target_bir_lowering=False)
    build_l1_proj(nc)
    maps = host_l1_proj(ins, xs_fm_shards)
    res = runner(nc, maps)
    qs_sh = [np.asarray(r["qs"]).reshape(128, DC, TOK) for r in res]
    ks_sh = [np.asarray(r["ks"]).reshape(128, DC, TOK) for r in res]
    vs_sh = [np.asarray(r["vs"]).reshape(128, DC, TOK) for r in res]
    bgm_sh = [np.asarray(r["bgm"]).reshape(12, TOK) for r in res]
    nc2 = bass.Bass("TRN2", target_bir_lowering=False)
    build_l1_scan(nc2)
    maps2 = host_l1_scan(ins, qs_sh, ks_sh, vs_sh, bgm_sh)
    res2 = runner(nc2, maps2)
    od_units = [np.asarray(r["od"]).reshape(128, 2, 2, 4096) for r in res2]
    osd = [np.zeros((128, DC, TOK), np.float32) for _ in range(NCORES)]
    for cs in range(NCORES):
        b = cs // 2
        hs = [2 * (cs % 2), 2 * (cs % 2) + 1]
        for ui, hh in enumerate(hs):
            for fc in range(2):
                col = od_units[cs][:, ui, fc, :]
                for cd in range(NCORES):
                    osd[cd][:, 2 * hh + fc, b * TPC:(b + 1) * TPC] = \
                        col[:, cd * TPC:(cd + 1) * TPC]
    nc3 = bass.Bass("TRN2", target_bir_lowering=False)
    build_proj_resid_ln(nc3)
    wp = np.ascontiguousarray(ins["L1_attn_Wo"].astype(np.float32).reshape(DC, 128, 1024))
    com3 = dict(wp=wp, wbias=np.zeros((128, DC), np.float32), **std_host_consts())
    maps3 = [dict(com3, xs=xs_fm_shards[c], osd=np.ascontiguousarray(osd[c]))
             for c in range(NCORES)]
    res3 = runner(nc3, maps3)
    return [np.ascontiguousarray(np.asarray(r["xo"], dtype=np.float32).reshape(128, DC, TOK))
            for r in res3]


def hw_runner(nc, maps):
    nc.finalize() if not nc.is_finalized else None
    res = run_bass_kernel_spmd(nc, maps, core_ids=list(range(NCORES)))
    return res.results


def kernel(**inputs):
    x = inputs["L0_attn_x"].astype(np.float32)
    shards = shard_tokens(x.reshape(B, T, D))
    xs = [to_fm(s) for s in shards]
    nc = bass.Bass("TRN2", target_bir_lowering=False)
    build_l0_attn(nc)
    maps = host_l0_attn(inputs, shards)
    res = run_bass_kernel_spmd(nc, maps, core_ids=list(range(NCORES)))
    xs = [np.ascontiguousarray(np.asarray(r["xo"]).reshape(128, DC, TOK)) for r in res.results]
    xs = run_launch(build_l0_ffn, host_l0_ffn(inputs), xs)
    xs = run_l1_attn(inputs, xs, hw_runner)
    xs = run_launch(build_l1_ffn, host_l1_ffn(inputs), xs)
    xs = run_l2_attn(inputs, xs, hw_runner)
    xs = run_launch(build_l2_ffn, host_l2_ffn(inputs), xs)
    xs = run_l3_attn(inputs, xs, hw_runner)
    xs = run_launch(build_l3_ffn, host_l3_ffn(inputs), xs)
    outs = [from_fm(s) for s in xs]
    return unshard_tokens(outs, D)
